# version 12
# speedup vs baseline: 1.1155x; 1.0159x over previous
import contextlib
import math

import numpy as np
import ml_dtypes

import concourse.bass as bass
import concourse.mybir as mybir
from concourse.bass_utils import run_bass_kernel_spmd

F32 = mybir.dt.float32
BF16 = mybir.dt.bfloat16
I8 = mybir.dt.int8
I32 = mybir.dt.int32
AF = mybir.ActivationFunctionType
ALU = mybir.AluOpType
AX = mybir.AxisListType
BF = ml_dtypes.bfloat16

D = 2048
INTER = 8192
DC = D // 128
IC = INTER // 128
HPC = 4
EPS = 1e-5
ISQ = 1.0 / math.sqrt(128.0)
PAY1 = 2176
QR_BASE = 1152
WPRE_COLS = 2112


def _had(n):
    h = np.array([[1.0]])
    while h.shape[0] < n:
        h = np.block([[h, h], [h, -h]])
    return h


def _bf(x):
    return np.ascontiguousarray(np.asarray(x)).astype(BF)


def _f32(x):
    return np.ascontiguousarray(np.asarray(x), dtype=np.float32)


def host_constants(SEQ):
    c = {}
    c["ident"] = _bf(np.eye(128))
    c["ones_col"] = _bf(np.ones((128, 1)))
    c["ones_row"] = _f32(np.ones((1, 128)))
    i = np.arange(128)[:, None]
    u = np.arange(1024)[None, :]
    c["causal"] = _bf((i <= (u - 512)).astype(np.float64))

    inv = 1.0 / (10000.0 ** (np.arange(0, 64, 2) / 64.0))
    fr = np.outer(np.arange(SEQ), inv)
    cosf, sinf = np.cos(fr).T, np.sin(fr).T
    cm = np.ones((128, SEQ))
    sm = np.zeros((128, SEQ))
    cm[64:96], cm[96:128] = cosf, cosf
    sm[64:96], sm[96:128] = -sinf, sinf
    c["l0_cmask"], c["l0_smask"] = _bf(cm), _bf(sm)
    p0 = np.zeros((128, 128))
    for j in range(32):
        p0[64 + j, 96 + j] = 1.0
        p0[96 + j, 64 + j] = 1.0
    c["l0_permT"] = _bf(p0.T)

    def r1(d):
        freq = np.arange(SEQ)[:, None] + 1.0
        pos = np.repeat(np.arange(d // 2)[:, None], 2, axis=-1).reshape(1, -1)
        theta = np.exp(-2.0 * pos / d * np.log(10000.0))
        ang = freq * theta
        return np.cos(ang).T, np.sin(ang).T

    c["l1_kr_cos"], c["l1_kr_sin"] = map(_bf, r1(64))
    c["l1_qr_cos"], c["l1_qr_sin"] = map(_bf, r1(1024))
    p1 = np.zeros((128, 128))
    for j in range(64):
        p1[2 * j, 2 * j + 1] = -1.0
        p1[2 * j + 1, 2 * j] = 1.0
    c["l1_permT"] = _bf(p1.T)

    c["h128"] = _bf(_had(128))
    c["h16"] = _bf(_had(16))
    c["h64"] = _bf(_had(64))
    return c


_PX = [""]


def _sb(nc, name, shape, dt):
    return nc.sbuf_tensor(_PX[0] + name, shape, dt)


def _pt(nc, name, shape, dt):
    return nc.psum_tensor(_PX[0] + name, shape, dt)


_SEMST = [None]


class _FenceState:
    def __init__(self, fstart, fl_sync, fl_gp):
        import collections
        self.fstart = fstart
        self.clean = collections.deque(range(fstart, 256))
        self.dirty = set()
        self.fl_sync = fl_sync
        self.fl_gp = fl_gp
        self.nf = 0


@contextlib.contextmanager
def _sm(nc, name):
    st = _SEMST[0]
    if st is None:
        with nc.semaphore(_PX[0] + name) as h:
            yield h
        return
    assert st.clean, f"sem pool exhausted at {_PX[0] + name}: add a _sem_fence"
    num = st.clean.popleft()
    h = nc.alloc_semaphore(name=_PX[0] + name, num=num)
    try:
        yield h
    finally:
        nc.release_semaphore(h)
        st.dirty.add(num)


def _sem_fence(nc, src_ap, sb_f):
    st = _SEMST[0]
    st.nf += 1
    k = st.nf
    with nc.Block(no_gpsimd_drain=True) as blk:
        @blk.sync
        def _(sy):
            sy.dma_start(out=sb_f[0:1, 0:16], in_=src_ap[0:1, 0:16]).then_inc(st.fl_sync, 16)
            sy.wait_ge(st.fl_sync, 16 * k)

        @blk.gpsimd
        def _(g_):
            g_.dma_start(out=sb_f[0:1, 16:32], in_=src_ap[0:1, 16:32]).then_inc(st.fl_gp, 16)
            g_.wait_ge(st.fl_gp, 16 * k)
    with nc.Block(no_gpsimd_drain=True) as blk:
        @blk.gpsimd
        def _(g_):
            g_.dma_reset(range(st.fstart, 256))
            g_.sem_clear(range(st.fstart, 256))
    st.clean.extend(sorted(st.dirty))
    st.dirty.clear()



def build_A(L, TPC):
    TG = TPC // 128
    NT = TPC // 512
    DPAY = D if L == 0 else PAY1
    nc = bass.Bass("TRN2", target_bir_lowering=False)

    x = nc.dram_tensor("x", [TPC, D], F32, kind="ExternalInput").ap()
    ident = nc.dram_tensor("ident", [128, 128], BF16, kind="ExternalInput").ap()
    wdt = I8 if L == 0 else BF16
    wg = nc.dram_tensor("wg", [D, INTER], wdt, kind="ExternalInput").ap()
    wu = nc.dram_tensor("wu", [D, INTER], wdt, kind="ExternalInput").ap()
    wd = nc.dram_tensor("wd", [INTER, D], wdt, kind="ExternalInput").ap()
    sg_rs = nc.dram_tensor("sg_rs", [128, IC], F32, kind="ExternalInput").ap()
    su_rs = nc.dram_tensor("su_rs", [128, IC], F32, kind="ExternalInput").ap()
    sd_row = nc.dram_tensor("sd_row", [1, D], F32, kind="ExternalInput").ap()
    ones_row = nc.dram_tensor("ones_row", [1, 128], F32, kind="ExternalInput").ap()
    if L == 1:
        wpre = nc.dram_tensor("wpre", [D, WPRE_COLS], F32, kind="ExternalInput").ap()
        bpre_rs = nc.dram_tensor("bpre_rs", [128, 17], F32, kind="ExternalInput").ap()
        krcos = nc.dram_tensor("krcos", [64, TPC], BF16, kind="ExternalInput").ap()
        krsin = nc.dram_tensor("krsin", [64, TPC], BF16, kind="ExternalInput").ap()
        qrcos = nc.dram_tensor("qrcos", [1024, TPC], BF16, kind="ExternalInput").ap()
        qrsin = nc.dram_tensor("qrsin", [1024, TPC], BF16, kind="ExternalInput").ap()
        permT = nc.dram_tensor("permT", [128, 128], BF16, kind="ExternalInput").ap()
        sudp_rs = nc.dram_tensor("sudp_rs", [128, IC], F32, kind="ExternalInput").ap()
        suug_rs = nc.dram_tensor("suug_rs", [128, DC], F32, kind="ExternalInput").ap()

    pay = nc.dram_tensor("pay", [DPAY, TPC], BF16, kind="ExternalOutput").ap()
    g_out = nc.dram_tensor("g", [TPC, D], F32, kind="ExternalOutput").ap()
    fuse_d = nc.dram_tensor("fuse", [INTER, TPC], BF16).ap()

    es = contextlib.ExitStack()
    with es:
        sb_ident = es.enter_context(_sb(nc, "c_ident", [128, 128], BF16))
        sb_sg = es.enter_context(_sb(nc, "c_sg", [128, IC], F32))
        sb_su = es.enter_context(_sb(nc, "c_su", [128, IC], F32))
        sb_sdb = es.enter_context(_sb(nc, "c_sdb", [128, D], F32))
        sb_or = es.enter_context(_sb(nc, "c_or", [1, 128], F32))
        sb_sdr = es.enter_context(_sb(nc, "c_sdr", [1, D], F32))
        if L == 1:
            sb_perm = es.enter_context(_sb(nc, "c_perm", [128, 128], BF16))
            sb_bpre = es.enter_context(_sb(nc, "c_bpre", [128, 17], F32))
            sb_sudp = es.enter_context(_sb(nc, "c_sudp", [128, IC], F32))
            sb_suug = es.enter_context(_sb(nc, "c_suug", [128, DC], F32))

        ld0 = es.enter_context(_sm(nc, "ld0"))
        c0 = es.enter_context(_sm(nc, "c0"))

        with nc.Block(no_gpsimd_drain=True) as block:
            @block.gpsimd
            def _(g_):
                n = 0
                pairs = [(sb_ident, ident), (sb_sg, sg_rs), (sb_su, su_rs),
                         (sb_or, ones_row), (sb_sdr, sd_row)]
                if L == 1:
                    pairs += [(sb_perm, permT), (sb_bpre, bpre_rs),
                              (sb_sudp, sudp_rs), (sb_suug, suug_rs)]
                for dst, src in pairs:
                    g_.dma_start(out=dst[:, :], in_=src).then_inc(ld0, 16)
                    n += 16
                g_.wait_ge(ld0, n)

        with contextlib.ExitStack() as bces:
            ps_bc0 = bces.enter_context(_pt(nc, "ps_bc0", [128, 512], F32))
            with nc.Block(no_gpsimd_drain=True) as block:
                @block.tensor
                def _(t):
                    for j in range(D // 512):
                        if j >= 1:
                            t.wait_ge(c0, 2 * j)
                        t.matmul(ps_bc0[:, :], sb_or[:, :], sb_sdr[:, j * 512:(j + 1) * 512],
                                 start=True, stop=True).then_inc(c0, 1)

                @block.scalar
                def _(s):
                    for j in range(D // 512):
                        s.wait_ge(c0, 2 * j + 1)
                        s.activation(sb_sdb[:, j * 512:(j + 1) * 512], ps_bc0[:, :],
                                     AF.Identity).then_inc(c0, 1)

        with contextlib.ExitStack() as exn:
            xn = exn.enter_context(_sb(nc, "xn", [128, DC * TPC], BF16))
            _emit_ln(nc, x, xn, pay if L == 0 else None, sb_ident, TPC)

            if L == 1:
                _emit_preproj(nc, xn, pay, wpre, sb_bpre, sb_perm,
                              krcos, krsin, qrcos, qrsin, TPC)
                _emit_scale_xn(nc, xn, sb_suug, TPC)

            _emit_ffn_upgate(nc, L, xn, wg, wu, sb_sg, sb_su,
                             sb_sudp if L == 1 else None, fuse_d, TPC)
        _emit_ffn_down(nc, L, wd, sb_sdb, fuse_d, g_out, TPC)

    return nc


def _emit_ln(nc, x, xn, pay_ap, sb_ident, TPC):
    TG = TPC // 128
    NDV = 6
    with contextlib.ExitStack() as es:
        sb_x = [es.enter_context(_sb(nc, f"ln_x{i}", [128, D], F32)) for i in range(2)]
        sb_sum = es.enter_context(_sb(nc, "ln_sum", [128, 1], F32))
        sb_negm = es.enter_context(_sb(nc, "ln_negm", [128, 1], F32))
        sb_xc = es.enter_context(_sb(nc, "ln_xc", [128, D], F32))
        sb_sq = es.enter_context(_sb(nc, "ln_sq", [128, D], F32))
        sb_ss = es.enter_context(_sb(nc, "ln_ss", [128, 1], F32))
        sb_veps = es.enter_context(_sb(nc, "ln_veps", [128, 1], F32))
        sb_iv = es.enter_context(_sb(nc, "ln_iv", [128, 1], F32))
        sb_rstd = es.enter_context(_sb(nc, "ln_rstd", [128, 1], F32))
        sb_xtm = [es.enter_context(_sb(nc, f"ln_xtm{i}", [128, D], BF16)) for i in range(2)]
        ps_tr = [es.enter_context(_pt(nc, f"ln_tr{i}", [128, 1024], BF16)) for i in range(2)]
        s_ld = [es.enter_context(_sm(nc, f"ln_ld{i}")) for i in range(2)]
        s_dv = es.enter_context(_sm(nc, "ln_dv"))
        s_ac = es.enter_context(_sm(nc, "ln_ac"))
        s_xn = es.enter_context(_sm(nc, "ln_xn"))
        s_pe = es.enter_context(_sm(nc, "ln_pe"))
        s_ev = es.enter_context(_sm(nc, "ln_ev"))
        s_ps = es.enter_context(_sm(nc, "ln_ps"))

        with nc.Block(no_gpsimd_drain=True) as block:
            @block.sync
            def _(sy):
                for tg in range(TG):
                    if tg >= 2:
                        sy.wait_ge(s_xn, tg - 1)
                    sy.dma_start(out=sb_x[tg % 2][:, :],
                                 in_=x[tg * 128:(tg + 1) * 128, :]).then_inc(s_ld[tg % 2], 16)
                if pay_ap is not None:
                    sy.wait_ge(s_ev, 2 * TG)
                    for ch in range(DC):
                        sy.dma_start(out=pay_ap[ch * 128:(ch + 1) * 128, :],
                                     in_=xn[:, ch * TPC:(ch + 1) * TPC]).then_inc(s_ps, 16)
                    sy.wait_ge(s_ps, DC * 16)

            @block.vector
            def _(v):
                xn3 = xn[:, :].rearrange("p (c t) -> p c t", c=DC)
                for tg in range(TG + 1):
                    if tg < TG:
                        xb = sb_x[tg % 2]
                        v.wait_ge(s_ld[tg % 2], (tg // 2 + 1) * 16)
                        v.tensor_reduce(sb_sum[:, :], xb[:, :], axis=AX.X, op=ALU.add).then_inc(s_dv, 1)
                        v.wait_ge(s_dv, tg * NDV + 1)
                        v.tensor_scalar(sb_negm[:, :], sb_sum[:, :], -1.0 / D, None, ALU.mult).then_inc(s_dv, 1)
                        v.wait_ge(s_dv, tg * NDV + 2)
                        v.tensor_scalar(sb_xc[:, :], xb[:, :], sb_negm[:, :], None, ALU.add).then_inc(s_dv, 1)
                        v.wait_ge(s_dv, tg * NDV + 3)
                        v.scalar_tensor_tensor(sb_sq[:, :], sb_xc[:, :], 1.0, sb_xc[:, :],
                                               ALU.mult, ALU.mult,
                                               accum_out=sb_ss[:, :]).then_inc(s_dv, 1)
                        v.wait_ge(s_dv, tg * NDV + 4)
                        v.tensor_scalar(sb_veps[:, :], sb_ss[:, :], 1.0 / D, EPS,
                                        ALU.mult, ALU.add).then_inc(s_dv, 1)
                        v.wait_ge(s_dv, tg * NDV + 5)
                        v.reciprocal(sb_iv[:, :], sb_veps[:, :]).then_inc(s_dv, 1)
                        v.wait_ge(s_ac, tg + 1)
                        if tg >= 2:
                            v.wait_ge(s_pe, tg - 1)
                        v.tensor_scalar(sb_xtm[tg % 2][:, :], sb_xc[:, :], sb_rstd[:, :], None,
                                        ALU.mult).then_inc(s_xn, 1)
                    if tg >= 1:
                        tp = tg - 1
                        v.wait_ge(s_pe, tp + 1)
                        for half in range(2):
                            v.tensor_copy(
                                xn3[:, half * 8:half * 8 + 8, tp * 128:(tp + 1) * 128],
                                ps_tr[half][:, :].rearrange("p (c t) -> p c t", c=8),
                            ).then_inc(s_ev, 1)

            @block.scalar
            def _(s):
                for tg in range(TG):
                    s.wait_ge(s_dv, tg * NDV + 6)
                    s.activation(sb_rstd[:, :], sb_iv[:, :], AF.Sqrt).then_inc(s_ac, 1)

            @block.tensor
            def _(t):
                for tg in range(TG):
                    t.wait_ge(s_xn, tg + 1)
                    if tg >= 1:
                        t.wait_ge(s_ev, 2 * tg)
                    ins = None
                    for half in range(2):
                        for j in range(8):
                            ch = half * 8 + j
                            ins = t.transpose(ps_tr[half][:, j * 128:(j + 1) * 128],
                                              sb_xtm[tg % 2][:, ch * 128:(ch + 1) * 128],
                                              sb_ident[:, :])
                    ins.then_inc(s_pe, 1)



def _emit_scale_xn(nc, xn, sb_suug, TPC):
    with contextlib.ExitStack() as es:
        s_dv = es.enter_context(_sm(nc, "sx_dv"))
        sb_t = es.enter_context(_sb(nc, "sx_t", [128, TPC], BF16))
        with nc.Block(no_gpsimd_drain=True) as block:
            @block.vector
            def _(v):
                n = 0
                for ch in range(DC):
                    v.tensor_scalar(sb_t[:, :], xn[:, ch * TPC:(ch + 1) * TPC],
                                    sb_suug[:, ch:ch + 1], None, ALU.mult).then_inc(s_dv, 1)
                    n += 1
                    v.wait_ge(s_dv, n)
                    v.tensor_copy(xn[:, ch * TPC:(ch + 1) * TPC], sb_t[:, :]).then_inc(s_dv, 1)
                    n += 1
                    v.wait_ge(s_dv, n)


def _emit_preproj(nc, xn, pay, wpre, sb_bpre, sb_perm, krcos, krsin, qrcos, qrsin, TPC):
    NT = TPC // 512
    chunks = ([(oc * 128, 128, oc * 128, oc, "plain") for oc in range(8)]
              + [(1024, 64, 1024, 8, "kr")]
              + [(QR_BASE + j * 128, 128, 1088 + j * 128, 9 + j, "qr") for j in range(8)])
    NCH = len(chunks)
    wpre_r = wpre.rearrange("(c p) n -> p c n", p=128)

    with contextlib.ExitStack() as es:
        sb_w = [es.enter_context(_sb(nc, f"pp_w{i}", [128, DC * 128], BF16)) for i in range(2)]
        sb_o = es.enter_context(_sb(nc, "pp_o", [128, TPC], BF16))
        sb_cos = es.enter_context(_sb(nc, "pp_cos", [128, TPC], BF16))
        sb_sin = es.enter_context(_sb(nc, "pp_sin", [128, TPC], BF16))
        sb_t1 = es.enter_context(_sb(nc, "pp_t1", [128, TPC], BF16))
        sb_pay = es.enter_context(_sb(nc, "pp_pay", [128, TPC], BF16))
        ps_pr = es.enter_context(_pt(nc, "pp_pr", [128, 4 * 512], F32))
        ps_rt = es.enter_context(_pt(nc, "pp_rt", [128, 4 * 512], F32))
        s_wl = [es.enter_context(_sm(nc, f"pp_wl{i}")) for i in range(2)]
        s_tb = es.enter_context(_sm(nc, "pp_tb"))
        s_pe = es.enter_context(_sm(nc, "pp_pe"))
        s_rt = es.enter_context(_sm(nc, "pp_rt"))
        s_ac = es.enter_context(_sm(nc, "pp_ac"))
        s_dv = es.enter_context(_sm(nc, "pp_dv"))
        s_st = es.enter_context(_sm(nc, "pp_st"))

        with nc.Block(no_gpsimd_drain=True) as block:
            @block.gpsimd
            def _(g_):
                row0, rows, w0, bc, kind = chunks[0]
                g_.dma_start(out=sb_w[0][:, 0:DC * rows],
                             in_=wpre_r[:, :, w0:w0 + rows]).then_inc(s_wl[0], 16)

        ndv = 0
        for i, (row0, rows, w0, bc, kind) in enumerate(chunks):
            rope = kind != "plain"
            ndv_start = ndv

            with nc.Block(no_gpsimd_drain=True) as block:
                @block.gpsimd
                def _(g_, i=i):
                    if i + 1 < NCH:
                        _r2, rows2, w02, _, _ = chunks[i + 1]
                        g_.dma_start(out=sb_w[(i + 1) % 2][:, 0:DC * rows2],
                                     in_=wpre_r[:, :, w02:w02 + rows2]).then_inc(s_wl[(i + 1) % 2], 16)

                @block.sync
                def _(sy, i=i, rope=rope, rows=rows, bc=bc, kind=kind, ndv_start=ndv_start, row0=row0):
                    if rope:
                        csrc = krcos if kind == "kr" else qrcos
                        ssrc = krsin if kind == "kr" else qrsin
                        r0 = 0 if kind == "kr" else (bc - 9) * 128
                        sy.dma_start(out=sb_cos[0:rows, :], in_=csrc[r0:r0 + rows, :]).then_inc(s_tb, 16)
                        sy.dma_start(out=sb_sin[0:rows, :], in_=ssrc[r0:r0 + rows, :]).then_inc(s_tb, 16)
                        sy.wait_ge(s_dv, ndv_start + 2 + NT)
                        src = sb_pay
                    else:
                        sy.wait_ge(s_ac, i + 1)
                        src = sb_o
                    sy.dma_start(out=pay[row0:row0 + rows, :],
                                 in_=src[0:rows, :]).then_inc(s_st, 16)
                    sy.wait_ge(s_st, 16 * (i + 1))

                @block.tensor
                def _(t, i=i, rows=rows, rope=rope, kind=kind):
                    t.wait_ge(s_wl[i % 2], (i // 2 + 1) * 16)
                    ins = None
                    for dc in range(DC):
                        for tt in range(NT):
                            ins = t.matmul(ps_pr[0:rows, tt * 512:(tt + 1) * 512],
                                           sb_w[i % 2][:, dc * rows:(dc + 1) * rows],
                                           xn[:, dc * TPC + tt * 512:dc * TPC + (tt + 1) * 512],
                                           start=(dc == 0), stop=(dc == DC - 1))
                    ins.then_inc(s_pe, 1)
                    if rope:
                        t.wait_ge(s_ac, i + 1)
                        psl = sb_perm[:, :] if kind == "qr" else sb_perm[0:64, 0:64]
                        for tt in range(NT):
                            ins = t.matmul(ps_rt[0:rows, tt * 512:(tt + 1) * 512], psl,
                                           sb_o[0:rows, tt * 512:(tt + 1) * 512],
                                           start=True, stop=True)
                        ins.then_inc(s_rt, 1)

                @block.scalar
                def _(s, i=i, rows=rows, bc=bc):
                    s.wait_ge(s_pe, i + 1)
                    for tt in range(NT):
                        ins = s.activation(sb_o[0:rows, tt * 512:(tt + 1) * 512],
                                           ps_pr[0:rows, tt * 512:(tt + 1) * 512], AF.Identity,
                                           bias=sb_bpre[0:rows, bc:bc + 1])
                        if tt == NT - 1:
                            ins.then_inc(s_ac, 1)

                if rope:
                    nrt = sum(1 for j in range(i + 1) if chunks[j][4] != "plain")

                    @block.vector
                    def _(v, i=i, rows=rows, ndv_start=ndv_start, nrt=nrt):
                        v.wait_ge(s_tb, 32 * nrt)
                        v.wait_ge(s_ac, i + 1)
                        v.wait_ge(s_rt, nrt)
                        v.tensor_tensor(sb_t1[0:rows, :], sb_o[0:rows, :],
                                        sb_cos[0:rows, :], ALU.mult).then_inc(s_dv, 1)
                        for tt in range(NT):
                            v.tensor_tensor(sb_pay[0:rows, tt * 512:(tt + 1) * 512],
                                            ps_rt[0:rows, tt * 512:(tt + 1) * 512],
                                            sb_sin[0:rows, tt * 512:(tt + 1) * 512],
                                            ALU.mult).then_inc(s_dv, 1)
                        v.wait_ge(s_dv, ndv_start + 1 + NT)
                        v.tensor_tensor(sb_pay[0:rows, :], sb_pay[0:rows, :],
                                        sb_t1[0:rows, :], ALU.add).then_inc(s_dv, 1)

                    ndv += 2 + NT


def _emit_ffn_upgate(nc, L, xn, wg, wu, sb_sg, sb_su, sb_sudp, fuse_d, TPC):
    TH = TPC // 2
    NT2 = TH // 512
    NSLOT = IC * 2
    wg_r = wg.rearrange("(c p) n -> p c n", p=128)
    wu_r = wu.rearrange("(c p) n -> p c n", p=128)
    cast = (L == 0)

    with contextlib.ExitStack() as es:
        sb_wg = [es.enter_context(_sb(nc, f"fg_wg{i}", [128, DC * 128], BF16)) for i in range(2)]
        sb_wu = [es.enter_context(_sb(nc, f"fg_wu{i}", [128, DC * 128], BF16)) for i in range(2)]
        sb_gate = [es.enter_context(_sb(nc, f"fg_gate{i}", [128, TH], BF16)) for i in range(2)]
        sb_f1 = [es.enter_context(_sb(nc, f"fg_f1{i}", [128, TH], BF16)) for i in range(2)]
        sb_fuse = [es.enter_context(_sb(nc, f"fg_fuse{i}", [128, TH], BF16)) for i in range(2)]
        ps_g = [es.enter_context(_pt(nc, f"fg_psg{i}", [128, TH if TH <= 1024 else 1024], F32))
                for i in range(2)]
        ps_u = [es.enter_context(_pt(nc, f"fg_psu{i}", [128, TH if TH <= 1024 else 1024], F32))
                for i in range(2)]
        s_wl = [es.enter_context(_sm(nc, f"fg_wl{i}")) for i in range(2)]
        s_pe = es.enter_context(_sm(nc, "fg_pe"))
        s_ac = es.enter_context(_sm(nc, "fg_ac"))
        s_dv = es.enter_context(_sm(nc, "fg_dv"))
        s_st = es.enter_context(_sm(nc, "fg_st"))
        assert TH <= 1024, "token half must fit 2 PSUM banks"

        def wload(g_, i):
            dma = g_.dma_start
            dma(out=sb_wg[i % 2][:, :], in_=wg_r[:, :, i * 128:(i + 1) * 128]).then_inc(s_wl[i % 2], 16)
            dma(out=sb_wu[i % 2][:, :], in_=wu_r[:, :, i * 128:(i + 1) * 128]).then_inc(s_wl[i % 2], 16)

        with nc.Block(no_gpsimd_drain=True) as block:
            @block.gpsimd
            def _(g_):
                wload(g_, 0)

        for k in range(NSLOT + 1):
            with nc.Block(no_gpsimd_drain=True) as block:
                @block.gpsimd
                def _(g_, k=k):
                    if k < NSLOT and k % 2 == 0 and (k // 2) + 1 < IC:
                        wload(g_, (k // 2) + 1)

                @block.tensor
                def _(t, k=k):
                    if k >= NSLOT:
                        return
                    i, half = k // 2, k % 2
                    par = k % 2
                    t.wait_ge(s_wl[i % 2], 32 * (i // 2 + 1))
                    if k >= 2:
                        t.wait_ge(s_dv, 2 * (k - 1))
                    ins = None
                    for dc in range(DC):
                        for tt in range(NT2):
                            c0 = half * TH + tt * 512
                            ins = t.matmul(ps_g[par][:, tt * 512:(tt + 1) * 512],
                                           sb_wg[i % 2][:, dc * 128:(dc + 1) * 128],
                                           xn[:, dc * TPC + c0:dc * TPC + c0 + 512],
                                           start=(dc == 0), stop=(dc == DC - 1))
                    ins.then_inc(s_pe, 1)
                    for dc in range(DC):
                        for tt in range(NT2):
                            c0 = half * TH + tt * 512
                            ins = t.matmul(ps_u[par][:, tt * 512:(tt + 1) * 512],
                                           sb_wu[i % 2][:, dc * 128:(dc + 1) * 128],
                                           xn[:, dc * TPC + c0:dc * TPC + c0 + 512],
                                           start=(dc == 0), stop=(dc == DC - 1))
                    ins.then_inc(s_pe, 1)

                @block.scalar
                def _(s, k=k):
                    j = k - 1
                    if not (0 <= j < NSLOT):
                        return
                    i = j // 2
                    par = j % 2
                    s.wait_ge(s_pe, 2 * j + 1)
                    s.activation(sb_gate[par][:, :], ps_g[par][:, 0:TH], AF.Silu,
                                 scale=sb_sg[:, i:i + 1]).then_inc(s_ac, 1)

                @block.vector
                def _(v, k=k):
                    j = k - 1
                    if not (0 <= j < NSLOT):
                        return
                    i = j // 2
                    par = j % 2
                    v.wait_ge(s_ac, j + 1)
                    v.wait_ge(s_pe, 2 * j + 2)
                    if k >= 3:
                        v.wait_ge(s_st, 16 * (k - 2))
                    if sb_sudp is None:
                        v.tensor_scalar(sb_f1[par][:, :], ps_u[par][:, 0:TH],
                                        sb_su[:, i:i + 1], None, ALU.mult).then_inc(s_dv, 1)
                    else:
                        v.tensor_scalar(sb_f1[par][:, :], ps_u[par][:, 0:TH],
                                        sb_su[:, i:i + 1], sb_sudp[:, i:i + 1],
                                        ALU.mult, ALU.mult).then_inc(s_dv, 1)
                    v.wait_ge(s_dv, 2 * j + 1)
                    v.tensor_tensor(sb_fuse[par][:, :], sb_gate[par][:, :],
                                    sb_f1[par][:, :], ALU.mult).then_inc(s_dv, 1)

                @block.sync
                def _(sy, k=k):
                    j = k - 1
                    if not (0 <= j < NSLOT):
                        return
                    i, half = j // 2, j % 2
                    sy.wait_ge(s_dv, 2 * j + 2)
                    sy.dma_start(out=fuse_d[i * 128:(i + 1) * 128, half * TH:(half + 1) * TH],
                                 in_=sb_fuse[j % 2][:, :]).then_inc(s_st, 16)
                    sy.wait_ge(s_st, 16 * (j + 1))

        with nc.Block(no_gpsimd_drain=True) as block:
            @block.sync
            def _(sy):
                sy.wait_ge(s_st, 16 * NSLOT)


def _emit_ffn_down(nc, L, wd, sb_sdb, fuse_d, g_out, TPC):
    TT = 128
    NTT = TPC // TT
    DH = D // 2
    NU = 2 * NTT
    cast = (L == 0)

    with contextlib.ExitStack() as es:
        sb_wr = es.enter_context(_sb(nc, "fd_wr", [128, IC * DH], BF16))
        sb_ft = [es.enter_context(_sb(nc, f"fd_ft{i}", [128, IC * TT], BF16)) for i in range(2)]
        sb_t = [es.enter_context(_sb(nc, f"fd_t{i}", [128, DH], F32)) for i in range(2)]
        sb_g = [es.enter_context(_sb(nc, f"fd_g{i}", [128, DH], F32)) for i in range(2)]
        ps_dn = [es.enter_context(_pt(nc, f"fd_ps{i}", [128, DH], F32)) for i in range(2)]
        s_fl = [es.enter_context(_sm(nc, f"fd_fl{i}")) for i in range(2)]
        s_wh = [es.enter_context(_sm(nc, f"fd_wh{i}")) for i in range(4)]
        s_pe = es.enter_context(_sm(nc, "fd_pe"))
        s_ac = es.enter_context(_sm(nc, "fd_ac"))
        s_dv = es.enter_context(_sm(nc, "fd_dv"))
        s_st = es.enter_context(_sm(nc, "fd_st"))

        fuse_r = fuse_d.rearrange("(c p) n -> p c n", p=128)
        wd_r = wd.rearrange("(c p) n -> p c n", p=128)
        GC = IC // 4

        def emit_wd_loads(eng, dd, groups):
            for g in groups:
                eng.dma_start(
                    out=sb_wr[:, g * GC * DH:(g + 1) * GC * DH]
                        .rearrange("p (c n) -> p c n", c=GC),
                    in_=wd_r[:, g * GC:(g + 1) * GC, dd * DH:(dd + 1) * DH],
                ).then_inc(s_wh[g], 16)

        sy_grps = () if cast else (0, 1)
        gp_grps = (0, 1, 2, 3) if cast else (2, 3)

        with nc.Block(no_gpsimd_drain=True) as block:
            @block.sync
            def _(sy):
                sy.dma_start(out=sb_ft[0][:, :].rearrange("p (c t) -> p c t", c=IC),
                             in_=fuse_r[:, :, 0:TT]).then_inc(s_fl[0], 16)
                emit_wd_loads(sy, 0, sy_grps)

            @block.gpsimd
            def _(g_):
                emit_wd_loads(g_, 0, gp_grps)

        for u in range(NU + 2):
            dd, tt = u // NTT, u % NTT
            if u == NTT:
                with nc.Block(no_gpsimd_drain=True) as blk2:
                    @blk2.sync
                    def _(sy):
                        sy.wait_ge(s_pe, NTT)
                        emit_wd_loads(sy, 1, sy_grps)

                    @blk2.gpsimd
                    def _(g_):
                        g_.wait_ge(s_pe, NTT)
                        emit_wd_loads(g_, 1, gp_grps)

            with nc.Block(no_gpsimd_drain=True) as block:
                @block.sync
                def _(sy, u=u):
                    if u + 1 < NU:
                        u1 = u + 1
                        sy.dma_start(
                            out=sb_ft[u1 % 2][:, :].rearrange("p (c t) -> p c t", c=IC),
                            in_=fuse_r[:, :, (u1 % NTT) * TT:(u1 % NTT + 1) * TT],
                        ).then_inc(s_fl[u1 % 2], 16)
                    us = u - 2
                    if us >= 0:
                        dds, tts = us // NTT, us % NTT
                        sy.wait_ge(s_ac, us + 1)
                        if us >= 1:
                            sy.wait_ge(s_st, 16 * us)
                        sy.dma_start(out=g_out[tts * TT:(tts + 1) * TT, dds * DH:(dds + 1) * DH],
                                     in_=sb_g[us % 2][:, :]).then_inc(s_st, 16)
                        if us == NU - 1:
                            sy.wait_ge(s_st, 16 * NU)

                @block.tensor
                def _(t, u=u, dd=dd, tt=tt):
                    if u >= NU:
                        return
                    t.wait_ge(s_fl[u % 2], 16 * (u // 2 + 1))
                    ins = None
                    for ic in range(IC):
                        if ic % GC == 0:
                            t.wait_ge(s_wh[ic // GC], 16 * (dd + 1))
                        for hw in range(DH // 512):
                            ins = t.matmul(ps_dn[u % 2][:, hw * 512:(hw + 1) * 512],
                                           sb_ft[u % 2][:, ic * TT:ic * TT + TT],
                                           sb_wr[:, ic * DH + hw * 512:ic * DH + (hw + 1) * 512],
                                           start=(ic == 0), stop=(ic == IC - 1))
                    ins.then_inc(s_pe, 1)

                @block.vector
                def _(v, u=u):
                    uv = u - 1
                    if 0 <= uv < NU:
                        v.tensor_tensor(sb_t[uv % 2][:, :], ps_dn[uv % 2][:, :],
                                        sb_sdb[:, (uv // NTT) * DH:(uv // NTT + 1) * DH],
                                        ALU.mult).then_inc(s_dv, 1)

                @block.scalar
                def _(s, u=u):
                    uv = u - 1
                    if 0 <= uv < NU:
                        s.wait_ge(s_dv, uv + 1)
                        if uv >= 2:
                            s.wait_ge(s_st, 16 * (uv - 1))
                        s.activation(sb_g[uv % 2][:, :], sb_t[uv % 2][:, :],
                                     AF.Silu).then_inc(s_ac, 1)



def build_B(L, TPC, SEQ):
    NRANK = SEQ // TPC
    DPAY = D if L == 0 else PAY1
    nc = bass.Bass("TRN2", target_bir_lowering=False)

    payf = nc.dram_tensor("payf", [NRANK * DPAY, TPC], BF16, kind="ExternalInput")
    ident = nc.dram_tensor("ident", [128, 128], BF16, kind="ExternalInput").ap()
    ones_col = nc.dram_tensor("ones_col", [128, 1], BF16, kind="ExternalInput").ap()
    ones_row = nc.dram_tensor("ones_row", [1, 128], F32, kind="ExternalInput").ap()
    causal = nc.dram_tensor("causal", [128, 1024], BF16, kind="ExternalInput").ap()
    meta = nc.dram_tensor("meta", [1, 32], I32, kind="ExternalInput").ap()
    if L == 0:
        wqkv = nc.dram_tensor("wqkv", [D, HPC * 384], F32, kind="ExternalInput").ap()
        cmask = nc.dram_tensor("cmask", [128, SEQ], BF16, kind="ExternalInput").ap()
        smask = nc.dram_tensor("smask", [128, SEQ], BF16, kind="ExternalInput").ap()
        permT = nc.dram_tensor("permT", [128, 128], BF16, kind="ExternalInput").ap()
    else:
        wukv = nc.dram_tensor("wukv", [512, HPC * 256], F32, kind="ExternalInput").ap()
        wuq = nc.dram_tensor("wuq", [512, HPC * 128], F32, kind="ExternalInput").ap()
        bkv_rs = nc.dram_tensor("bkv_rs", [128, HPC * 2], F32, kind="ExternalInput").ap()
        bq_rs = nc.dram_tensor("bq_rs", [128, HPC], F32, kind="ExternalInput").ap()

    attnout = nc.dram_tensor("attnout", [HPC * 128, SEQ], BF16, kind="ExternalOutput").ap()
    qd = nc.dram_tensor("qd", [HPC * 128, SEQ], BF16)
    kd = nc.dram_tensor("kd", [HPC * 128, SEQ], BF16)
    vd = nc.dram_tensor("vd", [HPC * SEQ, 128], BF16)
    qrd = (nc.dram_tensor("qrm", [HPC * 64, SEQ], BF16, kind="ExternalInput")
           if L == 1 else None)

    es = contextlib.ExitStack()
    with es:
        sb_ident = es.enter_context(_sb(nc, "c_ident", [128, 128], BF16))
        sb_ones = es.enter_context(_sb(nc, "c_ones", [128, 1], BF16))
        sb_or = es.enter_context(_sb(nc, "c_or", [1, 128], F32))
        sb_caus = es.enter_context(_sb(nc, "c_caus", [128, 1024], BF16))
        sb_meta = es.enter_context(_sb(nc, "c_meta", [1, 32], I32))
        if L == 0:
            sb_perm = es.enter_context(_sb(nc, "c_perm", [128, 128], BF16))
            sb_w = es.enter_context(_sb(nc, "c_wqkv", [128, DC * HPC * 384], BF16))
            WXW = HPC * 384
        else:
            sb_w = es.enter_context(_sb(nc, "c_wukv", [128, 4 * HPC * 256], BF16))
            sb_wq = es.enter_context(_sb(nc, "c_wuq", [128, 4 * HPC * 128], BF16))
            sb_bkv = es.enter_context(_sb(nc, "c_bkv", [128, HPC * 2], F32))
            sb_bq = es.enter_context(_sb(nc, "c_bq", [128, HPC], F32))
        ld0 = es.enter_context(_sm(nc, "ld0"))

        with nc.Block(no_gpsimd_drain=True) as block:
            @block.gpsimd
            def _(g_):
                n = 0
                pairs = [(sb_ident[:, :], ident), (sb_ones[:, :], ones_col),
                         (sb_or[:, :], ones_row), (sb_caus[:, :], causal),
                         (sb_meta[:, :], meta)]
                if L == 0:
                    pairs += [(sb_perm[:, :], permT),
                              (sb_w[:, :].rearrange("p (c n) -> p c n", c=DC),
                               wqkv.rearrange("(c p) n -> p c n", p=128))]
                else:
                    pairs += [(sb_w[:, :].rearrange("p (c n) -> p c n", c=4),
                               wukv.rearrange("(c p) n -> p c n", p=128)),
                              (sb_wq[:, :].rearrange("p (c n) -> p c n", c=4),
                               wuq.rearrange("(c p) n -> p c n", p=128)),
                              (sb_bkv[:, :], bkv_rs), (sb_bq[:, :], bq_rs)]
                for dst, src in pairs:
                    g_.dma_start(out=dst, in_=src).then_inc(ld0, 16)
                    n += 16
                g_.wait_ge(ld0, n)

        if L == 0:
            _emit_proj_l0(nc, payf, sb_w, sb_perm, sb_ident, cmask, smask,
                          qd, kd, vd, TPC, SEQ)
        else:
            _emit_proj_l1(nc, payf, sb_w, sb_wq, sb_bkv, sb_bq, sb_ident,
                          qd, kd, vd, qrd, TPC, SEQ)

        _emit_sdpa(nc, L, qd, kd, vd, qrd, payf, meta,
                   sb_ones, sb_or, sb_caus, attnout, TPC, SEQ)

    return nc


def _vd_store_ap(vd, h, t0, ngrp, SEQ):
    return bass.AP(vd, (h * SEQ + t0) * 128,
                   [[128, 128], [128 * 128, ngrp], [1, 128]])


def _emit_proj_l0(nc, payf, sb_w, sb_perm, sb_ident, cmask, smask, qd, kd, vd, TPC, SEQ):
    NRANK = SEQ // TPC
    TH = TPC // 2
    NT2 = TH // 512
    NG2 = TH // 128
    WXW = HPC * 384
    UPB = HPC * 3 * 2

    with contextlib.ExitStack() as es:
        sb_xb = es.enter_context(_sb(nc, "pj_xb", [128, DC * TPC], BF16))
        sb_raw = [es.enter_context(_sb(nc, f"pj_raw{i}", [128, TH], BF16)) for i in range(2)]
        sb_cm = es.enter_context(_sb(nc, "pj_cm", [128, TPC], BF16))
        sb_sm = es.enter_context(_sb(nc, "pj_sm", [128, TPC], BF16))
        sb_t1 = es.enter_context(_sb(nc, "pj_t1", [128, TH], BF16))
        sb_fin = [es.enter_context(_sb(nc, f"pj_fin{i}", [128, TH], BF16)) for i in range(2)]
        sb_vt = [es.enter_context(_sb(nc, f"pj_vt{i}", [128, TH], BF16)) for i in range(2)]
        ps_pr = [es.enter_context(_pt(nc, f"pj_pr{i}", [128, 1024], F32)) for i in range(2)]
        ps_rt = es.enter_context(_pt(nc, "pj_rt", [128, 1024], F32))
        ps_t = es.enter_context(_pt(nc, "pj_t", [128, 1024], BF16))
        s_xl = es.enter_context(_sm(nc, "pj_xl"))
        s_pe = es.enter_context(_sm(nc, "pj_pe"))
        s_rt = es.enter_context(_sm(nc, "pj_rt"))
        s_ac = es.enter_context(_sm(nc, "pj_ac"))
        s_dv = es.enter_context(_sm(nc, "pj_dv"))
        s_st = es.enter_context(_sm(nc, "pj_st"))
        s_dc = es.enter_context(_sm(nc, "pj_dc"))
        ndc = [0]

        def unit_info(u):
            tb, r = divmod(u, UPB)
            h, r2 = divmod(r, 6)
            o, half = divmod(r2, 2)
            return tb, h, o, half

        NU = NRANK * UPB
        for slot in range(NU + 1):
            with nc.Block(no_gpsimd_drain=True) as block:
                @block.sync
                def _(sy, slot=slot):
                    if slot < NU:
                        tb, h, o, half = unit_info(slot)
                        if slot % UPB == 0:
                            if tb >= 1:
                                sy.wait_ge(s_pe, tb * UPB)
                            sy.dma_start(
                                out=sb_xb[:, :].rearrange("p (c t) -> p c t", c=DC),
                                in_=payf.ap().rearrange("(r c p) t -> p r c t",
                                                        r=NRANK, p=128)[:, tb, :, :],
                            ).then_inc(s_xl, 16)
                            sy.dma_start(out=sb_cm[:, :],
                                         in_=cmask[:, tb * TPC:(tb + 1) * TPC]).then_inc(s_xl, 16)
                            sy.dma_start(out=sb_sm[:, :],
                                         in_=smask[:, tb * TPC:(tb + 1) * TPC]).then_inc(s_xl, 16)
                    u = slot - 1
                    if u >= 0:
                        tb, h, o, half = unit_info(u)
                        sy.wait_ge(s_dv, u + 1)
                        t0 = tb * TPC + half * TH
                        if o < 2:
                            dst = qd if o == 0 else kd
                            sy.dma_start(out=dst.ap()[h * 128:(h + 1) * 128, t0:t0 + TH],
                                         in_=sb_fin[u % 2][:, :]).then_inc(s_st, 16)
                        else:
                            sy.dma_start(out=_vd_store_ap(vd, h, t0, NG2, SEQ),
                                         in_=sb_vt[u % 2][:, :].rearrange(
                                             "p (g d) -> p g d", d=128)).then_inc(s_st, 16)
                        sy.wait_ge(s_st, 16 * (u + 1))

                @block.tensor
                def _(t, slot=slot):
                    if slot < NU:
                        u = slot
                        tb, h, o, half = unit_info(u)
                        par = u % 2
                        t.wait_ge(s_xl, 48 * (tb + 1))
                        if u >= 2:
                            t.wait_ge(s_ac, u - 1)
                        wcol = h * 384 + o * 128
                        ins = None
                        for dc in range(DC):
                            for tt in range(NT2):
                                c0 = half * TH + tt * 512
                                ins = t.matmul(ps_pr[par][:, tt * 512:(tt + 1) * 512],
                                               sb_w[:, dc * WXW + wcol:dc * WXW + wcol + 128],
                                               sb_xb[:, dc * TPC + c0:dc * TPC + c0 + 512],
                                               start=(dc == 0), stop=(dc == DC - 1))
                        ins.then_inc(s_pe, 1)
                    u = slot - 1
                    if u >= 0:
                        tb, h, o, half = unit_info(u)
                        par = u % 2
                        t.wait_ge(s_ac, u + 1)
                        if o < 2:
                            ins = None
                            for tt in range(NT2):
                                ins = t.matmul(ps_rt[:, tt * 512:(tt + 1) * 512], sb_perm[:, :],
                                               sb_raw[par][:, tt * 512:(tt + 1) * 512],
                                               start=True, stop=True)
                            ins.then_inc(s_rt, 1)
                        else:
                            t.wait_ge(s_dv, u)
                            ins = None
                            for gi in range(NG2):
                                ins = t.transpose(ps_t[:, gi * 128:(gi + 1) * 128],
                                                  sb_raw[par][:, gi * 128:(gi + 1) * 128],
                                                  sb_ident[:, :])
                            ins.then_inc(s_rt, 1)

                @block.scalar
                def _(s, slot=slot):
                    u = slot - 1
                    if u >= 0:
                        tb, h, o, half = unit_info(u)
                        par = u % 2
                        s.wait_ge(s_pe, u + 1)
                        if u >= 2:
                            s.wait_ge(s_dv, u - 1)
                        s.activation(sb_raw[par][:, :], ps_pr[par][:, 0:TH],
                                     AF.Identity).then_inc(s_ac, 1)

                @block.vector
                def _(v, slot=slot, ndc=ndc):
                    u = slot - 1
                    if u >= 0:
                        tb, h, o, half = unit_info(u)
                        par = u % 2
                        v.wait_ge(s_rt, u + 1)
                        if u >= 2:
                            v.wait_ge(s_st, 16 * (u - 1))
                        c0 = half * TH
                        if o < 2:
                            v.tensor_tensor(sb_t1[:, :], sb_raw[par][:, :],
                                            sb_cm[:, c0:c0 + TH], ALU.mult).then_inc(s_dc, 1)
                            ndc[0] += 1
                            v.tensor_tensor(sb_fin[u % 2][:, :], ps_rt[:, 0:TH],
                                            sb_sm[:, c0:c0 + TH], ALU.mult).then_inc(s_dc, 1)
                            ndc[0] += 1
                            v.wait_ge(s_dc, ndc[0])
                            v.tensor_tensor(sb_fin[u % 2][:, :], sb_fin[u % 2][:, :],
                                            sb_t1[:, :], ALU.add).then_inc(s_dv, 1)
                        else:
                            v.tensor_copy(sb_vt[u % 2][:, :], ps_t[:, 0:TH]).then_inc(s_dv, 1)


def _emit_proj_l1(nc, payf, sb_w, sb_wq, sb_bkv, sb_bq, sb_ident, qd, kd, vd, qrd, TPC, SEQ):
    NRANK = SEQ // TPC
    TH = TPC // 2
    NT2 = TH // 512
    NG2 = TH // 128
    UPB = HPC * 3 * 2
    WKV = HPC * 256
    WQ = HPC * 128

    with contextlib.ExitStack() as es:
        sb_cb = es.enter_context(_sb(nc, "pj_cb", [128, 8 * TPC], BF16))
        sb_raw = [es.enter_context(_sb(nc, f"pj_raw{i}", [128, TH], BF16)) for i in range(2)]
        sb_fin = [es.enter_context(_sb(nc, f"pj_fin{i}", [128, TH], BF16)) for i in range(2)]
        sb_vt = [es.enter_context(_sb(nc, f"pj_vt{i}", [128, TH], BF16)) for i in range(2)]
        sb_junk = es.enter_context(_sb(nc, "pj_junk", [1, 4], BF16))
        ps_pr = [es.enter_context(_pt(nc, f"pj_pr{i}", [128, 1024], F32)) for i in range(2)]
        ps_t = es.enter_context(_pt(nc, "pj_t", [128, 1024], BF16))
        s_xl = es.enter_context(_sm(nc, "pj_xl"))
        s_pe = es.enter_context(_sm(nc, "pj_pe"))
        s_rt = es.enter_context(_sm(nc, "pj_rt"))
        s_ac = es.enter_context(_sm(nc, "pj_ac"))
        s_dv = es.enter_context(_sm(nc, "pj_dv"))
        s_st = es.enter_context(_sm(nc, "pj_st"))
        s_qr = es.enter_context(_sm(nc, "pj_qr"))

        def unit_info(u):
            tb, r = divmod(u, UPB)
            h, r2 = divmod(r, 6)
            o, half = divmod(r2, 2)
            return tb, h, o, half

        NU = NRANK * UPB
        for slot in range(NU + 1):
            with nc.Block(no_gpsimd_drain=True) as block:
                @block.sync
                def _(sy, slot=slot):
                    if slot < NU:
                        tb, h, o, half = unit_info(slot)
                        if slot % UPB == 0:
                            if tb >= 1:
                                sy.wait_ge(s_pe, tb * UPB)
                            sy.dma_start(
                                out=sb_cb[:, :].rearrange("p (c t) -> p c t", c=8),
                                in_=bass.AP(payf, tb * PAY1 * TPC,
                                            [[TPC, 128], [128 * TPC, 8], [1, TPC]]),
                            ).then_inc(s_xl, 16)
                    u = slot - 1
                    if u >= 0:
                        tb, h, o, half = unit_info(u)
                        sy.wait_ge(s_dv, u + 1)
                        t0 = tb * TPC + half * TH
                        if o < 2:
                            dst = kd if o == 0 else qd
                            sy.dma_start(out=dst.ap()[h * 128:(h + 1) * 128, t0:t0 + TH],
                                         in_=sb_fin[u % 2][:, :]).then_inc(s_st, 16)
                        else:
                            sy.dma_start(out=_vd_store_ap(vd, h, t0, NG2, SEQ),
                                         in_=sb_vt[u % 2][:, :].rearrange(
                                             "p (g d) -> p g d", d=128)).then_inc(s_st, 16)
                        sy.wait_ge(s_st, 16 * (u + 1))

                @block.tensor
                def _(t, slot=slot):
                    if slot < NU:
                        u = slot
                        tb, h, o, half = unit_info(u)
                        par = u % 2
                        t.wait_ge(s_xl, 16 * (tb + 1))
                        if u >= 2:
                            t.wait_ge(s_ac, u - 1)
                        for cc in range(4):
                            if o == 1:
                                w_ap = sb_wq[:, cc * WQ + h * 128:cc * WQ + h * 128 + 128]
                                cbase = 4 + cc
                            else:
                                wcol = h * 256 + (0 if o == 0 else 128)
                                w_ap = sb_w[:, cc * WKV + wcol:cc * WKV + wcol + 128]
                                cbase = cc
                            for tt in range(NT2):
                                c0 = half * TH + tt * 512
                                ins = t.matmul(ps_pr[par][:, tt * 512:(tt + 1) * 512], w_ap,
                                               sb_cb[:, cbase * TPC + c0:cbase * TPC + c0 + 512],
                                               start=(cc == 0), stop=(cc == 3))
                        ins.then_inc(s_pe, 1)
                    u = slot - 1
                    if u >= 0:
                        tb, h, o, half = unit_info(u)
                        par = u % 2
                        if o == 2:
                            t.wait_ge(s_ac, u + 1)
                            t.wait_ge(s_dv, u)
                            ins = None
                            for gi in range(NG2):
                                ins = t.transpose(ps_t[:, gi * 128:(gi + 1) * 128],
                                                  sb_raw[par][:, gi * 128:(gi + 1) * 128],
                                                  sb_ident[:, :])
                            ins.then_inc(s_rt, 1)

                @block.scalar
                def _(s, slot=slot):
                    u = slot - 1
                    if u >= 0:
                        tb, h, o, half = unit_info(u)
                        par = u % 2
                        s.wait_ge(s_pe, u + 1)
                        if u >= 2:
                            s.wait_ge(s_dv, u - 1)
                        bias = (sb_bkv[:, 2 * h:2 * h + 1] if o == 0 else
                                sb_bq[:, h:h + 1] if o == 1 else
                                sb_bkv[:, 2 * h + 1:2 * h + 2])
                        if o == 2:
                            s.activation(sb_raw[par][:, :], ps_pr[par][:, 0:TH],
                                         AF.Identity, bias=bias).then_inc(s_ac, 1)
                        else:
                            s.activation(sb_fin[u % 2][:, :], ps_pr[par][:, 0:TH],
                                         AF.Identity, bias=bias).then_inc(s_ac, 1)

                @block.vector
                def _(v, slot=slot):
                    u = slot - 1
                    if u >= 0:
                        tb, h, o, half = unit_info(u)
                        if u >= 2:
                            v.wait_ge(s_st, 16 * (u - 1))
                        if o == 2:
                            v.wait_ge(s_rt, sum(1 for j in range(u + 1)
                                                if unit_info(j)[2] == 2))
                            v.tensor_copy(sb_vt[u % 2][:, :], ps_t[:, 0:TH]).then_inc(s_dv, 1)
                        else:
                            v.wait_ge(s_ac, u + 1)
                            v.tensor_copy(sb_junk[0:1, 0:2],
                                          sb_fin[u % 2][0:1, 0:2]).then_inc(s_dv, 1)



def _emit_qr_copy(nc, payf, qrd, sb_meta, TPC, SEQ):
    NRANK = SEQ // TPC
    with contextlib.ExitStack() as es:
        s_qr = es.enter_context(_sm(nc, "qrc"))
        with nc.Block(no_gpsimd_drain=True) as block:
            @block.gpsimd
            def _(g_):
                for r in range(NRANK):
                    val = g_.value_load(sb_meta[0:1, 8 + r:9 + r],
                                        min_val=0, max_val=NRANK * PAY1 - 256)
                    g_.dma_start(
                        out=qrd.ap()[:, r * TPC:(r + 1) * TPC],
                        in_=payf.ap()[bass.ds(val, HPC * 64), :],
                    ).then_inc(s_qr, 16)
                g_.wait_ge(s_qr, 16 * NRANK)


def _emit_sdpa(nc, L, qd, kd, vd, qrd, payf, meta, sb_ones, sb_or, sb_caus,
               attnout, TPC, SEQ):
    NRANK = SEQ // TPC
    SQT = SEQ // 512
    DPAY = D if L == 0 else PAY1

    with contextlib.ExitStack() as es:
        sb_meta = es.enter_context(_sb(nc, "sd_meta", [1, 8], I32))
        sb_k = es.enter_context(_sb(nc, "sd_k", [128, SEQ], BF16))
        sb_v = es.enter_context(_sb(nc, "sd_v", [128, SEQ], BF16))
        sb_qt = [es.enter_context(_sb(nc, f"sd_q{i}", [128, 512], BF16)) for i in range(2)]
        sb_p = [es.enter_context(_sb(nc, f"sd_p{i}", [128, 512], BF16)) for i in range(3)]
        sb_pt = [es.enter_context(_sb(nc, f"sd_pt{i}", [128, 512], BF16)) for i in range(2)]
        sb_li = es.enter_context(_sb(nc, "sd_li", [1, 512], F32))
        sb_bc = [es.enter_context(_sb(nc, f"sd_bc{i}", [128, 512], F32)) for i in range(2)]
        sb_ao = [es.enter_context(_sb(nc, f"sd_ao{i}", [128, 512], BF16)) for i in range(2)]
        if L == 1:
            sb_kr = es.enter_context(_sb(nc, "sd_kr", [64, SEQ], BF16))
            sb_qr = [es.enter_context(_sb(nc, f"sd_qr{i}", [64, 512], BF16)) for i in range(2)]
        ps_sc = [es.enter_context(_pt(nc, f"sd_sc{i}", [128, 512], F32)) for i in range(3)]
        ps_out = [es.enter_context(_pt(nc, f"sd_o{i}", [128, 512], F32)) for i in range(2)]
        ps_l = [es.enter_context(_pt(nc, f"sd_l{i}", [1, 512], F32)) for i in range(2)]
        ps_bc = es.enter_context(_pt(nc, "sd_bc", [128, 512], F32))
        s_kv = es.enter_context(_sm(nc, "sd_kv"))
        s_ql = [es.enter_context(_sm(nc, f"sd_ql{i}")) for i in range(2)]
        s_mm = es.enter_context(_sm(nc, "sd_mm"))
        s_ex = es.enter_context(_sm(nc, "sd_ex"))
        s_mk = es.enter_context(_sm(nc, "sd_mk"))
        s_pv = es.enter_context(_sm(nc, "sd_pv"))
        s_bm = es.enter_context(_sm(nc, "sd_bm"))
        s_bv = es.enter_context(_sm(nc, "sd_bv"))
        s_dc = es.enter_context(_sm(nc, "sd_dc"))
        s_fo = es.enter_context(_sm(nc, "sd_fo"))
        s_st = es.enter_context(_sm(nc, "sd_st"))

        def tiles_of():
            tl = []
            for qt in range(SQT):
                nk = 4 * qt + 4
                for kt in range(nk):
                    tl.append((qt, kt, kt >= 4 * qt))
            return tl

        TL = tiles_of()
        NTILE = len(TL)
        qt_last = {}
        qt_first = {}
        for m, (qt, kt, dg) in enumerate(TL):
            qt_last[qt] = m
            if qt not in qt_first:
                qt_first[qt] = m
        diag_upto = np.cumsum([1 if t[2] else 0 for t in TL]).tolist()

        nkv = 1 + 1 + (NRANK if L == 1 else 0)

        for h in range(HPC):
            with nc.Block(no_gpsimd_drain=True) as block:
                @block.sync
                def _(sy, h=h):
                    base_kv = h * nkv * 16
                    sy.dma_start(out=sb_k[:, :],
                                 in_=kd.ap()[h * 128:(h + 1) * 128, :]).then_inc(s_kv, 16)
                    sy.dma_start(out=sb_v[:, :].rearrange("p (g d) -> p g d", d=128),
                                 in_=bass.AP(vd, h * SEQ * 128,
                                             [[128, 128], [128 * 128, SEQ // 128], [1, 128]]),
                                 ).then_inc(s_kv, 16)
                    if L == 1:
                        for r in range(NRANK):
                            sy.dma_start(out=sb_kr[:, r * TPC:(r + 1) * TPC],
                                         in_=payf.ap()[r * PAY1 + 1024:r * PAY1 + 1088, :],
                                         ).then_inc(s_kv, 16)
                    for qt in range(SQT + 2):
                        if qt < SQT:
                            if qt >= 2:
                                sy.wait_ge(s_pv, h * NTILE + qt_last[qt - 2] + 1)
                            sy.dma_start(out=sb_qt[qt % 2][:, :],
                                         in_=qd.ap()[h * 128:(h + 1) * 128,
                                                     qt * 512:(qt + 1) * 512]).then_inc(s_ql[qt % 2], 16)
                            if L == 1:
                                sy.dma_start(out=sb_qr[qt % 2][0:64, :],
                                             in_=qrd.ap()[h * 64:(h + 1) * 64,
                                                          qt * 512:(qt + 1) * 512]).then_inc(s_ql[qt % 2], 16)
                        u = qt - 2
                        if u >= 0:
                            sy.wait_ge(s_fo, h * SQT + u + 1)
                            sy.dma_start(out=attnout[h * 128:(h + 1) * 128,
                                                     u * 512:(u + 1) * 512],
                                         in_=sb_ao[u % 2][:, :]).then_inc(s_st, 16)
                            sy.wait_ge(s_st, 16 * (h * SQT + u + 1))

                @block.tensor
                def _(t, h=h):
                    NQL = 1 if L == 0 else 2
                    t.wait_ge(s_kv, (h + 1) * nkv * 16)
                    mbase = h * NTILE

                    def emit_scores(m):
                        qt, kt, dg = TL[m]
                        if m == qt_first[qt]:
                            t.wait_ge(s_ql[qt % 2], 16 * NQL * ((h * SQT + qt) // 2 + 1))
                            if qt >= 2:
                                t.wait_ge(s_fo, h * SQT + qt - 1)
                        if m >= 3:
                            t.wait_ge(s_ex, mbase + m - 2)
                        ins = t.matmul(ps_sc[m % 3][:, :],
                                       sb_k[:, kt * 128:(kt + 1) * 128],
                                       sb_qt[qt % 2][:, :],
                                       start=True, stop=(L == 0))
                        if L == 1:
                            ins = t.matmul(ps_sc[m % 3][:, :],
                                           sb_kr[0:64, kt * 128:(kt + 1) * 128],
                                           sb_qr[qt % 2][0:64, :],
                                           start=False, stop=True)
                        ins.then_inc(s_mm, 1)

                    def emit_pv(m):
                        qt, kt, dg = TL[m]
                        if dg:
                            t.wait_ge(s_mk, _dg_count(TL, h, m, diag_upto))
                        else:
                            t.wait_ge(s_ex, mbase + m + 1)
                        first = (m == qt_first[qt])
                        last = (m == qt_last[qt])
                        t.matmul(ps_out[qt % 2][:, :],
                                 sb_v[:, kt * 128:(kt + 1) * 128],
                                 sb_p[m % 3][:, :],
                                 start=first, stop=last)
                        t.matmul(ps_l[qt % 2][0:1, :],
                                 sb_ones[:, 0:1],
                                 sb_p[m % 3][:, :],
                                 start=first, stop=last).then_inc(s_pv, 1)
                        if last:
                            t.wait_ge(s_dc, h * SQT + qt + 1)
                            if h * SQT + qt >= 1:
                                t.wait_ge(s_bv, h * SQT + qt)
                            t.matmul(ps_bc[:, :], sb_or[:, :], sb_li[:, :],
                                     start=True, stop=True).then_inc(s_bm, 1)

                    for m in range(NTILE):
                        emit_scores(m)
                        if m >= 2:
                            emit_pv(m - 2)
                    emit_pv(NTILE - 2)
                    emit_pv(NTILE - 1)

                @block.scalar
                def _(s, h=h):
                    mbase = h * NTILE
                    dbase = 0 if h == 0 else diag_upto[-1] * h
                    for m, (qt, kt, dg) in enumerate(TL):
                        s.wait_ge(s_mm, mbase + m + 1)
                        if dg:
                            dgc = dbase + diag_upto[m]
                            if dgc >= 3:
                                s.wait_ge(s_mk, dgc - 2)
                            s.activation(sb_pt[(dgc - 1) % 2][:, :], ps_sc[m % 3][:, :],
                                         AF.Exp, scale=ISQ).then_inc(s_ex, 1)
                        else:
                            if m >= 3:
                                s.wait_ge(s_pv, mbase + m - 2)
                            s.activation(sb_p[m % 3][:, :], ps_sc[m % 3][:, :],
                                         AF.Exp, scale=ISQ).then_inc(s_ex, 1)
                        if m == qt_last[qt]:
                            s.wait_ge(s_bm, h * SQT + qt + 1)
                            s.activation(sb_bc[qt % 2][:, :], ps_bc[:, :],
                                         AF.Identity).then_inc(s_bv, 1)

                @block.vector
                def _(v, h=h):
                    mbase = h * NTILE
                    dbase = 0 if h == 0 else diag_upto[-1] * h
                    for m, (qt, kt, dg) in enumerate(TL):
                        if dg:
                            dgc = dbase + diag_upto[m]
                            v.wait_ge(s_ex, mbase + m + 1)
                            if m >= 3:
                                v.wait_ge(s_pv, mbase + m - 2)
                            off = 512 - (kt * 128 - qt * 512)
                            v.tensor_tensor(sb_p[m % 3][:, :], sb_pt[(dgc - 1) % 2][:, :],
                                            sb_caus[:, off:off + 512], ALU.mult).then_inc(s_mk, 1)
                        if m == qt_last[qt]:
                            iq = h * SQT + qt
                            v.wait_ge(s_pv, mbase + m + 1)
                            if iq >= 1:
                                v.wait_ge(s_bm, iq)
                            v.reciprocal(sb_li[:, :], ps_l[qt % 2][0:1, :]).then_inc(s_dc, 1)
                            v.wait_ge(s_bv, iq + 1)
                            if iq >= 2:
                                v.wait_ge(s_st, 16 * (iq - 1))
                            v.tensor_tensor(sb_ao[qt % 2][:, :], ps_out[qt % 2][:, :],
                                            sb_bc[qt % 2][:, :], ALU.mult).then_inc(s_fo, 1)

        with nc.Block(no_gpsimd_drain=True) as block:
            @block.sync
            def _(sy):
                sy.wait_ge(s_st, 16 * HPC * SQT)


def _dg_count(TL, h, m, diag_upto):
    return (diag_upto[-1] * h) + diag_upto[m]



def _emit_post(nc, sin_dma, wo_cm, bias_row, ones_row_sb, g_in, x_in, xo,
               TPC, s_extld=None):
    NTG = TPC // 128
    es = contextlib.ExitStack()
    with es:
        sb_s = es.enter_context(_sb(nc, "po_s", [128, DC * TPC], BF16))
        sb_w = es.enter_context(_sb(nc, "po_w", [128, DC * D], BF16))
        sb_br = es.enter_context(_sb(nc, "po_br", [1, D], BF16))
        sb_one1 = es.enter_context(_sb(nc, "po_one1", [1, 128], BF16))
        sb_g = [es.enter_context(_sb(nc, f"po_g{i}", [128, D], F32)) for i in range(2)]
        sb_x = [es.enter_context(_sb(nc, f"po_x{i}", [128, D], F32)) for i in range(2)]
        sb_sb = [es.enter_context(_sb(nc, f"po_sb{i}", [128, D], F32)) for i in range(2)]
        ps_s = [es.enter_context(_pt(nc, f"po_ps{i}", [128, D], F32)) for i in range(2)]
        s_l0 = es.enter_context(_sm(nc, "po_l0"))
        s_lb = es.enter_context(_sm(nc, "po_lb"))
        s_ld = [es.enter_context(_sm(nc, f"po_ld{i}")) for i in range(2)]
        s_pe = es.enter_context(_sm(nc, "po_pe"))
        s_ac = es.enter_context(_sm(nc, "po_ac"))
        s_dv = es.enter_context(_sm(nc, "po_dv"))
        s_st = es.enter_context(_sm(nc, "po_st"))

        nl0 = [0]
        with nc.Block(no_gpsimd_drain=True) as block:
            @block.gpsimd
            def _(g_):
                g_.memset(sb_one1[:, :], 1.0)
                g_.dma_start(out=sb_w[:, :].rearrange("p (c n) -> p c n", c=DC),
                             in_=wo_cm.rearrange("(c p) n -> p c n", p=128)).then_inc(s_lb, 16)
                g_.dma_start(out=sb_br[:, :], in_=bias_row).then_inc(s_lb, 16)
                g_.wait_ge(s_lb, 32)

        for slot in range(NTG + 1):
            with nc.Block(no_gpsimd_drain=True) as block:
                if slot == 0:
                    @block.gpsimd
                    def _(g_):
                        nl0[0] += sin_dma(g_, sb_s, s_l0)

                @block.sync
                def _(sy, slot=slot):
                    if slot < NTG:
                        tg = slot
                        if tg >= 2:
                            sy.wait_ge(s_dv, 2 * (tg - 1))
                            sy.wait_ge(s_st, 16 * (tg - 1))
                        sy.dma_start(out=sb_g[tg % 2][:, :],
                                     in_=g_in[tg * 128:(tg + 1) * 128, :]).then_inc(s_ld[tg % 2], 16)
                        sy.dma_start(out=sb_x[tg % 2][:, :],
                                     in_=x_in[tg * 128:(tg + 1) * 128, :]).then_inc(s_ld[tg % 2], 16)
                    u = slot - 1
                    if u >= 0:
                        sy.wait_ge(s_dv, 2 * (u + 1))
                        if u >= 1:
                            sy.wait_ge(s_st, 16 * u)
                        sy.dma_start(out=xo[u * 128:(u + 1) * 128, :],
                                     in_=sb_x[u % 2][:, :]).then_inc(s_st, 16)
                        if u == NTG - 1:
                            sy.wait_ge(s_st, 16 * NTG)

                @block.tensor
                def _(t, slot=slot):
                    if slot < NTG:
                        tg = slot
                        par = tg % 2
                        if tg == 0:
                            t.wait_ge(s_l0, nl0[0])
                            t.wait_ge(s_lb, 32)
                            if s_extld is not None:
                                t.wait_ge(s_extld[0], s_extld[1])
                        if tg >= 2:
                            t.wait_ge(s_ac, tg - 1)
                        ins = None
                        for hw in range(D // 512):
                            t.matmul(ps_s[par][:, hw * 512:(hw + 1) * 512],
                                     sb_one1[:, :], sb_br[:, hw * 512:(hw + 1) * 512],
                                     start=True, stop=False)
                        for a in range(DC):
                            for hw in range(D // 512):
                                ins = t.matmul(ps_s[par][:, hw * 512:(hw + 1) * 512],
                                               sb_s[:, a * TPC + tg * 128:a * TPC + (tg + 1) * 128],
                                               sb_w[:, a * D + hw * 512:a * D + (hw + 1) * 512],
                                               start=False, stop=(a == DC - 1))
                        ins.then_inc(s_pe, 1)

                @block.scalar
                def _(s, slot=slot):
                    u = slot - 1
                    if u >= 0:
                        s.wait_ge(s_pe, u + 1)
                        if u >= 2:
                            s.wait_ge(s_dv, 2 * (u - 1))
                        s.activation(sb_sb[u % 2][:, :], ps_s[u % 2][:, 0:D],
                                     AF.Identity).then_inc(s_ac, 1)

                @block.vector
                def _(v, slot=slot):
                    u = slot - 1
                    if u >= 0:
                        par = u % 2
                        v.wait_ge(s_ac, u + 1)
                        v.wait_ge(s_ld[u % 2], 32 * (u // 2 + 1))
                        v.tensor_tensor(sb_sb[par][:, :], sb_sb[par][:, :],
                                        sb_g[par][:, :], ALU.mult).then_inc(s_dv, 1)
                        v.wait_ge(s_dv, 2 * u + 1)
                        v.tensor_tensor(sb_x[par][:, :], sb_sb[par][:, :],
                                        sb_x[par][:, :], ALU.add).then_inc(s_dv, 1)


def build_C(TPC):
    NTG = TPC // 128
    nc = bass.Bass("TRN2", target_bir_lowering=False)
    s_in = nc.dram_tensor("s_in", [D, TPC], BF16, kind="ExternalInput").ap()
    wo_cm = nc.dram_tensor("wo_cm", [D, D], F32, kind="ExternalInput").ap()
    bias_row = nc.dram_tensor("bias_row", [1, D], F32, kind="ExternalInput").ap()
    ones_row = nc.dram_tensor("ones_row", [1, 128], F32, kind="ExternalInput").ap()
    g_in = nc.dram_tensor("g", [TPC, D], F32, kind="ExternalInput").ap()
    x_in = nc.dram_tensor("x", [TPC, D], F32, kind="ExternalInput").ap()
    xo = nc.dram_tensor("xo", [TPC, D], F32, kind="ExternalOutput").ap()
    es = contextlib.ExitStack()
    with es:
        sb_or = es.enter_context(_sb(nc, "c_or", [1, 128], F32))
        s_c = es.enter_context(_sm(nc, "c_or_ld"))
        with nc.Block(no_gpsimd_drain=True) as block:
            @block.gpsimd
            def _(g_):
                g_.dma_start(out=sb_or[:, :], in_=ones_row).then_inc(s_c, 16)
                g_.wait_ge(s_c, 16)

        def sin_dma(g_, sb_s, sem):
            g_.dma_start(out=sb_s[:, :].rearrange("p (c t) -> p c t", c=DC),
                         in_=s_in.rearrange("(c p) t -> p c t", p=128)).then_inc(sem, 16)
            return 16

        _emit_post(nc, sin_dma, wo_cm, bias_row, sb_or, g_in, x_in, xo, TPC)
    return nc


def _emit_tr(nc, src, dst, R, C, sb_ident, px, src_off=0):
    NR = R // 128
    NCC = C // 128
    RB = min(NR, 64)
    NPASS = (NR + RB - 1) // RB
    es = contextlib.ExitStack()
    with es:
        sb_a = [es.enter_context(_sb(nc, f"{px}tr_a{i}", [128, RB * 128], BF16)) for i in range(2)]
        sb_o = [es.enter_context(_sb(nc, f"{px}tr_o{i}", [128, RB * 128], BF16)) for i in range(2)]
        ps = [es.enter_context(_pt(nc, f"{px}tr_ps{i}", [128, 1024], BF16)) for i in range(2)]
        s_ld = [es.enter_context(_sm(nc, f"{px}tr_ld{i}")) for i in range(2)]
        s_pe = es.enter_context(_sm(nc, f"{px}tr_pe"))
        s_dv = es.enter_context(_sm(nc, f"{px}tr_dv"))
        s_st = [es.enter_context(_sm(nc, f"{px}tr_st{i}")) for i in range(2)]
        G = [0]

        for cc in range(NCC):
            for p in range(NPASS):
                it = cc * NPASS + p
                par = it % 2
                rb = min(RB, NR - p * RB)
                ng = (rb + 7) // 8
                g0 = G[0]
                with nc.Block(no_gpsimd_drain=True) as block:
                    @block.sync
                    def _(sy, cc=cc, p=p, it=it, par=par, rb=rb, ng=ng, g0=g0):
                        sy.dma_start(
                            out=sb_a[par][:, 0:rb * 128].rearrange("q (r t) -> q r t", t=128),
                            in_=bass.AP(src, src_off + (p * RB) * 128 * C + cc * 128,
                                        [[C, 128], [128 * C, rb], [1, 128]]),
                        ).then_inc(s_ld[par], 16)
                        sy.wait_ge(s_dv, g0 + ng)
                        sy.dma_start(
                            out=bass.AP(dst, cc * 128 * R + p * RB * 128,
                                        [[R, 128], [1, rb * 128]]),
                            in_=sb_o[par][:, 0:rb * 128],
                        ).then_inc(s_st[par], 16)
                        sy.wait_ge(s_st[par], 16 * (it // 2 + 1))

                    @block.tensor
                    def _(t, it=it, par=par, rb=rb, ng=ng, g0=g0):
                        t.wait_ge(s_ld[par], 16 * (it // 2 + 1))
                        for g8 in range(ng):
                            gg = g0 + g8
                            if gg >= 2:
                                t.wait_ge(s_dv, gg - 1)
                            n8 = min(8, rb - g8 * 8)
                            ins = None
                            for j in range(n8):
                                r_ = g8 * 8 + j
                                ins = t.transpose(ps[gg % 2][:, j * 128:(j + 1) * 128],
                                                  sb_a[par][:, r_ * 128:(r_ + 1) * 128],
                                                  sb_ident[:, :])
                            ins.then_inc(s_pe, 1)

                    @block.vector
                    def _(v, it=it, par=par, rb=rb, ng=ng, g0=g0):
                        for g8 in range(ng):
                            gg = g0 + g8
                            n8 = min(8, rb - g8 * 8)
                            v.wait_ge(s_pe, gg + 1)
                            v.tensor_copy(
                                sb_o[par][:, g8 * 8 * 128:(g8 * 8 + n8) * 128]
                                .rearrange("q (r t) -> q r t", t=128),
                                ps[gg % 2][:, 0:n8 * 128].rearrange("q (r t) -> q r t", t=128),
                            ).then_inc(s_dv, 1)
                G[0] += ng



def _emit_wfold(nc, wupT, wgT, wdT, sb_h128, sb_h16, sb_h64, upf, gatef, downf, u1, u1g, u2):
    COLS = 1024
    HCOL = 256
    es = contextlib.ExitStack()
    with es:
        sb_in = [es.enter_context(_sb(nc, f"w_in{i}", [128, 2048], BF16)) for i in range(2)]
        sb_o = [es.enter_context(_sb(nc, f"w_o{i}", [128, 2048], BF16)) for i in range(2)]
        ps = [es.enter_context(_pt(nc, f"w_ps{i}", [128, 2048], F32)) for i in range(2)]
        s_ld = [es.enter_context(_sm(nc, f"w_ld{i}")) for i in range(2)]
        s_ls = [es.enter_context(_sm(nc, f"w_ls{i}")) for i in range(2)]
        s_pe = es.enter_context(_sm(nc, "w_pe"))
        s_ac = es.enter_context(_sm(nc, "w_ac"))
        s_st = [es.enter_context(_sm(nc, f"w_st{i}")) for i in range(2)]

        descs1, descs2 = [], []
        for (wt, out_d, mid_d, cols, nA) in [
            (wupT, upf, u1, COLS, 16), (wgT, gatef, u1g, COLS, 16),
            (wdT, downf, u2, HCOL, 64),
        ]:
            hm = sb_h16 if nA == 16 else sb_h64
            nrm = 1.0 / math.sqrt(128.0 * nA)
            BA = max(1, 2048 // cols // 2)
            for a0 in range(0, nA, BA):
                ba = min(BA, nA - a0)
                descs1.append((
                    bass.AP(wt, a0 * 128 * cols, [[cols, 128], [128 * cols, ba], [1, cols]]),
                    bass.AP(mid_d, a0 * 128 * cols, [[cols, 128], [128 * cols, ba], [1, cols]]),
                    128, ba * cols, sb_h128[:, :], nrm))
            BC = max(1, 2048 // cols)
            for c0 in range(0, 128, BC):
                bc_ = min(BC, 128 - c0)
                descs2.append((
                    bass.AP(mid_d, c0 * cols, [[128 * cols, nA], [cols, bc_], [1, cols]]),
                    bass.AP(out_d, c0 * cols, [[128 * cols, nA], [cols, bc_], [1, cols]]),
                    nA, bc_ * cols, hm[0:nA, 0:nA], 1.0))

        def emit_phase(descs, base_it, swdge):
            assert base_it % 2 == 0
            s_l = s_ld if swdge else s_ls

            def load(eng, idx):
                _, _, rows, cols, _, _ = descs[idx]
                eng.dma_start(out=sb_in[(base_it + idx) % 2][0:rows, 0:cols],
                              in_=descs[idx][0]).then_inc(s_l[(base_it + idx) % 2], 16)

            with nc.Block(no_gpsimd_drain=True) as block:
                if swdge:
                    @block.gpsimd
                    def _(g_):
                        load(g_, 0)
                else:
                    @block.sync
                    def _(sy):
                        load(sy, 0)

            for idx in range(len(descs)):
                it = base_it + idx
                par = it % 2
                _, out_ap, rows, cols, lhs_ap, scale = descs[idx]
                with nc.Block(no_gpsimd_drain=True) as block:
                    if swdge:
                        @block.gpsimd
                        def _(g_, idx=idx):
                            if idx + 1 < len(descs):
                                load(g_, idx + 1)

                    @block.tensor
                    def _(t, it=it, par=par, idx=idx, rows=rows, cols=cols,
                          lhs_ap=lhs_ap):
                        t.wait_ge(s_l[par], 16 * (idx // 2 + 1))
                        ins = None
                        for j in range((cols + 511) // 512):
                            w = min(512, cols - j * 512)
                            ins = t.matmul(ps[par][0:rows, j * 512:j * 512 + w], lhs_ap,
                                           sb_in[par][0:rows, j * 512:j * 512 + w],
                                           start=True, stop=True)
                        ins.then_inc(s_pe, 1)

                    @block.scalar
                    def _(s, it=it, par=par, rows=rows, cols=cols, scale=scale):
                        s.wait_ge(s_pe, it + 1)
                        if it >= 2:
                            s.wait_ge(s_st[par], 16 * (it // 2))
                        s.activation(sb_o[par][0:rows, 0:cols], ps[par][0:rows, 0:cols],
                                     AF.Identity, scale=scale).then_inc(s_ac, 1)

                    @block.sync
                    def _(sy, it=it, par=par, idx=idx, out_ap=out_ap, rows=rows,
                          cols=cols):
                        if not swdge and idx + 1 < len(descs):
                            load(sy, idx + 1)
                        sy.wait_ge(s_ac, it + 1)
                        sy.dma_start(out=out_ap,
                                     in_=sb_o[par][0:rows, 0:cols]).then_inc(s_st[par], 16)

        emit_phase(descs1, 0, swdge=True)
        n1 = len(descs1)
        with nc.Block(no_gpsimd_drain=True) as block:
            @block.sync
            def _(sy):
                sy.wait_ge(s_st[0], 16 * ((n1 + 1) // 2))
                sy.wait_ge(s_st[1], 16 * (n1 // 2))
        emit_phase(descs2, n1, swdge=False)
        n2 = n1 + len(descs2)
        with nc.Block(no_gpsimd_drain=True) as block:
            @block.sync
            def _(sy):
                sy.wait_ge(s_st[0], 16 * ((n2 + 1) // 2))
                sy.wait_ge(s_st[1], 16 * (n2 // 2))


def build_W():
    nc = bass.Bass("TRN2", target_bir_lowering=False)
    wupT = nc.dram_tensor("wupT", [D, 1024], F32, kind="ExternalInput")
    wgT = nc.dram_tensor("wgT", [D, 1024], F32, kind="ExternalInput")
    wdT = nc.dram_tensor("wdT", [INTER, 256], F32, kind="ExternalInput")
    h128 = nc.dram_tensor("h128", [128, 128], BF16, kind="ExternalInput")
    h16 = nc.dram_tensor("h16", [16, 16], BF16, kind="ExternalInput")
    h64 = nc.dram_tensor("h64", [64, 64], BF16, kind="ExternalInput")
    upf = nc.dram_tensor("upf", [D, 1024], BF16, kind="ExternalOutput")
    gatef = nc.dram_tensor("gatef", [D, 1024], BF16, kind="ExternalOutput")
    downf = nc.dram_tensor("downf", [INTER, 256], BF16, kind="ExternalOutput")
    u1 = nc.dram_tensor("u1", [D, 1024], BF16)
    u1g = nc.dram_tensor("u1g", [D, 1024], BF16)
    u2 = nc.dram_tensor("u2", [INTER, 256], BF16)
    es = contextlib.ExitStack()
    with es:
        sb_h128 = es.enter_context(_sb(nc, "wc_h128", [128, 128], BF16))
        sb_h16 = es.enter_context(_sb(nc, "wc_h16", [16, 16], BF16))
        sb_h64 = es.enter_context(_sb(nc, "wc_h64", [64, 64], BF16))
        s_l0 = es.enter_context(_sm(nc, "wc_l0"))
        with nc.Block(no_gpsimd_drain=True) as block:
            @block.gpsimd
            def _(g_):
                for dst, hsrc in [(sb_h128, h128), (sb_h16, h16), (sb_h64, h64)]:
                    g_.dma_start(out=dst[0:hsrc.shape[0], 0:hsrc.shape[1]],
                                 in_=hsrc.ap()).then_inc(s_l0, 16)
                g_.wait_ge(s_l0, 48)
        _emit_wfold(nc, wupT, wgT, wdT, sb_h128, sb_h16, sb_h64, upf, gatef, downf, u1, u1g, u2)
    return nc



TPC = 2048
SEQ = 8192
NCORE = 8


def _rs128(v):
    return np.ascontiguousarray(_f32(v).reshape(-1, 128).T)


def _build_wqkv_l0(inp, h0):
    blk = np.empty((D, HPC * 384), np.float32)
    qs, qg = inp["L0_attn_Wq_sem"], inp["L0_attn_Wq_geo"]
    ks, kg = inp["L0_attn_Wk_sem"], inp["L0_attn_Wk_geo"]
    wv = inp["L0_attn_Wv"]
    for j in range(HPC):
        h = h0 + j
        blk[:, j * 384 + 0:j * 384 + 64] = qs[h * 64:(h + 1) * 64, :].T
        blk[:, j * 384 + 64:j * 384 + 128] = qg[h * 64:(h + 1) * 64, :].T
        blk[:, j * 384 + 128:j * 384 + 192] = ks[h * 64:(h + 1) * 64, :].T
        blk[:, j * 384 + 192:j * 384 + 256] = kg[h * 64:(h + 1) * 64, :].T
        blk[:, j * 384 + 256:j * 384 + 384] = wv[h * 128:(h + 1) * 128, :].T
    return np.ascontiguousarray(blk)


def _slice_qrm(payf_b, h0, tpc=None, seq=None):
    tpc = tpc or TPC
    seq = seq or SEQ
    nrank = seq // tpc
    out = np.empty((HPC * 64, seq), payf_b.dtype)
    for r in range(nrank):
        r0 = r * PAY1 + QR_BASE + h0 * 64
        out[:, r * tpc:(r + 1) * tpc] = payf_b[r0:r0 + HPC * 64, :]
    return np.ascontiguousarray(out)


def _build_meta(h0):
    m = np.zeros((1, 32), np.int32)
    m[0, 0] = QR_BASE + h0 * 64
    for hloc in range(HPC):
        for r in range(SEQ // TPC):
            m[0, 8 + hloc * 4 + r] = r * PAY1 + QR_BASE + (h0 + hloc) * 64
    return m


def _run(nc, in_maps):
    return run_bass_kernel_spmd(nc, in_maps, core_ids=list(range(NCORE)))


def kernel(**inp):
    C = host_constants(SEQ)
    x = _f32(inp["L0_attn_x"])
    xs = [np.ascontiguousarray(x[c // 4, (c % 4) * TPC:(c % 4 + 1) * TPC, :])
          for c in range(NCORE)]

    ncW = build_W()
    inW = []
    for c in range(NCORE):
        inW.append(dict(
            wupT=np.ascontiguousarray(_f32(inp["L1_ffn_W_up"])[c * 1024:(c + 1) * 1024, :].T),
            wgT=np.ascontiguousarray(_f32(inp["L1_ffn_W_gate"])[c * 1024:(c + 1) * 1024, :].T),
            wdT=np.ascontiguousarray(_f32(inp["L1_ffn_W_down"])[c * 256:(c + 1) * 256, :].T),
            h128=C["h128"], h16=C["h16"], h64=C["h64"]))
    rW = _run(ncW, inW)
    wupf = np.concatenate([np.asarray(rW.results[c]["upf"]) for c in range(NCORE)], axis=1)
    wgf = np.concatenate([np.asarray(rW.results[c]["gatef"]) for c in range(NCORE)], axis=1)
    wdf = np.concatenate([np.asarray(rW.results[c]["downf"]) for c in range(NCORE)], axis=1)

    wsug = _f32(inp["L1_ffn_Wscale_ug"])
    for L in (0, 1):
        ncA = build_A(L, TPC)
        inA = []
        for c in range(NCORE):
            qc = c % 4
            sl = slice(qc * TPC, (qc + 1) * TPC)
            d = dict(x=xs[c], ident=C["ident"], ones_row=C["ones_row"])
            if L == 0:
                d.update(wg=inp["L0_ffn_w_gate"], wu=inp["L0_ffn_w_up"],
                         wd=inp["L0_ffn_w_down"],
                         sg_rs=_rs128(inp["L0_ffn_s_gate"]),
                         su_rs=_rs128(inp["L0_ffn_s_up"]),
                         sd_row=_f32(inp["L0_ffn_s_down"])[None, :])
            else:
                bpre = np.zeros((128, 17), np.float32)
                wdb = _f32(inp["L1_attn_W_down_b"])
                for ocx in range(8):
                    bpre[:, ocx] = wdb[ocx * 128:(ocx + 1) * 128]
                bpre[0:64, 8] = _f32(inp["L1_attn_Wkr_b"])
                wqrb = _f32(inp["L1_attn_Wqr_b"])
                for j in range(8):
                    bpre[:, 9 + j] = wqrb[j * 128:(j + 1) * 128]
                d.update(wg=wgf, wu=wupf, wd=wdf,
                         sg_rs=_rs128(wsug[INTER:]), su_rs=_rs128(wsug[:INTER]),
                         sd_row=_f32(inp["L1_ffn_Wscale_dp"])[None, :],
                         wpre=np.ascontiguousarray(np.hstack([
                             _f32(inp["L1_attn_W_down_k"]),
                             _f32(inp["L1_attn_Wkr_k"]),
                             _f32(inp["L1_attn_Wqr_k"])])),
                         bpre_rs=bpre,
                         krcos=np.ascontiguousarray(C["l1_kr_cos"][:, sl]),
                         krsin=np.ascontiguousarray(C["l1_kr_sin"][:, sl]),
                         qrcos=np.ascontiguousarray(C["l1_qr_cos"][:, sl]),
                         qrsin=np.ascontiguousarray(C["l1_qr_sin"][:, sl]),
                         permT=C["l1_permT"],
                         sudp_rs=_rs128(inp["L1_ffn_SU_dp"]),
                         suug_rs=_rs128(inp["L1_ffn_SU_ug"]))
            inA.append(d)
        rA = _run(ncA, inA)
        pays = [np.asarray(rA.results[c]["pay"]) for c in range(NCORE)]
        gs = [np.asarray(rA.results[c]["g"]) for c in range(NCORE)]
        payf = {b: np.ascontiguousarray(np.concatenate([pays[b * 4 + r] for r in range(4)], axis=0))
                for b in (0, 1)}

        ncB = build_B(L, TPC, SEQ)
        inB = []
        for c in range(NCORE):
            b = c // 4
            h0 = (c % 4) * HPC
            d = dict(payf=payf[b], ident=C["ident"], ones_col=C["ones_col"],
                     ones_row=C["ones_row"], causal=C["causal"],
                     meta=_build_meta(h0))
            if L == 1:
                d["qrm"] = _slice_qrm(payf[b], h0)
            if L == 0:
                d.update(wqkv=_build_wqkv_l0(inp, h0), cmask=C["l0_cmask"],
                         smask=C["l0_smask"], permT=C["l0_permT"])
            else:
                wukv_f = _f32(inp["L1_attn_W_uKV_k"])
                wuq_f = _f32(inp["L1_attn_W_uQ_k"])
                bkv_f = _f32(inp["L1_attn_W_uKV_b"])
                bq_f = _f32(inp["L1_attn_W_uQ_b"])
                wukv = np.empty((512, HPC * 256), np.float32)
                bkv = np.empty((128, HPC * 2), np.float32)
                wuq = np.empty((512, HPC * 128), np.float32)
                bq = np.empty((128, HPC), np.float32)
                for j in range(HPC):
                    h = h0 + j
                    wukv[:, j * 256:j * 256 + 128] = wukv_f[:, h * 128:(h + 1) * 128]
                    wukv[:, j * 256 + 128:j * 256 + 256] = wukv_f[:, 2048 + h * 128:2048 + (h + 1) * 128]
                    bkv[:, 2 * j] = bkv_f[h * 128:(h + 1) * 128]
                    bkv[:, 2 * j + 1] = bkv_f[2048 + h * 128:2048 + (h + 1) * 128]
                    wuq[:, j * 128:(j + 1) * 128] = wuq_f[:, h * 128:(h + 1) * 128]
                    bq[:, j] = bq_f[h * 128:(h + 1) * 128]
                d.update(wukv=np.ascontiguousarray(wukv), wuq=np.ascontiguousarray(wuq),
                         bkv_rs=np.ascontiguousarray(bkv), bq_rs=np.ascontiguousarray(bq))
            inB.append(d)
        rB = _run(ncB, inB)
        att = [np.asarray(rB.results[c]["attnout"]) for c in range(NCORE)]

        ncC = build_C(TPC)
        inC = []
        for c in range(NCORE):
            b = c // 4
            qc = c % 4
            sl = slice(qc * TPC, (qc + 1) * TPC)
            s_in = np.ascontiguousarray(
                np.concatenate([att[b * 4 + j][:, sl] for j in range(4)], axis=0))
            if L == 0:
                wo_cm = np.ascontiguousarray(_f32(inp["L0_attn_Wo"]).T)
                bias_row = np.zeros((1, D), np.float32)
            else:
                wo_cm = _f32(inp["L1_attn_Wout_k"])
                bias_row = _f32(inp["L1_attn_Wout_b"])[None, :]
            inC.append(dict(s_in=s_in, wo_cm=wo_cm, bias_row=bias_row,
                            ones_row=C["ones_row"], g=gs[c], x=xs[c]))
        rC = _run(ncC, inC)
        xs = [np.ascontiguousarray(np.asarray(rC.results[c]["xo"])) for c in range(NCORE)]

    y = np.empty((2, SEQ, D), np.float32)
    for c in range(NCORE):
        y[c // 4, (c % 4) * TPC:(c % 4 + 1) * TPC, :] = xs[c]
    return y



_FUSE_SKIP = set()
_FUSE_OPTS = {}


def build_fused(TPC, SEQ):
    NRANK = SEQ // TPC
    _SEMST[0] = None
    nc = bass.Bass("TRN2", target_bir_lowering=False)
    DT = nc.dram_tensor

    x0 = DT("x", [TPC, D], F32, kind="ExternalInput")
    ident = DT("ident", [128, 128], BF16, kind="ExternalInput")
    ones_col = DT("ones_col", [128, 1], BF16, kind="ExternalInput")
    ones_row = DT("ones_row", [1, 128], F32, kind="ExternalInput")
    causal = DT("causal", [128, 1024], BF16, kind="ExternalInput")
    meta = DT("meta", [1, 8], I32, kind="ExternalInput")
    wupT = DT("wupT", [D, 1024], F32, kind="ExternalInput")
    wgT = DT("wgT", [D, 1024], F32, kind="ExternalInput")
    wdT = DT("wdT", [INTER, 256], F32, kind="ExternalInput")
    h128 = DT("h128", [128, 128], BF16, kind="ExternalInput")
    h16 = DT("h16", [16, 16], BF16, kind="ExternalInput")
    h64 = DT("h64", [64, 64], BF16, kind="ExternalInput")
    wg0 = DT("wg0", [D, INTER], I8, kind="ExternalInput")
    wu0 = DT("wu0", [D, INTER], I8, kind="ExternalInput")
    wd0 = DT("wd0", [INTER, D], I8, kind="ExternalInput")
    sg0 = DT("sg0", [128, IC], F32, kind="ExternalInput")
    su0 = DT("su0", [128, IC], F32, kind="ExternalInput")
    sd0r = DT("sd0r", [1, D], F32, kind="ExternalInput")
    wqkv = DT("wqkv", [D, HPC * 384], F32, kind="ExternalInput")
    cmask = DT("cmask", [128, SEQ], BF16, kind="ExternalInput")
    smask = DT("smask", [128, SEQ], BF16, kind="ExternalInput")
    permT0 = DT("permT0", [128, 128], BF16, kind="ExternalInput")
    wocm0 = DT("wocm0", [D, D], F32, kind="ExternalInput")
    b0r = DT("b0r", [1, D], F32, kind="ExternalInput")
    wpre = DT("wpre", [D, WPRE_COLS], F32, kind="ExternalInput")
    bpre = DT("bpre", [128, 17], F32, kind="ExternalInput")
    krcos = DT("krcos", [64, TPC], BF16, kind="ExternalInput")
    krsin = DT("krsin", [64, TPC], BF16, kind="ExternalInput")
    qrcos = DT("qrcos", [1024, TPC], BF16, kind="ExternalInput")
    qrsin = DT("qrsin", [1024, TPC], BF16, kind="ExternalInput")
    permT1 = DT("permT1", [128, 128], BF16, kind="ExternalInput")
    sudp = DT("sudp", [128, IC], F32, kind="ExternalInput")
    suug = DT("suug", [128, DC], F32, kind="ExternalInput")
    sg1 = DT("sg1", [128, IC], F32, kind="ExternalInput")
    su1 = DT("su1", [128, IC], F32, kind="ExternalInput")
    sd1r = DT("sd1r", [1, D], F32, kind="ExternalInput")
    wukv = DT("wukv", [512, HPC * 256], F32, kind="ExternalInput")
    wuq = DT("wuq", [512, HPC * 128], F32, kind="ExternalInput")
    bkv = DT("bkv", [128, HPC * 2], F32, kind="ExternalInput")
    bq = DT("bq", [128, HPC], F32, kind="ExternalInput")
    wocm1 = DT("wocm1", [D, D], F32, kind="ExternalInput")
    b1r = DT("b1r", [1, D], F32, kind="ExternalInput")

    y = DT("y", [TPC, D], F32, kind="ExternalOutput")

    fuse_d = DT("fuse", [INTER, TPC], BF16)
    gbuf = DT("gbuf", [TPC, D], F32)
    x1 = DT("x1", [TPC, D], F32)
    pay0_in = DT("pay0_in", [D, TPC], BF16)
    pay0_g = DT("pay0_g", [NRANK * D, TPC], BF16)
    pay1_in = DT("pay1_in", [PAY1, TPC], BF16)
    pay1_g = DT("pay1_g", [NRANK * PAY1, TPC], BF16)
    qrt_in = DT("qrt_in", [TPC, 1024], BF16)
    qrt_g = DT("qrt_g", [SEQ, 1024], BF16)
    qrh = DT("qrh", [SEQ, 256], BF16)
    qrm_scr = DT("qrm_scr", [256, SEQ], BF16)
    att_in = DT("att_in", [HPC * 128, SEQ], BF16)
    s_scr = DT("s_scr", [D, TPC], BF16)
    att_g = [DT(f"att_g{i}", [NRANK * HPC * 128, SEQ], BF16) for i in range(2)]
    qd = DT("qd", [HPC * 128, SEQ], BF16)
    kd = DT("kd", [HPC * 128, SEQ], BF16)
    vd = DT("vd", [HPC * SEQ, 128], BF16)
    wfu_in = DT("wfu_in", [D, 1024], BF16)
    wfg_in = DT("wfg_in", [D, 1024], BF16)
    wfd_in = DT("wfd_in", [INTER, 256], BF16)
    wfu_g = DT("wfu_g", [8 * D, 1024], BF16)
    wfg_g = DT("wfg_g", [8 * D, 1024], BF16)
    wfd_g = DT("wfd_g", [8 * INTER, 256], BF16)
    wu1f = DT("wu1f", [D, INTER], BF16)
    wg1f = DT("wg1f", [D, INTER], BF16)
    wd1f = DT("wd1f", [INTER, D], BF16)
    u1s = DT("u1s", [D, 1024], BF16)
    u1gs = DT("u1gs", [D, 1024], BF16)
    u2s = DT("u2s", [INTER, 256], BF16)

    GROUPS = [[0, 1, 2, 3], [4, 5, 6, 7]]
    ALLG = [[0, 1, 2, 3, 4, 5, 6, 7]]

    es = contextlib.ExitStack()
    with es:
        sb_ident = es.enter_context(_sb(nc, "g_ident", [128, 128], BF16))
        sb_ones = es.enter_context(_sb(nc, "g_ones", [128, 1], BF16))
        sb_or = es.enter_context(_sb(nc, "g_or", [1, 128], F32))
        sb_caus = es.enter_context(_sb(nc, "g_caus", [128, 1024], BF16))
        sb_meta = es.enter_context(_sb(nc, "g_meta", [1, 8], I32))
        sb_h128 = es.enter_context(_sb(nc, "g_h128", [128, 128], BF16))
        sb_h16 = es.enter_context(_sb(nc, "g_h16", [16, 16], BF16))
        sb_h64 = es.enter_context(_sb(nc, "g_h64", [64, 64], BF16))
        s_g0 = es.enter_context(_sm(nc, "g_l0"))
        cc_w = [es.enter_context(_sm(nc, f"cc_w{i}")) for i in range(3)]
        s_wcp = es.enter_context(_sm(nc, "s_wcp"))
        cc_pay = [es.enter_context(_sm(nc, f"cc_pay{i}")) for i in range(2)]
        cc_att = [es.enter_context(_sm(nc, f"cc_att{i}")) for i in range(2)]
        cc_qrt = es.enter_context(_sm(nc, "cc_qrt"))
        s_qrh = es.enter_context(_sm(nc, "s_qrh"))

        with nc.Block(no_gpsimd_drain=True) as block:
            @block.gpsimd
            def _(g_):
                n = 0
                for dst, srt in [(sb_ident, ident), (sb_ones, ones_col), (sb_or, ones_row),
                                 (sb_caus, causal), (sb_meta, meta), (sb_h128, h128),
                                 (sb_h16, h16), (sb_h64, h64)]:
                    g_.dma_start(out=dst[0:srt.shape[0], 0:srt.shape[1]],
                                 in_=srt.ap()).then_inc(s_g0, 16)
                    n += 16
                g_.wait_ge(s_g0, n)

        s_fls = es.enter_context(nc.semaphore("s_fls"))
        s_flg = es.enter_context(nc.semaphore("s_flg"))
        sb_fence = es.enter_context(_sb(nc, "g_fence", [1, 64], BF16))
        assert s_flg.num <= 169, s_flg.num
        _SEMST[0] = _FenceState(s_flg.num + 1, s_fls, s_flg)

        if "wfold" not in _FUSE_SKIP:
            _PX[0] = "w_"
            _emit_wfold(nc, wupT, wgT, wdT, sb_h128, sb_h16, sb_h64,
                        wfu_in, wfg_in, wfd_in, u1s, u1gs, u2s)
            _PX[0] = ""
        with nc.Block(no_gpsimd_drain=True) as block:
            @block.gpsimd
            def _(g_):
                for i, (src, dstg) in enumerate([(wfu_in, wfu_g), (wfg_in, wfg_g),
                                                 (wfd_in, wfd_g)]):
                    if "wag" in _FUSE_SKIP:
                        g_.sem_inc(cc_w[i], 1)
                        continue
                    g_.collective_compute("AllGather", ALU.bypass, replica_groups=ALLG,
                                          ins=[src.ap()], outs=[dstg.ap()]).then_inc(cc_w[i])

        def wcp_copies(eng):
            for i, (srcg, dstf, rows, cols) in enumerate([
                    (wfu_g, wu1f, D, 1024), (wfg_g, wg1f, D, 1024),
                    (wfd_g, wd1f, INTER, 256)]):
                eng.wait_ge(cc_w[i], 1)
                nq = max(4, (rows * 8 + 16383) // 16384 + 1)
                while rows % nq:
                    nq += 1
                for q in range(nq):
                    r0 = q * (rows // nq)
                    eng.dma_start(
                        out=bass.AP(dstf, r0 * 8 * cols,
                                    [[8 * cols, rows // nq], [cols, 8], [1, cols]]),
                        in_=bass.AP(srcg, r0 * cols,
                                    [[cols, rows // nq], [rows * cols, 8], [1, cols]]),
                    ).then_inc(s_wcp, 16)

        layers = [Lx for Lx in (0, 1) if f"layer{Lx}" not in _FUSE_SKIP]
        for li, L in enumerate(layers):
            if li > 0:
                _sem_fence(nc, ident.ap(), sb_fence)
            px = f"l{L}"
            x_ap = (x0 if L == 0 else x1).ap()
            xo_ap = (x1 if L == 0 else y).ap()
            payg = pay0_g if L == 0 else pay1_g

            with contextlib.ExitStack() as el:
                sb_sg = el.enter_context(_sb(nc, px + "c_sg", [128, IC], F32))
                sb_su = el.enter_context(_sb(nc, px + "c_su", [128, IC], F32))
                sb_sdb = el.enter_context(_sb(nc, px + "c_sdb", [128, D], F32))
                sb_sdr = el.enter_context(_sb(nc, px + "c_sdr", [1, D], F32))
                if L == 1:
                    sb_perm = el.enter_context(_sb(nc, px + "c_perm", [128, 128], BF16))
                    sb_bpre = el.enter_context(_sb(nc, px + "c_bpre", [128, 17], F32))
                    sb_sudp = el.enter_context(_sb(nc, px + "c_sudp", [128, IC], F32))
                    sb_suug = el.enter_context(_sb(nc, px + "c_suug", [128, DC], F32))
                s_lc = el.enter_context(_sm(nc, px + "c_ld"))
                s_bc = el.enter_context(_sm(nc, px + "c_bc"))

                with nc.Block(no_gpsimd_drain=True) as block:
                    @block.gpsimd
                    def _(g_, L=L):
                        n = 0
                        pairs = ([(sb_sg, sg0), (sb_su, su0), (sb_sdr, sd0r)] if L == 0 else
                                 [(sb_sg, sg1), (sb_su, su1), (sb_sdr, sd1r),
                                  (sb_perm, permT1), (sb_bpre, bpre),
                                  (sb_sudp, sudp), (sb_suug, suug)])
                        for dst, srt in pairs:
                            g_.dma_start(out=dst[0:srt.shape[0], 0:srt.shape[1]],
                                         in_=srt.ap()).then_inc(s_lc, 16)
                            n += 16
                        if L == 1 and "wcp" in _FUSE_SKIP:
                            for _i in range(16):
                                g_.sem_inc(s_wcp, 16)
                        elif L == 1 and 0 not in layers:
                            wcp_copies(g_)
                        g_.wait_ge(s_lc, n)

                with contextlib.ExitStack() as eb:
                    ps_b = eb.enter_context(_pt(nc, px + "c_psb", [128, 512], F32))
                    with nc.Block(no_gpsimd_drain=True) as block:
                        @block.tensor
                        def _(t):
                            for j in range(D // 512):
                                if j >= 1:
                                    t.wait_ge(s_bc, 2 * j)
                                t.matmul(ps_b[:, :], sb_or[:, :],
                                         sb_sdr[:, j * 512:(j + 1) * 512],
                                         start=True, stop=True).then_inc(s_bc, 1)

                        @block.scalar
                        def _(s):
                            for j in range(D // 512):
                                s.wait_ge(s_bc, 2 * j + 1)
                                s.activation(sb_sdb[:, j * 512:(j + 1) * 512], ps_b[:, :],
                                             AF.Identity).then_inc(s_bc, 1)

                with contextlib.ExitStack() as ex:
                    xn = ex.enter_context(_sb(nc, px + "xn", [128, DC * TPC], BF16))
                    if f"ln{L}" not in _FUSE_SKIP:
                        _PX[0] = px + "ln_"
                        _emit_ln(nc, x_ap, xn, pay0_in.ap() if L == 0 else None, sb_ident, TPC)
                    if L == 1 and "pp1" not in _FUSE_SKIP:
                        _PX[0] = px + "pp_"
                        _emit_preproj(nc, xn, pay1_in.ap(), wpre.ap(), sb_bpre, sb_perm,
                                      krcos.ap(), krsin.ap(), qrcos.ap(), qrsin.ap(), TPC)
                        _PX[0] = px + "qt_"
                        _emit_tr(nc, pay1_in, qrt_in, 1024, TPC, sb_ident, px + "qt_",
                                 src_off=QR_BASE * TPC)
                    _PX[0] = ""
                    with nc.Block(no_gpsimd_drain=True) as block:
                        @block.gpsimd
                        def _(g_, L=L):
                            pin = pay0_in if L == 0 else pay1_in
                            g_.collective_compute("AllGather", ALU.bypass,
                                                  replica_groups=GROUPS,
                                                  ins=[pin.ap()], outs=[payg.ap()],
                                                  ).then_inc(cc_pay[L])
                            if L == 1:
                                if "payag" in _FUSE_SKIP:
                                    g_.sem_inc(cc_qrt, 1)
                                else:
                                    g_.collective_compute("AllGather", ALU.bypass,
                                                          replica_groups=GROUPS,
                                                          ins=[qrt_in.ap()], outs=[qrt_g.ap()],
                                                          ).then_inc(cc_qrt)
                    if L == 1 and "sx1" not in _FUSE_SKIP:
                        _PX[0] = px + "sx_"
                        _emit_scale_xn(nc, xn, sb_suug, TPC)
                    if L == 1:
                        with nc.Block(no_gpsimd_drain=True) as block:
                            @block.gpsimd
                            def _(g_):
                                g_.wait_ge(s_wcp, 16 * 16)
                    if f"ffn{L}" not in _FUSE_SKIP:
                        _PX[0] = px + "fg_"
                        _emit_ffn_upgate(nc, L, xn, (wg0 if L == 0 else wg1f).ap(),
                                         (wu0 if L == 0 else wu1f).ap(), sb_sg, sb_su,
                                         sb_sudp if L == 1 else None, fuse_d.ap(), TPC)
                if f"ffn{L}" not in _FUSE_SKIP:
                    _PX[0] = px + "fd_"
                    _emit_ffn_down(nc, L, (wd0 if L == 0 else wd1f).ap(), sb_sdb,
                                   fuse_d.ap(), gbuf.ap(), TPC)
                _PX[0] = ""

                with nc.Block(no_gpsimd_drain=True) as block:
                    if L == 0 and 1 in layers and "wcp" not in _FUSE_SKIP:
                        @block.scalar
                        def _(sc):
                            wcp_copies(sc)

                    @block.gpsimd
                    def _(g_, L=L):
                        g_.wait_ge(cc_pay[L], 1)
                        if L == 1 and "qrh" not in _FUSE_SKIP:
                            g_.wait_ge(cc_qrt, 1)
                            with g_.register("rq") as rq:
                                g_.reg_load(rq, sb_meta[0:1, 1:2])
                                val = g_.snap(rq, min_val=0, max_val=1024 - 256)
                                g_.dma_start(out=qrh.ap(),
                                             in_=qrt_g.ap()[:, bass.ds(val, 256)]).then_inc(s_qrh, 16)
                            g_.wait_ge(s_qrh, 16)
                if L == 1 and "qu" not in _FUSE_SKIP:
                    _emit_tr(nc, qrh, qrm_scr, SEQ, 256, sb_ident, px + "qu_")

                with contextlib.ExitStack() as ea:
                    if L == 0:
                        sb_w = ea.enter_context(_sb(nc, px + "a_w", [128, DC * HPC * 384], BF16))
                        sb_p0 = ea.enter_context(_sb(nc, px + "a_p0", [128, 128], BF16))
                        s_al = ea.enter_context(_sm(nc, px + "a_ld"))
                        with nc.Block(no_gpsimd_drain=True) as block:
                            @block.gpsimd
                            def _(g_):
                                g_.dma_start(out=sb_w[:, :].rearrange("p (c n) -> p c n", c=DC),
                                             in_=wqkv.ap().rearrange("(c p) n -> p c n", p=128),
                                             ).then_inc(s_al, 16)
                                g_.dma_start(out=sb_p0[:, :], in_=permT0.ap()).then_inc(s_al, 16)
                                g_.wait_ge(s_al, 32)
                        _PX[0] = px + "pj_"
                        _emit_proj_l0(nc, pay0_g, sb_w, sb_p0, sb_ident,
                                      cmask.ap(), smask.ap(), qd, kd, vd, TPC, SEQ)
                    else:
                        sb_w = ea.enter_context(_sb(nc, px + "a_w", [128, 4 * HPC * 256], BF16))
                        sb_wq = ea.enter_context(_sb(nc, px + "a_wq", [128, 4 * HPC * 128], BF16))
                        sb_bkv = ea.enter_context(_sb(nc, px + "a_bkv", [128, HPC * 2], F32))
                        sb_bq = ea.enter_context(_sb(nc, px + "a_bq", [128, HPC], F32))
                        s_al = ea.enter_context(_sm(nc, px + "a_ld"))
                        with nc.Block(no_gpsimd_drain=True) as block:
                            @block.gpsimd
                            def _(g_):
                                g_.dma_start(out=sb_w[:, :].rearrange("p (c n) -> p c n", c=4),
                                             in_=wukv.ap().rearrange("(c p) n -> p c n", p=128),
                                             ).then_inc(s_al, 16)
                                g_.dma_start(out=sb_wq[:, :].rearrange("p (c n) -> p c n", c=4),
                                             in_=wuq.ap().rearrange("(c p) n -> p c n", p=128),
                                             ).then_inc(s_al, 16)
                                g_.dma_start(out=sb_bkv[:, :], in_=bkv.ap()).then_inc(s_al, 16)
                                g_.dma_start(out=sb_bq[:, :], in_=bq.ap()).then_inc(s_al, 16)
                                g_.wait_ge(s_al, 64)
                        if "pj1" not in _FUSE_SKIP:
                            _PX[0] = px + "pj_"
                            _emit_proj_l1(nc, pay1_g, sb_w, sb_wq, sb_bkv, sb_bq, sb_ident,
                                          qd, kd, vd, qrm_scr, TPC, SEQ)
                    if f"sdpa{L}" not in _FUSE_SKIP:
                        _PX[0] = px + "sd_"
                        _emit_sdpa(nc, L, qd, kd, vd, qrm_scr if L == 1 else None,
                                   pay1_g if L == 1 else pay0_g, meta,
                                   sb_ones, sb_or, sb_caus, att_in.ap(), TPC, SEQ)
                    _PX[0] = ""

                with nc.Block(no_gpsimd_drain=True) as block:
                    @block.gpsimd
                    def _(g_, L=L):
                        if "attag" in _FUSE_SKIP:
                            g_.sem_inc(cc_att[L], 1)
                        else:
                            g_.collective_compute("AllGather", ALU.bypass, replica_groups=GROUPS,
                                                  ins=[att_in.ap()], outs=[att_g[L].ap()],
                                                  ).then_inc(cc_att[L])

                def sin_dma(g_, sb_s, sem, L=L):
                    g_.wait_ge(cc_att[L], 1)
                    with g_.register(f"rt{L}") as rt:
                        g_.reg_load(rt, sb_meta[0:1, 0:1])
                        val = g_.snap(rt, min_val=0, max_val=SEQ - TPC)
                        g_.dma_start(out=s_scr.ap(),
                                     in_=att_g[L].ap()[:, bass.ds(val, TPC)]).then_inc(sem, 16)
                    g_.wait_ge(sem, 16)
                    g_.dma_start(
                        out=sb_s[:, :].rearrange("p (c t) -> p c t", c=DC),
                        in_=s_scr.ap().rearrange("(c p) t -> p c t", p=128),
                    ).then_inc(sem, 16)
                    return 32

                _PX[0] = px + "po_"
                if f"post{L}" not in _FUSE_SKIP:
                    _emit_post(nc, sin_dma, (wocm0 if L == 0 else wocm1).ap(),
                               (b0r if L == 0 else b1r).ap(), sb_or,
                               gbuf.ap(), x_ap, xo_ap, TPC)
                _PX[0] = ""

    _SEMST[0] = None
    return nc


def _dyn3(handle, seq, dc, tpc, val):
    ap = bass.AP(handle, 0, [[seq, 128], [128 * seq, dc], [1, seq]])
    return ap[:, :, bass.ds(val, tpc)]


FUSED = True


def kernel_fused(**inp):
    C = host_constants(SEQ)
    x = _f32(inp["L0_attn_x"])
    wsug = _f32(inp["L1_ffn_Wscale_ug"])
    nc = build_fused(TPC, SEQ)
    in_maps = []
    bpre = np.zeros((128, 17), np.float32)
    wdb = _f32(inp["L1_attn_W_down_b"])
    for ocx in range(8):
        bpre[:, ocx] = wdb[ocx * 128:(ocx + 1) * 128]
    bpre[0:64, 8] = _f32(inp["L1_attn_Wkr_b"])
    wqrb = _f32(inp["L1_attn_Wqr_b"])
    for j in range(8):
        bpre[:, 9 + j] = wqrb[j * 128:(j + 1) * 128]
    wpre_h = np.ascontiguousarray(np.hstack([
        _f32(inp["L1_attn_W_down_k"]), _f32(inp["L1_attn_Wkr_k"]),
        _f32(inp["L1_attn_Wqr_k"])]))
    wukv_f, wuq_f = _f32(inp["L1_attn_W_uKV_k"]), _f32(inp["L1_attn_W_uQ_k"])
    bkv_f, bq_f = _f32(inp["L1_attn_W_uKV_b"]), _f32(inp["L1_attn_W_uQ_b"])

    for c in range(NCORE):
        qc = c % 4
        h0 = qc * HPC
        sl = slice(qc * TPC, (qc + 1) * TPC)
        wukv = np.empty((512, HPC * 256), np.float32)
        bkv = np.empty((128, HPC * 2), np.float32)
        wuq = np.empty((512, HPC * 128), np.float32)
        bq = np.empty((128, HPC), np.float32)
        for j in range(HPC):
            h = h0 + j
            wukv[:, j * 256:j * 256 + 128] = wukv_f[:, h * 128:(h + 1) * 128]
            wukv[:, j * 256 + 128:j * 256 + 256] = wukv_f[:, 2048 + h * 128:2048 + (h + 1) * 128]
            bkv[:, 2 * j] = bkv_f[h * 128:(h + 1) * 128]
            bkv[:, 2 * j + 1] = bkv_f[2048 + h * 128:2048 + (h + 1) * 128]
            wuq[:, j * 128:(j + 1) * 128] = wuq_f[:, h * 128:(h + 1) * 128]
            bq[:, j] = bq_f[h * 128:(h + 1) * 128]
        m = np.zeros((1, 8), np.int32)
        m[0, 0] = qc * TPC
        m[0, 1] = h0 * 64
        d = dict(
            x=np.ascontiguousarray(x[c // 4, sl, :]),
            ident=C["ident"], ones_col=C["ones_col"], ones_row=C["ones_row"],
            causal=C["causal"], meta=m,
            wupT=np.ascontiguousarray(_f32(inp["L1_ffn_W_up"])[c * 1024:(c + 1) * 1024, :].T),
            wgT=np.ascontiguousarray(_f32(inp["L1_ffn_W_gate"])[c * 1024:(c + 1) * 1024, :].T),
            wdT=np.ascontiguousarray(_f32(inp["L1_ffn_W_down"])[c * 256:(c + 1) * 256, :].T),
            h128=C["h128"], h16=C["h16"], h64=C["h64"],
            wg0=inp["L0_ffn_w_gate"], wu0=inp["L0_ffn_w_up"], wd0=inp["L0_ffn_w_down"],
            sg0=_rs128(inp["L0_ffn_s_gate"]), su0=_rs128(inp["L0_ffn_s_up"]),
            sd0r=_f32(inp["L0_ffn_s_down"])[None, :],
            wqkv=_build_wqkv_l0(inp, h0),
            cmask=C["l0_cmask"], smask=C["l0_smask"], permT0=C["l0_permT"],
            wocm0=np.ascontiguousarray(_f32(inp["L0_attn_Wo"]).T),
            b0r=np.zeros((1, D), np.float32),
            wpre=wpre_h, bpre=bpre,
            krcos=np.ascontiguousarray(C["l1_kr_cos"][:, sl]),
            krsin=np.ascontiguousarray(C["l1_kr_sin"][:, sl]),
            qrcos=np.ascontiguousarray(C["l1_qr_cos"][:, sl]),
            qrsin=np.ascontiguousarray(C["l1_qr_sin"][:, sl]),
            permT1=C["l1_permT"],
            sudp=_rs128(inp["L1_ffn_SU_dp"]), suug=_rs128(inp["L1_ffn_SU_ug"]),
            sg1=_rs128(wsug[INTER:]), su1=_rs128(wsug[:INTER]),
            sd1r=_f32(inp["L1_ffn_Wscale_dp"])[None, :],
            wukv=np.ascontiguousarray(wukv), wuq=np.ascontiguousarray(wuq),
            bkv_rs=np.ascontiguousarray(bkv), bq_rs=np.ascontiguousarray(bq),
            wocm1=_f32(inp["L1_attn_Wout_k"]),
            b1r=_f32(inp["L1_attn_Wout_b"])[None, :],
        )
        d["bkv"] = d.pop("bkv_rs")
        d["bq"] = d.pop("bq_rs")
        in_maps.append(d)
    res = _run(nc, in_maps)
    yout = np.empty((2, SEQ, D), np.float32)
    for c in range(NCORE):
        yout[c // 4, (c % 4) * TPC:(c % 4 + 1) * TPC, :] = np.asarray(res.results[c]["y"])
    return yout


_kernel_unfused = kernel


def kernel(**inp):
    if FUSED:
        return kernel_fused(**inp)
    return _kernel_unfused(**inp)


# version 13
# speedup vs baseline: 1.1196x; 1.0036x over previous
import contextlib
import math

import numpy as np
import ml_dtypes

import concourse.bass as bass
import concourse.mybir as mybir
from concourse.bass_utils import run_bass_kernel_spmd

F32 = mybir.dt.float32
BF16 = mybir.dt.bfloat16
I8 = mybir.dt.int8
I32 = mybir.dt.int32
AF = mybir.ActivationFunctionType
ALU = mybir.AluOpType
AX = mybir.AxisListType
BF = ml_dtypes.bfloat16

D = 2048
INTER = 8192
DC = D // 128
IC = INTER // 128
HPC = 4
EPS = 1e-5
ISQ = 1.0 / math.sqrt(128.0)
PAY1 = 2176
QR_BASE = 1152
WPRE_COLS = 2112


def _had(n):
    h = np.array([[1.0]])
    while h.shape[0] < n:
        h = np.block([[h, h], [h, -h]])
    return h


def _bf(x):
    return np.ascontiguousarray(np.asarray(x)).astype(BF)


def _f32(x):
    return np.ascontiguousarray(np.asarray(x), dtype=np.float32)


def host_constants(SEQ):
    c = {}
    c["ident"] = _bf(np.eye(128))
    c["ones_col"] = _bf(np.ones((128, 1)))
    c["ones_row"] = _f32(np.ones((1, 128)))
    i = np.arange(128)[:, None]
    u = np.arange(1024)[None, :]
    c["causal"] = _bf((i <= (u - 512)).astype(np.float64))

    inv = 1.0 / (10000.0 ** (np.arange(0, 64, 2) / 64.0))
    fr = np.outer(np.arange(SEQ), inv)
    cosf, sinf = np.cos(fr).T, np.sin(fr).T
    cm = np.ones((128, SEQ))
    sm = np.zeros((128, SEQ))
    cm[64:96], cm[96:128] = cosf, cosf
    sm[64:96], sm[96:128] = -sinf, sinf
    c["l0_cmask"], c["l0_smask"] = _bf(cm), _bf(sm)
    p0 = np.zeros((128, 128))
    for j in range(32):
        p0[64 + j, 96 + j] = 1.0
        p0[96 + j, 64 + j] = 1.0
    c["l0_permT"] = _bf(p0.T)

    def r1(d):
        freq = np.arange(SEQ)[:, None] + 1.0
        pos = np.repeat(np.arange(d // 2)[:, None], 2, axis=-1).reshape(1, -1)
        theta = np.exp(-2.0 * pos / d * np.log(10000.0))
        ang = freq * theta
        return np.cos(ang).T, np.sin(ang).T

    c["l1_kr_cos"], c["l1_kr_sin"] = map(_bf, r1(64))
    c["l1_qr_cos"], c["l1_qr_sin"] = map(_bf, r1(1024))
    p1 = np.zeros((128, 128))
    for j in range(64):
        p1[2 * j, 2 * j + 1] = -1.0
        p1[2 * j + 1, 2 * j] = 1.0
    c["l1_permT"] = _bf(p1.T)

    c["h128"] = _bf(_had(128))
    c["h16"] = _bf(_had(16))
    c["h64"] = _bf(_had(64))
    return c


_PX = [""]


def _sb(nc, name, shape, dt):
    return nc.sbuf_tensor(_PX[0] + name, shape, dt)


def _pt(nc, name, shape, dt):
    return nc.psum_tensor(_PX[0] + name, shape, dt)


_SEMST = [None]


class _FenceState:
    def __init__(self, fstart, fl_sync, fl_gp):
        import collections
        self.fstart = fstart
        self.clean = collections.deque(range(fstart, 256))
        self.dirty = set()
        self.fl_sync = fl_sync
        self.fl_gp = fl_gp
        self.nf = 0


@contextlib.contextmanager
def _sm(nc, name):
    st = _SEMST[0]
    if st is None:
        with nc.semaphore(_PX[0] + name) as h:
            yield h
        return
    assert st.clean, f"sem pool exhausted at {_PX[0] + name}: add a _sem_fence"
    num = st.clean.popleft()
    h = nc.alloc_semaphore(name=_PX[0] + name, num=num)
    try:
        yield h
    finally:
        nc.release_semaphore(h)
        st.dirty.add(num)


def _sem_fence(nc, src_ap, sb_f):
    st = _SEMST[0]
    st.nf += 1
    k = st.nf
    with nc.Block(no_gpsimd_drain=True) as blk:
        @blk.sync
        def _(sy):
            sy.dma_start(out=sb_f[0:1, 0:16], in_=src_ap[0:1, 0:16]).then_inc(st.fl_sync, 16)
            sy.wait_ge(st.fl_sync, 16 * k)

        @blk.gpsimd
        def _(g_):
            g_.dma_start(out=sb_f[0:1, 16:32], in_=src_ap[0:1, 16:32]).then_inc(st.fl_gp, 16)
            g_.wait_ge(st.fl_gp, 16 * k)
    with nc.Block(no_gpsimd_drain=True) as blk:
        @blk.gpsimd
        def _(g_):
            g_.dma_reset(range(st.fstart, 256))
            g_.sem_clear(range(st.fstart, 256))
    st.clean.extend(sorted(st.dirty))
    st.dirty.clear()



def build_A(L, TPC):
    TG = TPC // 128
    NT = TPC // 512
    DPAY = D if L == 0 else PAY1
    nc = bass.Bass("TRN2", target_bir_lowering=False)

    x = nc.dram_tensor("x", [TPC, D], F32, kind="ExternalInput").ap()
    ident = nc.dram_tensor("ident", [128, 128], BF16, kind="ExternalInput").ap()
    wdt = I8 if L == 0 else BF16
    wg = nc.dram_tensor("wg", [D, INTER], wdt, kind="ExternalInput").ap()
    wu = nc.dram_tensor("wu", [D, INTER], wdt, kind="ExternalInput").ap()
    wd = nc.dram_tensor("wd", [INTER, D], wdt, kind="ExternalInput").ap()
    sg_rs = nc.dram_tensor("sg_rs", [128, IC], F32, kind="ExternalInput").ap()
    su_rs = nc.dram_tensor("su_rs", [128, IC], F32, kind="ExternalInput").ap()
    sd_row = nc.dram_tensor("sd_row", [1, D], F32, kind="ExternalInput").ap()
    ones_row = nc.dram_tensor("ones_row", [1, 128], F32, kind="ExternalInput").ap()
    if L == 1:
        wpre = nc.dram_tensor("wpre", [D, WPRE_COLS], F32, kind="ExternalInput").ap()
        bpre_rs = nc.dram_tensor("bpre_rs", [128, 17], F32, kind="ExternalInput").ap()
        krcos = nc.dram_tensor("krcos", [64, TPC], BF16, kind="ExternalInput").ap()
        krsin = nc.dram_tensor("krsin", [64, TPC], BF16, kind="ExternalInput").ap()
        qrcos = nc.dram_tensor("qrcos", [1024, TPC], BF16, kind="ExternalInput").ap()
        qrsin = nc.dram_tensor("qrsin", [1024, TPC], BF16, kind="ExternalInput").ap()
        permT = nc.dram_tensor("permT", [128, 128], BF16, kind="ExternalInput").ap()
        sudp_rs = nc.dram_tensor("sudp_rs", [128, IC], F32, kind="ExternalInput").ap()
        suug_rs = nc.dram_tensor("suug_rs", [128, DC], F32, kind="ExternalInput").ap()

    pay = nc.dram_tensor("pay", [DPAY, TPC], BF16, kind="ExternalOutput").ap()
    g_out = nc.dram_tensor("g", [TPC, D], F32, kind="ExternalOutput").ap()
    fuse_d = nc.dram_tensor("fuse", [INTER, TPC], BF16).ap()

    es = contextlib.ExitStack()
    with es:
        sb_ident = es.enter_context(_sb(nc, "c_ident", [128, 128], BF16))
        sb_sg = es.enter_context(_sb(nc, "c_sg", [128, IC], F32))
        sb_su = es.enter_context(_sb(nc, "c_su", [128, IC], F32))
        sb_sdb = es.enter_context(_sb(nc, "c_sdb", [128, D], F32))
        sb_or = es.enter_context(_sb(nc, "c_or", [1, 128], F32))
        sb_sdr = es.enter_context(_sb(nc, "c_sdr", [1, D], F32))
        if L == 1:
            sb_perm = es.enter_context(_sb(nc, "c_perm", [128, 128], BF16))
            sb_bpre = es.enter_context(_sb(nc, "c_bpre", [128, 17], F32))
            sb_sudp = es.enter_context(_sb(nc, "c_sudp", [128, IC], F32))
            sb_suug = es.enter_context(_sb(nc, "c_suug", [128, DC], F32))

        ld0 = es.enter_context(_sm(nc, "ld0"))
        c0 = es.enter_context(_sm(nc, "c0"))

        with nc.Block(no_gpsimd_drain=True) as block:
            @block.gpsimd
            def _(g_):
                n = 0
                pairs = [(sb_ident, ident), (sb_sg, sg_rs), (sb_su, su_rs),
                         (sb_or, ones_row), (sb_sdr, sd_row)]
                if L == 1:
                    pairs += [(sb_perm, permT), (sb_bpre, bpre_rs),
                              (sb_sudp, sudp_rs), (sb_suug, suug_rs)]
                for dst, src in pairs:
                    g_.dma_start(out=dst[:, :], in_=src).then_inc(ld0, 16)
                    n += 16
                g_.wait_ge(ld0, n)

        with contextlib.ExitStack() as bces:
            ps_bc0 = bces.enter_context(_pt(nc, "ps_bc0", [128, 512], F32))
            with nc.Block(no_gpsimd_drain=True) as block:
                @block.tensor
                def _(t):
                    for j in range(D // 512):
                        if j >= 1:
                            t.wait_ge(c0, 2 * j)
                        t.matmul(ps_bc0[:, :], sb_or[:, :], sb_sdr[:, j * 512:(j + 1) * 512],
                                 start=True, stop=True).then_inc(c0, 1)

                @block.scalar
                def _(s):
                    for j in range(D // 512):
                        s.wait_ge(c0, 2 * j + 1)
                        s.activation(sb_sdb[:, j * 512:(j + 1) * 512], ps_bc0[:, :],
                                     AF.Identity).then_inc(c0, 1)

        with contextlib.ExitStack() as exn:
            xn = exn.enter_context(_sb(nc, "xn", [128, DC * TPC], BF16))
            _emit_ln(nc, x, xn, pay if L == 0 else None, sb_ident, TPC)

            if L == 1:
                _emit_preproj(nc, xn, pay, wpre, sb_bpre, sb_perm,
                              krcos, krsin, qrcos, qrsin, TPC)
                _emit_scale_xn(nc, xn, sb_suug, TPC)

            _emit_ffn_upgate(nc, L, xn, wg, wu, sb_sg, sb_su,
                             sb_sudp if L == 1 else None, fuse_d, TPC)
        _emit_ffn_down(nc, L, wd, sb_sdb, fuse_d, g_out, TPC)

    return nc


def _emit_ln(nc, x, xn, pay_ap, sb_ident, TPC):
    TG = TPC // 128
    NDV = 6
    with contextlib.ExitStack() as es:
        sb_x = [es.enter_context(_sb(nc, f"ln_x{i}", [128, D], F32)) for i in range(2)]
        sb_sum = es.enter_context(_sb(nc, "ln_sum", [128, 1], F32))
        sb_negm = es.enter_context(_sb(nc, "ln_negm", [128, 1], F32))
        sb_xc = es.enter_context(_sb(nc, "ln_xc", [128, D], F32))
        sb_sq = es.enter_context(_sb(nc, "ln_sq", [128, D], F32))
        sb_ss = es.enter_context(_sb(nc, "ln_ss", [128, 1], F32))
        sb_veps = es.enter_context(_sb(nc, "ln_veps", [128, 1], F32))
        sb_iv = es.enter_context(_sb(nc, "ln_iv", [128, 1], F32))
        sb_rstd = es.enter_context(_sb(nc, "ln_rstd", [128, 1], F32))
        sb_xtm = [es.enter_context(_sb(nc, f"ln_xtm{i}", [128, D], BF16)) for i in range(2)]
        ps_tr = [es.enter_context(_pt(nc, f"ln_tr{i}", [128, 1024], BF16)) for i in range(2)]
        s_ld = [es.enter_context(_sm(nc, f"ln_ld{i}")) for i in range(2)]
        s_dv = es.enter_context(_sm(nc, "ln_dv"))
        s_ac = es.enter_context(_sm(nc, "ln_ac"))
        s_xn = es.enter_context(_sm(nc, "ln_xn"))
        s_pe = es.enter_context(_sm(nc, "ln_pe"))
        s_ev = es.enter_context(_sm(nc, "ln_ev"))
        s_ps = es.enter_context(_sm(nc, "ln_ps"))

        with nc.Block(no_gpsimd_drain=True) as block:
            @block.sync
            def _(sy):
                for tg in range(TG):
                    if tg >= 2:
                        sy.wait_ge(s_xn, tg - 1)
                    sy.dma_start(out=sb_x[tg % 2][:, :],
                                 in_=x[tg * 128:(tg + 1) * 128, :]).then_inc(s_ld[tg % 2], 16)
                if pay_ap is not None:
                    sy.wait_ge(s_ev, 2 * TG)
                    for ch in range(DC):
                        sy.dma_start(out=pay_ap[ch * 128:(ch + 1) * 128, :],
                                     in_=xn[:, ch * TPC:(ch + 1) * TPC]).then_inc(s_ps, 16)
                    sy.wait_ge(s_ps, DC * 16)

            @block.vector
            def _(v):
                xn3 = xn[:, :].rearrange("p (c t) -> p c t", c=DC)
                for tg in range(TG + 1):
                    if tg < TG:
                        xb = sb_x[tg % 2]
                        v.wait_ge(s_ld[tg % 2], (tg // 2 + 1) * 16)
                        v.tensor_reduce(sb_sum[:, :], xb[:, :], axis=AX.X, op=ALU.add).then_inc(s_dv, 1)
                        v.wait_ge(s_dv, tg * NDV + 1)
                        v.tensor_scalar(sb_negm[:, :], sb_sum[:, :], -1.0 / D, None, ALU.mult).then_inc(s_dv, 1)
                        v.wait_ge(s_dv, tg * NDV + 2)
                        v.tensor_scalar(sb_xc[:, :], xb[:, :], sb_negm[:, :], None, ALU.add).then_inc(s_dv, 1)
                        v.wait_ge(s_dv, tg * NDV + 3)
                        v.scalar_tensor_tensor(sb_sq[:, :], sb_xc[:, :], 1.0, sb_xc[:, :],
                                               ALU.mult, ALU.mult,
                                               accum_out=sb_ss[:, :]).then_inc(s_dv, 1)
                        v.wait_ge(s_dv, tg * NDV + 4)
                        v.tensor_scalar(sb_veps[:, :], sb_ss[:, :], 1.0 / D, EPS,
                                        ALU.mult, ALU.add).then_inc(s_dv, 1)
                        v.wait_ge(s_dv, tg * NDV + 5)
                        v.reciprocal(sb_iv[:, :], sb_veps[:, :]).then_inc(s_dv, 1)
                        v.wait_ge(s_ac, tg + 1)
                        if tg >= 2:
                            v.wait_ge(s_pe, tg - 1)
                        v.tensor_scalar(sb_xtm[tg % 2][:, :], sb_xc[:, :], sb_rstd[:, :], None,
                                        ALU.mult).then_inc(s_xn, 1)
                    if tg >= 1:
                        tp = tg - 1
                        v.wait_ge(s_pe, tp + 1)
                        for half in range(2):
                            v.tensor_copy(
                                xn3[:, half * 8:half * 8 + 8, tp * 128:(tp + 1) * 128],
                                ps_tr[half][:, :].rearrange("p (c t) -> p c t", c=8),
                            ).then_inc(s_ev, 1)

            @block.scalar
            def _(s):
                for tg in range(TG):
                    s.wait_ge(s_dv, tg * NDV + 6)
                    s.activation(sb_rstd[:, :], sb_iv[:, :], AF.Sqrt).then_inc(s_ac, 1)

            @block.tensor
            def _(t):
                for tg in range(TG):
                    t.wait_ge(s_xn, tg + 1)
                    if tg >= 1:
                        t.wait_ge(s_ev, 2 * tg)
                    ins = None
                    for half in range(2):
                        for j in range(8):
                            ch = half * 8 + j
                            ins = t.transpose(ps_tr[half][:, j * 128:(j + 1) * 128],
                                              sb_xtm[tg % 2][:, ch * 128:(ch + 1) * 128],
                                              sb_ident[:, :])
                    ins.then_inc(s_pe, 1)



def _emit_scale_xn(nc, xn, sb_suug, TPC):
    with contextlib.ExitStack() as es:
        s_dv = es.enter_context(_sm(nc, "sx_dv"))
        sb_t = es.enter_context(_sb(nc, "sx_t", [128, TPC], BF16))
        with nc.Block(no_gpsimd_drain=True) as block:
            @block.vector
            def _(v):
                n = 0
                for ch in range(DC):
                    v.tensor_scalar(sb_t[:, :], xn[:, ch * TPC:(ch + 1) * TPC],
                                    sb_suug[:, ch:ch + 1], None, ALU.mult).then_inc(s_dv, 1)
                    n += 1
                    v.wait_ge(s_dv, n)
                    v.tensor_copy(xn[:, ch * TPC:(ch + 1) * TPC], sb_t[:, :]).then_inc(s_dv, 1)
                    n += 1
                    v.wait_ge(s_dv, n)


def _emit_preproj(nc, xn, pay, wpre, sb_bpre, sb_perm, krcos, krsin, qrcos, qrsin, TPC):
    NT = TPC // 512
    chunks = ([(oc * 128, 128, oc * 128, oc, "plain") for oc in range(8)]
              + [(1024, 64, 1024, 8, "kr")]
              + [(QR_BASE + j * 128, 128, 1088 + j * 128, 9 + j, "qr") for j in range(8)])
    NCH = len(chunks)
    wpre_r = wpre.rearrange("(c p) n -> p c n", p=128)

    with contextlib.ExitStack() as es:
        sb_w = [es.enter_context(_sb(nc, f"pp_w{i}", [128, DC * 128], BF16)) for i in range(2)]
        sb_o = es.enter_context(_sb(nc, "pp_o", [128, TPC], BF16))
        sb_cos = es.enter_context(_sb(nc, "pp_cos", [128, TPC], BF16))
        sb_sin = es.enter_context(_sb(nc, "pp_sin", [128, TPC], BF16))
        sb_t1 = es.enter_context(_sb(nc, "pp_t1", [128, TPC], BF16))
        sb_pay = es.enter_context(_sb(nc, "pp_pay", [128, TPC], BF16))
        ps_pr = es.enter_context(_pt(nc, "pp_pr", [128, 4 * 512], F32))
        ps_rt = es.enter_context(_pt(nc, "pp_rt", [128, 4 * 512], F32))
        s_wl = [es.enter_context(_sm(nc, f"pp_wl{i}")) for i in range(2)]
        s_tb = es.enter_context(_sm(nc, "pp_tb"))
        s_pe = es.enter_context(_sm(nc, "pp_pe"))
        s_rt = es.enter_context(_sm(nc, "pp_rt"))
        s_ac = es.enter_context(_sm(nc, "pp_ac"))
        s_dv = es.enter_context(_sm(nc, "pp_dv"))
        s_st = es.enter_context(_sm(nc, "pp_st"))

        with nc.Block(no_gpsimd_drain=True) as block:
            @block.gpsimd
            def _(g_):
                row0, rows, w0, bc, kind = chunks[0]
                g_.dma_start(out=sb_w[0][:, 0:DC * rows],
                             in_=wpre_r[:, :, w0:w0 + rows]).then_inc(s_wl[0], 16)

        ndv = 0
        for i, (row0, rows, w0, bc, kind) in enumerate(chunks):
            rope = kind != "plain"
            ndv_start = ndv

            with nc.Block(no_gpsimd_drain=True) as block:
                @block.gpsimd
                def _(g_, i=i):
                    if i + 1 < NCH:
                        _r2, rows2, w02, _, _ = chunks[i + 1]
                        g_.dma_start(out=sb_w[(i + 1) % 2][:, 0:DC * rows2],
                                     in_=wpre_r[:, :, w02:w02 + rows2]).then_inc(s_wl[(i + 1) % 2], 16)

                @block.sync
                def _(sy, i=i, rope=rope, rows=rows, bc=bc, kind=kind, ndv_start=ndv_start, row0=row0):
                    if rope:
                        csrc = krcos if kind == "kr" else qrcos
                        ssrc = krsin if kind == "kr" else qrsin
                        r0 = 0 if kind == "kr" else (bc - 9) * 128
                        sy.dma_start(out=sb_cos[0:rows, :], in_=csrc[r0:r0 + rows, :]).then_inc(s_tb, 16)
                        sy.dma_start(out=sb_sin[0:rows, :], in_=ssrc[r0:r0 + rows, :]).then_inc(s_tb, 16)
                        sy.wait_ge(s_dv, ndv_start + 2 + NT)
                        src = sb_pay
                    else:
                        sy.wait_ge(s_ac, i + 1)
                        src = sb_o
                    sy.dma_start(out=pay[row0:row0 + rows, :],
                                 in_=src[0:rows, :]).then_inc(s_st, 16)
                    sy.wait_ge(s_st, 16 * (i + 1))

                @block.tensor
                def _(t, i=i, rows=rows, rope=rope, kind=kind):
                    t.wait_ge(s_wl[i % 2], (i // 2 + 1) * 16)
                    ins = None
                    for dc in range(DC):
                        for tt in range(NT):
                            ins = t.matmul(ps_pr[0:rows, tt * 512:(tt + 1) * 512],
                                           sb_w[i % 2][:, dc * rows:(dc + 1) * rows],
                                           xn[:, dc * TPC + tt * 512:dc * TPC + (tt + 1) * 512],
                                           start=(dc == 0), stop=(dc == DC - 1))
                    ins.then_inc(s_pe, 1)
                    if rope:
                        t.wait_ge(s_ac, i + 1)
                        psl = sb_perm[:, :] if kind == "qr" else sb_perm[0:64, 0:64]
                        for tt in range(NT):
                            ins = t.matmul(ps_rt[0:rows, tt * 512:(tt + 1) * 512], psl,
                                           sb_o[0:rows, tt * 512:(tt + 1) * 512],
                                           start=True, stop=True)
                        ins.then_inc(s_rt, 1)

                @block.scalar
                def _(s, i=i, rows=rows, bc=bc):
                    s.wait_ge(s_pe, i + 1)
                    for tt in range(NT):
                        ins = s.activation(sb_o[0:rows, tt * 512:(tt + 1) * 512],
                                           ps_pr[0:rows, tt * 512:(tt + 1) * 512], AF.Identity,
                                           bias=sb_bpre[0:rows, bc:bc + 1])
                        if tt == NT - 1:
                            ins.then_inc(s_ac, 1)

                if rope:
                    nrt = sum(1 for j in range(i + 1) if chunks[j][4] != "plain")

                    @block.vector
                    def _(v, i=i, rows=rows, ndv_start=ndv_start, nrt=nrt):
                        v.wait_ge(s_tb, 32 * nrt)
                        v.wait_ge(s_ac, i + 1)
                        v.wait_ge(s_rt, nrt)
                        v.tensor_tensor(sb_t1[0:rows, :], sb_o[0:rows, :],
                                        sb_cos[0:rows, :], ALU.mult).then_inc(s_dv, 1)
                        for tt in range(NT):
                            v.tensor_tensor(sb_pay[0:rows, tt * 512:(tt + 1) * 512],
                                            ps_rt[0:rows, tt * 512:(tt + 1) * 512],
                                            sb_sin[0:rows, tt * 512:(tt + 1) * 512],
                                            ALU.mult).then_inc(s_dv, 1)
                        v.wait_ge(s_dv, ndv_start + 1 + NT)
                        v.tensor_tensor(sb_pay[0:rows, :], sb_pay[0:rows, :],
                                        sb_t1[0:rows, :], ALU.add).then_inc(s_dv, 1)

                    ndv += 2 + NT


def _emit_ffn_upgate(nc, L, xn, wg, wu, sb_sg, sb_su, sb_sudp, fuse_d, TPC):
    TH = TPC // 2
    NT2 = TH // 512
    NSLOT = IC * 2
    wg_r = wg.rearrange("(c p) n -> p c n", p=128)
    wu_r = wu.rearrange("(c p) n -> p c n", p=128)
    cast = (L == 0)

    with contextlib.ExitStack() as es:
        sb_wg = [es.enter_context(_sb(nc, f"fg_wg{i}", [128, DC * 128], BF16)) for i in range(2)]
        sb_wu = [es.enter_context(_sb(nc, f"fg_wu{i}", [128, DC * 128], BF16)) for i in range(2)]
        sb_gate = [es.enter_context(_sb(nc, f"fg_gate{i}", [128, TH], BF16)) for i in range(2)]
        sb_f1 = [es.enter_context(_sb(nc, f"fg_f1{i}", [128, TH], BF16)) for i in range(2)]
        sb_fuse = [es.enter_context(_sb(nc, f"fg_fuse{i}", [128, TH], BF16)) for i in range(2)]
        ps_g = [es.enter_context(_pt(nc, f"fg_psg{i}", [128, TH if TH <= 1024 else 1024], F32))
                for i in range(2)]
        ps_u = [es.enter_context(_pt(nc, f"fg_psu{i}", [128, TH if TH <= 1024 else 1024], F32))
                for i in range(2)]
        s_wl = [es.enter_context(_sm(nc, f"fg_wl{i}")) for i in range(2)]
        s_pe = es.enter_context(_sm(nc, "fg_pe"))
        s_ac = es.enter_context(_sm(nc, "fg_ac"))
        s_dv = es.enter_context(_sm(nc, "fg_dv"))
        s_st = es.enter_context(_sm(nc, "fg_st"))
        assert TH <= 1024, "token half must fit 2 PSUM banks"

        def wload(g_, i):
            dma = g_.dma_start
            dma(out=sb_wg[i % 2][:, :], in_=wg_r[:, :, i * 128:(i + 1) * 128]).then_inc(s_wl[i % 2], 16)
            dma(out=sb_wu[i % 2][:, :], in_=wu_r[:, :, i * 128:(i + 1) * 128]).then_inc(s_wl[i % 2], 16)

        with nc.Block(no_gpsimd_drain=True) as block:
            @block.gpsimd
            def _(g_):
                wload(g_, 0)

        for k in range(NSLOT + 1):
            with nc.Block(no_gpsimd_drain=True) as block:
                @block.gpsimd
                def _(g_, k=k):
                    if k < NSLOT and k % 2 == 0 and (k // 2) + 1 < IC:
                        wload(g_, (k // 2) + 1)

                @block.tensor
                def _(t, k=k):
                    if k >= NSLOT:
                        return
                    i, half = k // 2, k % 2
                    par = k % 2
                    t.wait_ge(s_wl[i % 2], 32 * (i // 2 + 1))
                    if k >= 2:
                        t.wait_ge(s_dv, 2 * (k - 1))
                    ins = None
                    for dc in range(DC):
                        for tt in range(NT2):
                            c0 = half * TH + tt * 512
                            ins = t.matmul(ps_g[par][:, tt * 512:(tt + 1) * 512],
                                           sb_wg[i % 2][:, dc * 128:(dc + 1) * 128],
                                           xn[:, dc * TPC + c0:dc * TPC + c0 + 512],
                                           start=(dc == 0), stop=(dc == DC - 1))
                    ins.then_inc(s_pe, 1)
                    for dc in range(DC):
                        for tt in range(NT2):
                            c0 = half * TH + tt * 512
                            ins = t.matmul(ps_u[par][:, tt * 512:(tt + 1) * 512],
                                           sb_wu[i % 2][:, dc * 128:(dc + 1) * 128],
                                           xn[:, dc * TPC + c0:dc * TPC + c0 + 512],
                                           start=(dc == 0), stop=(dc == DC - 1))
                    ins.then_inc(s_pe, 1)

                @block.scalar
                def _(s, k=k):
                    j = k - 1
                    if not (0 <= j < NSLOT):
                        return
                    i = j // 2
                    par = j % 2
                    s.wait_ge(s_pe, 2 * j + 1)
                    s.activation(sb_gate[par][:, :], ps_g[par][:, 0:TH], AF.Silu,
                                 scale=sb_sg[:, i:i + 1]).then_inc(s_ac, 1)

                @block.vector
                def _(v, k=k):
                    j = k - 1
                    if not (0 <= j < NSLOT):
                        return
                    i = j // 2
                    par = j % 2
                    v.wait_ge(s_ac, j + 1)
                    v.wait_ge(s_pe, 2 * j + 2)
                    if k >= 3:
                        v.wait_ge(s_st, 16 * (k - 2))
                    if sb_sudp is None:
                        v.tensor_scalar(sb_f1[par][:, :], ps_u[par][:, 0:TH],
                                        sb_su[:, i:i + 1], None, ALU.mult).then_inc(s_dv, 1)
                    else:
                        v.tensor_scalar(sb_f1[par][:, :], ps_u[par][:, 0:TH],
                                        sb_su[:, i:i + 1], sb_sudp[:, i:i + 1],
                                        ALU.mult, ALU.mult).then_inc(s_dv, 1)
                    v.wait_ge(s_dv, 2 * j + 1)
                    v.tensor_tensor(sb_fuse[par][:, :], sb_gate[par][:, :],
                                    sb_f1[par][:, :], ALU.mult).then_inc(s_dv, 1)

                @block.sync
                def _(sy, k=k):
                    j = k - 1
                    if not (0 <= j < NSLOT):
                        return
                    i, half = j // 2, j % 2
                    sy.wait_ge(s_dv, 2 * j + 2)
                    sy.dma_start(out=fuse_d[i * 128:(i + 1) * 128, half * TH:(half + 1) * TH],
                                 in_=sb_fuse[j % 2][:, :]).then_inc(s_st, 16)
                    sy.wait_ge(s_st, 16 * (j + 1))

        with nc.Block(no_gpsimd_drain=True) as block:
            @block.sync
            def _(sy):
                sy.wait_ge(s_st, 16 * NSLOT)


def _emit_ffn_down(nc, L, wd, sb_sdb, fuse_d, g_out, TPC):
    TT = 128
    NTT = TPC // TT
    DH = D // 2
    NU = 2 * NTT
    cast = (L == 0)

    with contextlib.ExitStack() as es:
        sb_wr = es.enter_context(_sb(nc, "fd_wr", [128, IC * DH], BF16))
        sb_ft = [es.enter_context(_sb(nc, f"fd_ft{i}", [128, IC * TT], BF16)) for i in range(2)]
        sb_t = [es.enter_context(_sb(nc, f"fd_t{i}", [128, DH], F32)) for i in range(2)]
        sb_g = [es.enter_context(_sb(nc, f"fd_g{i}", [128, DH], F32)) for i in range(2)]
        ps_dn = [es.enter_context(_pt(nc, f"fd_ps{i}", [128, DH], F32)) for i in range(2)]
        s_fl = [es.enter_context(_sm(nc, f"fd_fl{i}")) for i in range(2)]
        s_wh = [es.enter_context(_sm(nc, f"fd_wh{i}")) for i in range(4)]
        s_pe = es.enter_context(_sm(nc, "fd_pe"))
        s_ac = es.enter_context(_sm(nc, "fd_ac"))
        s_dv = es.enter_context(_sm(nc, "fd_dv"))
        s_st = es.enter_context(_sm(nc, "fd_st"))

        fuse_r = fuse_d.rearrange("(c p) n -> p c n", p=128)
        wd_r = wd.rearrange("(c p) n -> p c n", p=128)
        GC = IC // 4

        def emit_wd_loads(eng, dd, groups):
            for g in groups:
                eng.dma_start(
                    out=sb_wr[:, g * GC * DH:(g + 1) * GC * DH]
                        .rearrange("p (c n) -> p c n", c=GC),
                    in_=wd_r[:, g * GC:(g + 1) * GC, dd * DH:(dd + 1) * DH],
                ).then_inc(s_wh[g], 16)

        sy_grps = () if cast else (0, 1)
        gp_grps = (0, 1, 2, 3) if cast else (2, 3)

        with nc.Block(no_gpsimd_drain=True) as block:
            @block.sync
            def _(sy):
                sy.dma_start(out=sb_ft[0][:, :].rearrange("p (c t) -> p c t", c=IC),
                             in_=fuse_r[:, :, 0:TT]).then_inc(s_fl[0], 16)
                emit_wd_loads(sy, 0, sy_grps)

            @block.gpsimd
            def _(g_):
                emit_wd_loads(g_, 0, gp_grps)

        for u in range(NU + 2):
            dd, tt = u // NTT, u % NTT
            if u == NTT:
                with nc.Block(no_gpsimd_drain=True) as blk2:
                    @blk2.sync
                    def _(sy):
                        sy.wait_ge(s_pe, NTT)
                        emit_wd_loads(sy, 1, sy_grps)

                    @blk2.gpsimd
                    def _(g_):
                        g_.wait_ge(s_pe, NTT)
                        emit_wd_loads(g_, 1, gp_grps)

            with nc.Block(no_gpsimd_drain=True) as block:
                @block.sync
                def _(sy, u=u):
                    if u + 1 < NU:
                        u1 = u + 1
                        sy.dma_start(
                            out=sb_ft[u1 % 2][:, :].rearrange("p (c t) -> p c t", c=IC),
                            in_=fuse_r[:, :, (u1 % NTT) * TT:(u1 % NTT + 1) * TT],
                        ).then_inc(s_fl[u1 % 2], 16)
                    us = u - 2
                    if us >= 0:
                        dds, tts = us // NTT, us % NTT
                        sy.wait_ge(s_ac, us + 1)
                        if us >= 1:
                            sy.wait_ge(s_st, 16 * us)
                        sy.dma_start(out=g_out[tts * TT:(tts + 1) * TT, dds * DH:(dds + 1) * DH],
                                     in_=sb_g[us % 2][:, :]).then_inc(s_st, 16)
                        if us == NU - 1:
                            sy.wait_ge(s_st, 16 * NU)

                @block.tensor
                def _(t, u=u, dd=dd, tt=tt):
                    if u >= NU:
                        return
                    t.wait_ge(s_fl[u % 2], 16 * (u // 2 + 1))
                    ins = None
                    for ic in range(IC):
                        if ic % GC == 0:
                            t.wait_ge(s_wh[ic // GC], 16 * (dd + 1))
                        for hw in range(DH // 512):
                            ins = t.matmul(ps_dn[u % 2][:, hw * 512:(hw + 1) * 512],
                                           sb_ft[u % 2][:, ic * TT:ic * TT + TT],
                                           sb_wr[:, ic * DH + hw * 512:ic * DH + (hw + 1) * 512],
                                           start=(ic == 0), stop=(ic == IC - 1))
                    ins.then_inc(s_pe, 1)

                @block.vector
                def _(v, u=u):
                    uv = u - 1
                    if 0 <= uv < NU:
                        v.tensor_tensor(sb_t[uv % 2][:, :], ps_dn[uv % 2][:, :],
                                        sb_sdb[:, (uv // NTT) * DH:(uv // NTT + 1) * DH],
                                        ALU.mult).then_inc(s_dv, 1)

                @block.scalar
                def _(s, u=u):
                    uv = u - 1
                    if 0 <= uv < NU:
                        s.wait_ge(s_dv, uv + 1)
                        if uv >= 2:
                            s.wait_ge(s_st, 16 * (uv - 1))
                        s.activation(sb_g[uv % 2][:, :], sb_t[uv % 2][:, :],
                                     AF.Silu).then_inc(s_ac, 1)



def build_B(L, TPC, SEQ):
    NRANK = SEQ // TPC
    DPAY = D if L == 0 else PAY1
    nc = bass.Bass("TRN2", target_bir_lowering=False)

    payf = nc.dram_tensor("payf", [NRANK * DPAY, TPC], BF16, kind="ExternalInput")
    ident = nc.dram_tensor("ident", [128, 128], BF16, kind="ExternalInput").ap()
    ones_col = nc.dram_tensor("ones_col", [128, 1], BF16, kind="ExternalInput").ap()
    ones_row = nc.dram_tensor("ones_row", [1, 128], F32, kind="ExternalInput").ap()
    causal = nc.dram_tensor("causal", [128, 1024], BF16, kind="ExternalInput").ap()
    meta = nc.dram_tensor("meta", [1, 32], I32, kind="ExternalInput").ap()
    if L == 0:
        wqkv = nc.dram_tensor("wqkv", [D, HPC * 384], F32, kind="ExternalInput").ap()
        cmask = nc.dram_tensor("cmask", [128, SEQ], BF16, kind="ExternalInput").ap()
        smask = nc.dram_tensor("smask", [128, SEQ], BF16, kind="ExternalInput").ap()
        permT = nc.dram_tensor("permT", [128, 128], BF16, kind="ExternalInput").ap()
    else:
        wukv = nc.dram_tensor("wukv", [512, HPC * 256], F32, kind="ExternalInput").ap()
        wuq = nc.dram_tensor("wuq", [512, HPC * 128], F32, kind="ExternalInput").ap()
        bkv_rs = nc.dram_tensor("bkv_rs", [128, HPC * 2], F32, kind="ExternalInput").ap()
        bq_rs = nc.dram_tensor("bq_rs", [128, HPC], F32, kind="ExternalInput").ap()

    attnout = nc.dram_tensor("attnout", [HPC * 128, SEQ], BF16, kind="ExternalOutput").ap()
    qd = nc.dram_tensor("qd", [HPC * 128, SEQ], BF16)
    kd = nc.dram_tensor("kd", [HPC * 128, SEQ], BF16)
    vd = nc.dram_tensor("vd", [HPC * SEQ, 128], BF16)
    qrd = (nc.dram_tensor("qrm", [HPC * 64, SEQ], BF16, kind="ExternalInput")
           if L == 1 else None)

    es = contextlib.ExitStack()
    with es:
        sb_ident = es.enter_context(_sb(nc, "c_ident", [128, 128], BF16))
        sb_ones = es.enter_context(_sb(nc, "c_ones", [128, 1], BF16))
        sb_or = es.enter_context(_sb(nc, "c_or", [1, 128], F32))
        sb_caus = es.enter_context(_sb(nc, "c_caus", [128, 1024], BF16))
        sb_meta = es.enter_context(_sb(nc, "c_meta", [1, 32], I32))
        if L == 0:
            sb_perm = es.enter_context(_sb(nc, "c_perm", [128, 128], BF16))
            sb_w = es.enter_context(_sb(nc, "c_wqkv", [128, DC * HPC * 384], BF16))
            WXW = HPC * 384
        else:
            sb_w = es.enter_context(_sb(nc, "c_wukv", [128, 4 * HPC * 256], BF16))
            sb_wq = es.enter_context(_sb(nc, "c_wuq", [128, 4 * HPC * 128], BF16))
            sb_bkv = es.enter_context(_sb(nc, "c_bkv", [128, HPC * 2], F32))
            sb_bq = es.enter_context(_sb(nc, "c_bq", [128, HPC], F32))
        ld0 = es.enter_context(_sm(nc, "ld0"))

        with nc.Block(no_gpsimd_drain=True) as block:
            @block.gpsimd
            def _(g_):
                n = 0
                pairs = [(sb_ident[:, :], ident), (sb_ones[:, :], ones_col),
                         (sb_or[:, :], ones_row), (sb_caus[:, :], causal),
                         (sb_meta[:, :], meta)]
                if L == 0:
                    pairs += [(sb_perm[:, :], permT),
                              (sb_w[:, :].rearrange("p (c n) -> p c n", c=DC),
                               wqkv.rearrange("(c p) n -> p c n", p=128))]
                else:
                    pairs += [(sb_w[:, :].rearrange("p (c n) -> p c n", c=4),
                               wukv.rearrange("(c p) n -> p c n", p=128)),
                              (sb_wq[:, :].rearrange("p (c n) -> p c n", c=4),
                               wuq.rearrange("(c p) n -> p c n", p=128)),
                              (sb_bkv[:, :], bkv_rs), (sb_bq[:, :], bq_rs)]
                for dst, src in pairs:
                    g_.dma_start(out=dst, in_=src).then_inc(ld0, 16)
                    n += 16
                g_.wait_ge(ld0, n)

        if L == 0:
            _emit_proj_l0(nc, payf, sb_w, sb_perm, sb_ident, cmask, smask,
                          qd, kd, vd, TPC, SEQ)
        else:
            _emit_proj_l1(nc, payf, sb_w, sb_wq, sb_bkv, sb_bq, sb_ident,
                          qd, kd, vd, qrd, TPC, SEQ)

        _emit_sdpa(nc, L, qd, kd, vd, qrd, payf, meta,
                   sb_ones, sb_or, sb_caus, attnout, TPC, SEQ)

    return nc


def _vd_store_ap(vd, h, t0, ngrp, SEQ):
    return bass.AP(vd, (h * SEQ + t0) * 128,
                   [[128, 128], [128 * 128, ngrp], [1, 128]])


def _emit_proj_l0(nc, payf, sb_w, sb_perm, sb_ident, cmask, smask, qd, kd, vd, TPC, SEQ):
    NRANK = SEQ // TPC
    TH = TPC // 2
    NT2 = TH // 512
    NG2 = TH // 128
    WXW = HPC * 384
    UPB = HPC * 3 * 2

    with contextlib.ExitStack() as es:
        sb_xb = es.enter_context(_sb(nc, "pj_xb", [128, DC * TPC], BF16))
        sb_raw = [es.enter_context(_sb(nc, f"pj_raw{i}", [128, TH], BF16)) for i in range(2)]
        sb_cm = es.enter_context(_sb(nc, "pj_cm", [128, TPC], BF16))
        sb_sm = es.enter_context(_sb(nc, "pj_sm", [128, TPC], BF16))
        sb_t1 = es.enter_context(_sb(nc, "pj_t1", [128, TH], BF16))
        sb_fin = [es.enter_context(_sb(nc, f"pj_fin{i}", [128, TH], BF16)) for i in range(2)]
        sb_vt = [es.enter_context(_sb(nc, f"pj_vt{i}", [128, TH], BF16)) for i in range(2)]
        ps_pr = [es.enter_context(_pt(nc, f"pj_pr{i}", [128, 1024], F32)) for i in range(2)]
        ps_rt = es.enter_context(_pt(nc, "pj_rt", [128, 1024], F32))
        ps_t = es.enter_context(_pt(nc, "pj_t", [128, 1024], BF16))
        s_xl = es.enter_context(_sm(nc, "pj_xl"))
        s_pe = es.enter_context(_sm(nc, "pj_pe"))
        s_rt = es.enter_context(_sm(nc, "pj_rt"))
        s_ac = es.enter_context(_sm(nc, "pj_ac"))
        s_dv = es.enter_context(_sm(nc, "pj_dv"))
        s_st = es.enter_context(_sm(nc, "pj_st"))
        s_dc = es.enter_context(_sm(nc, "pj_dc"))
        ndc = [0]

        def unit_info(u):
            tb, r = divmod(u, UPB)
            h, r2 = divmod(r, 6)
            o, half = divmod(r2, 2)
            return tb, h, o, half

        NU = NRANK * UPB
        for slot in range(NU + 1):
            with nc.Block(no_gpsimd_drain=True) as block:
                @block.sync
                def _(sy, slot=slot):
                    if slot < NU:
                        tb, h, o, half = unit_info(slot)
                        if slot % UPB == 0:
                            if tb >= 1:
                                sy.wait_ge(s_pe, tb * UPB)
                            sy.dma_start(
                                out=sb_xb[:, :].rearrange("p (c t) -> p c t", c=DC),
                                in_=payf.ap().rearrange("(r c p) t -> p r c t",
                                                        r=NRANK, p=128)[:, tb, :, :],
                            ).then_inc(s_xl, 16)
                            sy.dma_start(out=sb_cm[:, :],
                                         in_=cmask[:, tb * TPC:(tb + 1) * TPC]).then_inc(s_xl, 16)
                            sy.dma_start(out=sb_sm[:, :],
                                         in_=smask[:, tb * TPC:(tb + 1) * TPC]).then_inc(s_xl, 16)
                    u = slot - 1
                    if u >= 0:
                        tb, h, o, half = unit_info(u)
                        sy.wait_ge(s_dv, u + 1)
                        t0 = tb * TPC + half * TH
                        if o < 2:
                            dst = qd if o == 0 else kd
                            sy.dma_start(out=dst.ap()[h * 128:(h + 1) * 128, t0:t0 + TH],
                                         in_=sb_fin[u % 2][:, :]).then_inc(s_st, 16)
                        else:
                            sy.dma_start(out=_vd_store_ap(vd, h, t0, NG2, SEQ),
                                         in_=sb_vt[u % 2][:, :].rearrange(
                                             "p (g d) -> p g d", d=128)).then_inc(s_st, 16)
                        sy.wait_ge(s_st, 16 * (u + 1))

                @block.tensor
                def _(t, slot=slot):
                    if slot < NU:
                        u = slot
                        tb, h, o, half = unit_info(u)
                        par = u % 2
                        t.wait_ge(s_xl, 48 * (tb + 1))
                        if u >= 2:
                            t.wait_ge(s_ac, u - 1)
                        wcol = h * 384 + o * 128
                        ins = None
                        for dc in range(DC):
                            for tt in range(NT2):
                                c0 = half * TH + tt * 512
                                ins = t.matmul(ps_pr[par][:, tt * 512:(tt + 1) * 512],
                                               sb_w[:, dc * WXW + wcol:dc * WXW + wcol + 128],
                                               sb_xb[:, dc * TPC + c0:dc * TPC + c0 + 512],
                                               start=(dc == 0), stop=(dc == DC - 1))
                        ins.then_inc(s_pe, 1)
                    u = slot - 1
                    if u >= 0:
                        tb, h, o, half = unit_info(u)
                        par = u % 2
                        t.wait_ge(s_ac, u + 1)
                        if o < 2:
                            ins = None
                            for tt in range(NT2):
                                ins = t.matmul(ps_rt[:, tt * 512:(tt + 1) * 512], sb_perm[:, :],
                                               sb_raw[par][:, tt * 512:(tt + 1) * 512],
                                               start=True, stop=True)
                            ins.then_inc(s_rt, 1)
                        else:
                            t.wait_ge(s_dv, u)
                            ins = None
                            for gi in range(NG2):
                                ins = t.transpose(ps_t[:, gi * 128:(gi + 1) * 128],
                                                  sb_raw[par][:, gi * 128:(gi + 1) * 128],
                                                  sb_ident[:, :])
                            ins.then_inc(s_rt, 1)

                @block.scalar
                def _(s, slot=slot):
                    u = slot - 1
                    if u >= 0:
                        tb, h, o, half = unit_info(u)
                        par = u % 2
                        s.wait_ge(s_pe, u + 1)
                        if u >= 2:
                            s.wait_ge(s_dv, u - 1)
                        s.activation(sb_raw[par][:, :], ps_pr[par][:, 0:TH],
                                     AF.Identity).then_inc(s_ac, 1)

                @block.vector
                def _(v, slot=slot, ndc=ndc):
                    u = slot - 1
                    if u >= 0:
                        tb, h, o, half = unit_info(u)
                        par = u % 2
                        v.wait_ge(s_rt, u + 1)
                        if u >= 2:
                            v.wait_ge(s_st, 16 * (u - 1))
                        c0 = half * TH
                        if o < 2:
                            v.tensor_tensor(sb_t1[:, :], sb_raw[par][:, :],
                                            sb_cm[:, c0:c0 + TH], ALU.mult).then_inc(s_dc, 1)
                            ndc[0] += 1
                            v.tensor_tensor(sb_fin[u % 2][:, :], ps_rt[:, 0:TH],
                                            sb_sm[:, c0:c0 + TH], ALU.mult).then_inc(s_dc, 1)
                            ndc[0] += 1
                            v.wait_ge(s_dc, ndc[0])
                            v.tensor_tensor(sb_fin[u % 2][:, :], sb_fin[u % 2][:, :],
                                            sb_t1[:, :], ALU.add).then_inc(s_dv, 1)
                        else:
                            v.tensor_copy(sb_vt[u % 2][:, :], ps_t[:, 0:TH]).then_inc(s_dv, 1)


def _emit_proj_l1(nc, payf, sb_w, sb_wq, sb_bkv, sb_bq, sb_ident, qd, kd, vd, qrd, TPC, SEQ):
    NRANK = SEQ // TPC
    TH = TPC // 2
    NT2 = TH // 512
    NG2 = TH // 128
    UPB = HPC * 3 * 2
    WKV = HPC * 256
    WQ = HPC * 128

    with contextlib.ExitStack() as es:
        sb_cb = es.enter_context(_sb(nc, "pj_cb", [128, 8 * TPC], BF16))
        sb_raw = [es.enter_context(_sb(nc, f"pj_raw{i}", [128, TH], BF16)) for i in range(2)]
        sb_fin = [es.enter_context(_sb(nc, f"pj_fin{i}", [128, TH], BF16)) for i in range(2)]
        sb_vt = [es.enter_context(_sb(nc, f"pj_vt{i}", [128, TH], BF16)) for i in range(2)]
        sb_junk = es.enter_context(_sb(nc, "pj_junk", [1, 4], BF16))
        ps_pr = [es.enter_context(_pt(nc, f"pj_pr{i}", [128, 1024], F32)) for i in range(2)]
        ps_t = es.enter_context(_pt(nc, "pj_t", [128, 1024], BF16))
        s_xl = es.enter_context(_sm(nc, "pj_xl"))
        s_pe = es.enter_context(_sm(nc, "pj_pe"))
        s_rt = es.enter_context(_sm(nc, "pj_rt"))
        s_ac = es.enter_context(_sm(nc, "pj_ac"))
        s_dv = es.enter_context(_sm(nc, "pj_dv"))
        s_st = es.enter_context(_sm(nc, "pj_st"))
        s_qr = es.enter_context(_sm(nc, "pj_qr"))

        def unit_info(u):
            tb, r = divmod(u, UPB)
            h, r2 = divmod(r, 6)
            o, half = divmod(r2, 2)
            return tb, h, o, half

        NU = NRANK * UPB
        for slot in range(NU + 1):
            with nc.Block(no_gpsimd_drain=True) as block:
                @block.sync
                def _(sy, slot=slot):
                    if slot < NU:
                        tb, h, o, half = unit_info(slot)
                        if slot % UPB == 0:
                            if tb >= 1:
                                sy.wait_ge(s_pe, tb * UPB)
                            sy.dma_start(
                                out=sb_cb[:, :].rearrange("p (c t) -> p c t", c=8),
                                in_=bass.AP(payf, tb * PAY1 * TPC,
                                            [[TPC, 128], [128 * TPC, 8], [1, TPC]]),
                            ).then_inc(s_xl, 16)
                    u = slot - 1
                    if u >= 0:
                        tb, h, o, half = unit_info(u)
                        sy.wait_ge(s_dv, u + 1)
                        t0 = tb * TPC + half * TH
                        if o < 2:
                            dst = kd if o == 0 else qd
                            sy.dma_start(out=dst.ap()[h * 128:(h + 1) * 128, t0:t0 + TH],
                                         in_=sb_fin[u % 2][:, :]).then_inc(s_st, 16)
                        else:
                            sy.dma_start(out=_vd_store_ap(vd, h, t0, NG2, SEQ),
                                         in_=sb_vt[u % 2][:, :].rearrange(
                                             "p (g d) -> p g d", d=128)).then_inc(s_st, 16)
                        sy.wait_ge(s_st, 16 * (u + 1))

                @block.tensor
                def _(t, slot=slot):
                    if slot < NU:
                        u = slot
                        tb, h, o, half = unit_info(u)
                        par = u % 2
                        t.wait_ge(s_xl, 16 * (tb + 1))
                        if u >= 2:
                            t.wait_ge(s_ac, u - 1)
                        for cc in range(4):
                            if o == 1:
                                w_ap = sb_wq[:, cc * WQ + h * 128:cc * WQ + h * 128 + 128]
                                cbase = 4 + cc
                            else:
                                wcol = h * 256 + (0 if o == 0 else 128)
                                w_ap = sb_w[:, cc * WKV + wcol:cc * WKV + wcol + 128]
                                cbase = cc
                            for tt in range(NT2):
                                c0 = half * TH + tt * 512
                                ins = t.matmul(ps_pr[par][:, tt * 512:(tt + 1) * 512], w_ap,
                                               sb_cb[:, cbase * TPC + c0:cbase * TPC + c0 + 512],
                                               start=(cc == 0), stop=(cc == 3))
                        ins.then_inc(s_pe, 1)
                    u = slot - 1
                    if u >= 0:
                        tb, h, o, half = unit_info(u)
                        par = u % 2
                        if o == 2:
                            t.wait_ge(s_ac, u + 1)
                            t.wait_ge(s_dv, u)
                            ins = None
                            for gi in range(NG2):
                                ins = t.transpose(ps_t[:, gi * 128:(gi + 1) * 128],
                                                  sb_raw[par][:, gi * 128:(gi + 1) * 128],
                                                  sb_ident[:, :])
                            ins.then_inc(s_rt, 1)

                @block.scalar
                def _(s, slot=slot):
                    u = slot - 1
                    if u >= 0:
                        tb, h, o, half = unit_info(u)
                        par = u % 2
                        s.wait_ge(s_pe, u + 1)
                        if u >= 2:
                            s.wait_ge(s_dv, u - 1)
                        bias = (sb_bkv[:, 2 * h:2 * h + 1] if o == 0 else
                                sb_bq[:, h:h + 1] if o == 1 else
                                sb_bkv[:, 2 * h + 1:2 * h + 2])
                        if o == 2:
                            s.activation(sb_raw[par][:, :], ps_pr[par][:, 0:TH],
                                         AF.Identity, bias=bias).then_inc(s_ac, 1)
                        else:
                            s.activation(sb_fin[u % 2][:, :], ps_pr[par][:, 0:TH],
                                         AF.Identity, bias=bias).then_inc(s_ac, 1)

                @block.vector
                def _(v, slot=slot):
                    u = slot - 1
                    if u >= 0:
                        tb, h, o, half = unit_info(u)
                        if u >= 2:
                            v.wait_ge(s_st, 16 * (u - 1))
                        if o == 2:
                            v.wait_ge(s_rt, sum(1 for j in range(u + 1)
                                                if unit_info(j)[2] == 2))
                            v.tensor_copy(sb_vt[u % 2][:, :], ps_t[:, 0:TH]).then_inc(s_dv, 1)
                        else:
                            v.wait_ge(s_ac, u + 1)
                            v.tensor_copy(sb_junk[0:1, 0:2],
                                          sb_fin[u % 2][0:1, 0:2]).then_inc(s_dv, 1)



def _emit_qr_copy(nc, payf, qrd, sb_meta, TPC, SEQ):
    NRANK = SEQ // TPC
    with contextlib.ExitStack() as es:
        s_qr = es.enter_context(_sm(nc, "qrc"))
        with nc.Block(no_gpsimd_drain=True) as block:
            @block.gpsimd
            def _(g_):
                for r in range(NRANK):
                    val = g_.value_load(sb_meta[0:1, 8 + r:9 + r],
                                        min_val=0, max_val=NRANK * PAY1 - 256)
                    g_.dma_start(
                        out=qrd.ap()[:, r * TPC:(r + 1) * TPC],
                        in_=payf.ap()[bass.ds(val, HPC * 64), :],
                    ).then_inc(s_qr, 16)
                g_.wait_ge(s_qr, 16 * NRANK)


def _emit_sdpa(nc, L, qd, kd, vd, qrd, payf, meta, sb_ones, sb_or, sb_caus,
               attnout, TPC, SEQ):
    NRANK = SEQ // TPC
    SQT = SEQ // 512
    DPAY = D if L == 0 else PAY1

    with contextlib.ExitStack() as es:
        sb_meta = es.enter_context(_sb(nc, "sd_meta", [1, 8], I32))
        sb_k = es.enter_context(_sb(nc, "sd_k", [128, SEQ], BF16))
        sb_v = es.enter_context(_sb(nc, "sd_v", [128, SEQ], BF16))
        sb_qt = [es.enter_context(_sb(nc, f"sd_q{i}", [128, 512], BF16)) for i in range(2)]
        sb_p = [es.enter_context(_sb(nc, f"sd_p{i}", [128, 512], BF16)) for i in range(3)]
        sb_pt = [es.enter_context(_sb(nc, f"sd_pt{i}", [128, 512], BF16)) for i in range(2)]
        sb_li = es.enter_context(_sb(nc, "sd_li", [1, 512], F32))
        sb_bc = [es.enter_context(_sb(nc, f"sd_bc{i}", [128, 512], F32)) for i in range(2)]
        sb_ao = [es.enter_context(_sb(nc, f"sd_ao{i}", [128, 512], BF16)) for i in range(2)]
        if L == 1:
            sb_kr = es.enter_context(_sb(nc, "sd_kr", [64, SEQ], BF16))
            sb_qr = [es.enter_context(_sb(nc, f"sd_qr{i}", [64, 512], BF16)) for i in range(2)]
        ps_sc = [es.enter_context(_pt(nc, f"sd_sc{i}", [128, 512], F32)) for i in range(3)]
        ps_out = [es.enter_context(_pt(nc, f"sd_o{i}", [128, 512], F32)) for i in range(2)]
        ps_l = [es.enter_context(_pt(nc, f"sd_l{i}", [1, 512], F32)) for i in range(2)]
        ps_bc = es.enter_context(_pt(nc, "sd_bc", [128, 512], F32))
        s_kv = es.enter_context(_sm(nc, "sd_kv"))
        s_ql = [es.enter_context(_sm(nc, f"sd_ql{i}")) for i in range(2)]
        s_mm = es.enter_context(_sm(nc, "sd_mm"))
        s_ex = es.enter_context(_sm(nc, "sd_ex"))
        s_mk = es.enter_context(_sm(nc, "sd_mk"))
        s_pv = es.enter_context(_sm(nc, "sd_pv"))
        s_bm = es.enter_context(_sm(nc, "sd_bm"))
        s_bv = es.enter_context(_sm(nc, "sd_bv"))
        s_dc = es.enter_context(_sm(nc, "sd_dc"))
        s_fo = es.enter_context(_sm(nc, "sd_fo"))
        s_st = es.enter_context(_sm(nc, "sd_st"))

        def tiles_of():
            tl = []
            for qt in range(SQT):
                nk = 4 * qt + 4
                for kt in range(nk):
                    tl.append((qt, kt, kt >= 4 * qt))
            return tl

        TL = tiles_of()
        NTILE = len(TL)
        qt_last = {}
        qt_first = {}
        for m, (qt, kt, dg) in enumerate(TL):
            qt_last[qt] = m
            if qt not in qt_first:
                qt_first[qt] = m
        diag_upto = np.cumsum([1 if t[2] else 0 for t in TL]).tolist()

        nkv = 1 + 1 + (NRANK if L == 1 else 0)

        for h in range(HPC):
            with nc.Block(no_gpsimd_drain=True) as block:
                @block.sync
                def _(sy, h=h):
                    base_kv = h * nkv * 16
                    sy.dma_start(out=sb_k[:, :],
                                 in_=kd.ap()[h * 128:(h + 1) * 128, :]).then_inc(s_kv, 16)
                    sy.dma_start(out=sb_v[:, :].rearrange("p (g d) -> p g d", d=128),
                                 in_=bass.AP(vd, h * SEQ * 128,
                                             [[128, 128], [128 * 128, SEQ // 128], [1, 128]]),
                                 ).then_inc(s_kv, 16)
                    if L == 1:
                        for r in range(NRANK):
                            sy.dma_start(out=sb_kr[:, r * TPC:(r + 1) * TPC],
                                         in_=payf.ap()[r * PAY1 + 1024:r * PAY1 + 1088, :],
                                         ).then_inc(s_kv, 16)
                    for qt in range(SQT + 2):
                        if qt < SQT:
                            if qt >= 2:
                                sy.wait_ge(s_pv, h * NTILE + qt_last[qt - 2] + 1)
                            sy.dma_start(out=sb_qt[qt % 2][:, :],
                                         in_=qd.ap()[h * 128:(h + 1) * 128,
                                                     qt * 512:(qt + 1) * 512]).then_inc(s_ql[qt % 2], 16)
                            if L == 1:
                                sy.dma_start(out=sb_qr[qt % 2][0:64, :],
                                             in_=qrd.ap()[h * 64:(h + 1) * 64,
                                                          qt * 512:(qt + 1) * 512]).then_inc(s_ql[qt % 2], 16)
                        u = qt - 2
                        if u >= 0:
                            sy.wait_ge(s_fo, h * SQT + u + 1)
                            sy.dma_start(out=attnout[h * 128:(h + 1) * 128,
                                                     u * 512:(u + 1) * 512],
                                         in_=sb_ao[u % 2][:, :]).then_inc(s_st, 16)
                            sy.wait_ge(s_st, 16 * (h * SQT + u + 1))

                @block.tensor
                def _(t, h=h):
                    NQL = 1 if L == 0 else 2
                    t.wait_ge(s_kv, (h + 1) * nkv * 16)
                    mbase = h * NTILE

                    def emit_scores(m):
                        qt, kt, dg = TL[m]
                        if m == qt_first[qt]:
                            t.wait_ge(s_ql[qt % 2], 16 * NQL * ((h * SQT + qt) // 2 + 1))
                            if qt >= 2:
                                t.wait_ge(s_fo, h * SQT + qt - 1)
                        if m >= 3:
                            t.wait_ge(s_ex, mbase + m - 2)
                        ins = t.matmul(ps_sc[m % 3][:, :],
                                       sb_k[:, kt * 128:(kt + 1) * 128],
                                       sb_qt[qt % 2][:, :],
                                       start=True, stop=(L == 0))
                        if L == 1:
                            ins = t.matmul(ps_sc[m % 3][:, :],
                                           sb_kr[0:64, kt * 128:(kt + 1) * 128],
                                           sb_qr[qt % 2][0:64, :],
                                           start=False, stop=True)
                        ins.then_inc(s_mm, 1)

                    def emit_pv(m):
                        qt, kt, dg = TL[m]
                        if dg:
                            t.wait_ge(s_mk, _dg_count(TL, h, m, diag_upto))
                        else:
                            t.wait_ge(s_ex, mbase + m + 1)
                        first = (m == qt_first[qt])
                        last = (m == qt_last[qt])
                        t.matmul(ps_out[qt % 2][:, :],
                                 sb_v[:, kt * 128:(kt + 1) * 128],
                                 sb_p[m % 3][:, :],
                                 start=first, stop=last)
                        t.matmul(ps_l[qt % 2][0:1, :],
                                 sb_ones[:, 0:1],
                                 sb_p[m % 3][:, :],
                                 start=first, stop=last).then_inc(s_pv, 1)
                        if last:
                            t.wait_ge(s_dc, h * SQT + qt + 1)
                            if h * SQT + qt >= 1:
                                t.wait_ge(s_bv, h * SQT + qt)
                            t.matmul(ps_bc[:, :], sb_or[:, :], sb_li[:, :],
                                     start=True, stop=True).then_inc(s_bm, 1)

                    for m in range(NTILE):
                        emit_scores(m)
                        if m >= 2:
                            emit_pv(m - 2)
                    emit_pv(NTILE - 2)
                    emit_pv(NTILE - 1)

                @block.scalar
                def _(s, h=h):
                    mbase = h * NTILE
                    dbase = 0 if h == 0 else diag_upto[-1] * h
                    for m, (qt, kt, dg) in enumerate(TL):
                        s.wait_ge(s_mm, mbase + m + 1)
                        if dg:
                            dgc = dbase + diag_upto[m]
                            if dgc >= 3:
                                s.wait_ge(s_mk, dgc - 2)
                            s.activation(sb_pt[(dgc - 1) % 2][:, :], ps_sc[m % 3][:, :],
                                         AF.Exp, scale=ISQ).then_inc(s_ex, 1)
                        else:
                            if m >= 3:
                                s.wait_ge(s_pv, mbase + m - 2)
                            s.activation(sb_p[m % 3][:, :], ps_sc[m % 3][:, :],
                                         AF.Exp, scale=ISQ).then_inc(s_ex, 1)
                        if m == qt_last[qt]:
                            s.wait_ge(s_bm, h * SQT + qt + 1)
                            s.activation(sb_bc[qt % 2][:, :], ps_bc[:, :],
                                         AF.Identity).then_inc(s_bv, 1)

                @block.vector
                def _(v, h=h):
                    mbase = h * NTILE
                    dbase = 0 if h == 0 else diag_upto[-1] * h
                    for m, (qt, kt, dg) in enumerate(TL):
                        if dg:
                            dgc = dbase + diag_upto[m]
                            v.wait_ge(s_ex, mbase + m + 1)
                            if m >= 3:
                                v.wait_ge(s_pv, mbase + m - 2)
                            off = 512 - (kt * 128 - qt * 512)
                            v.tensor_tensor(sb_p[m % 3][:, :], sb_pt[(dgc - 1) % 2][:, :],
                                            sb_caus[:, off:off + 512], ALU.mult).then_inc(s_mk, 1)
                        if m == qt_last[qt]:
                            iq = h * SQT + qt
                            v.wait_ge(s_pv, mbase + m + 1)
                            if iq >= 1:
                                v.wait_ge(s_bm, iq)
                            v.reciprocal(sb_li[:, :], ps_l[qt % 2][0:1, :]).then_inc(s_dc, 1)
                            v.wait_ge(s_bv, iq + 1)
                            if iq >= 2:
                                v.wait_ge(s_st, 16 * (iq - 1))
                            v.tensor_tensor(sb_ao[qt % 2][:, :], ps_out[qt % 2][:, :],
                                            sb_bc[qt % 2][:, :], ALU.mult).then_inc(s_fo, 1)

        with nc.Block(no_gpsimd_drain=True) as block:
            @block.sync
            def _(sy):
                sy.wait_ge(s_st, 16 * HPC * SQT)


def _dg_count(TL, h, m, diag_upto):
    return (diag_upto[-1] * h) + diag_upto[m]



def _emit_post(nc, sin_dma, wo_cm, bias_row, ones_row_sb, g_in, x_in, xo,
               TPC, s_extld=None):
    NTG = TPC // 128
    es = contextlib.ExitStack()
    with es:
        sb_s = es.enter_context(_sb(nc, "po_s", [128, DC * TPC], BF16))
        sb_w = es.enter_context(_sb(nc, "po_w", [128, DC * D], BF16))
        sb_br = es.enter_context(_sb(nc, "po_br", [1, D], BF16))
        sb_one1 = es.enter_context(_sb(nc, "po_one1", [1, 128], BF16))
        sb_g = [es.enter_context(_sb(nc, f"po_g{i}", [128, D], F32)) for i in range(2)]
        sb_x = [es.enter_context(_sb(nc, f"po_x{i}", [128, D], F32)) for i in range(2)]
        sb_sb = [es.enter_context(_sb(nc, f"po_sb{i}", [128, D], F32)) for i in range(2)]
        ps_s = [es.enter_context(_pt(nc, f"po_ps{i}", [128, D], F32)) for i in range(2)]
        s_l0 = es.enter_context(_sm(nc, "po_l0"))
        s_lb = es.enter_context(_sm(nc, "po_lb"))
        s_ld = [es.enter_context(_sm(nc, f"po_ld{i}")) for i in range(2)]
        s_pe = es.enter_context(_sm(nc, "po_pe"))
        s_ac = es.enter_context(_sm(nc, "po_ac"))
        s_dv = es.enter_context(_sm(nc, "po_dv"))
        s_st = es.enter_context(_sm(nc, "po_st"))

        nl0 = [0]
        with nc.Block(no_gpsimd_drain=True) as block:
            @block.gpsimd
            def _(g_):
                g_.memset(sb_one1[:, :], 1.0)
                g_.dma_start(out=sb_w[:, :].rearrange("p (c n) -> p c n", c=DC),
                             in_=wo_cm.rearrange("(c p) n -> p c n", p=128)).then_inc(s_lb, 16)
                g_.dma_start(out=sb_br[:, :], in_=bias_row).then_inc(s_lb, 16)
                g_.wait_ge(s_lb, 32)

        for slot in range(NTG + 1):
            with nc.Block(no_gpsimd_drain=True) as block:
                if slot == 0:
                    @block.gpsimd
                    def _(g_):
                        nl0[0] += sin_dma(g_, sb_s, s_l0)

                @block.sync
                def _(sy, slot=slot):
                    if slot < NTG:
                        tg = slot
                        if tg >= 2:
                            sy.wait_ge(s_dv, 2 * (tg - 1))
                            sy.wait_ge(s_st, 16 * (tg - 1))
                        sy.dma_start(out=sb_g[tg % 2][:, :],
                                     in_=g_in[tg * 128:(tg + 1) * 128, :]).then_inc(s_ld[tg % 2], 16)
                        sy.dma_start(out=sb_x[tg % 2][:, :],
                                     in_=x_in[tg * 128:(tg + 1) * 128, :]).then_inc(s_ld[tg % 2], 16)
                    u = slot - 1
                    if u >= 0:
                        sy.wait_ge(s_dv, 2 * (u + 1))
                        if u >= 1:
                            sy.wait_ge(s_st, 16 * u)
                        sy.dma_start(out=xo[u * 128:(u + 1) * 128, :],
                                     in_=sb_x[u % 2][:, :]).then_inc(s_st, 16)
                        if u == NTG - 1:
                            sy.wait_ge(s_st, 16 * NTG)

                @block.tensor
                def _(t, slot=slot):
                    if slot < NTG:
                        tg = slot
                        par = tg % 2
                        if tg == 0:
                            t.wait_ge(s_l0, nl0[0])
                            t.wait_ge(s_lb, 32)
                            if s_extld is not None:
                                t.wait_ge(s_extld[0], s_extld[1])
                        if tg >= 2:
                            t.wait_ge(s_ac, tg - 1)
                        ins = None
                        for hw in range(D // 512):
                            t.matmul(ps_s[par][:, hw * 512:(hw + 1) * 512],
                                     sb_one1[:, :], sb_br[:, hw * 512:(hw + 1) * 512],
                                     start=True, stop=False)
                        for a in range(DC):
                            for hw in range(D // 512):
                                ins = t.matmul(ps_s[par][:, hw * 512:(hw + 1) * 512],
                                               sb_s[:, a * TPC + tg * 128:a * TPC + (tg + 1) * 128],
                                               sb_w[:, a * D + hw * 512:a * D + (hw + 1) * 512],
                                               start=False, stop=(a == DC - 1))
                        ins.then_inc(s_pe, 1)

                @block.scalar
                def _(s, slot=slot):
                    u = slot - 1
                    if u >= 0:
                        s.wait_ge(s_pe, u + 1)
                        if u >= 2:
                            s.wait_ge(s_dv, 2 * (u - 1))
                        s.activation(sb_sb[u % 2][:, :], ps_s[u % 2][:, 0:D],
                                     AF.Identity).then_inc(s_ac, 1)

                @block.vector
                def _(v, slot=slot):
                    u = slot - 1
                    if u >= 0:
                        par = u % 2
                        v.wait_ge(s_ac, u + 1)
                        v.wait_ge(s_ld[u % 2], 32 * (u // 2 + 1))
                        v.tensor_tensor(sb_sb[par][:, :], sb_sb[par][:, :],
                                        sb_g[par][:, :], ALU.mult).then_inc(s_dv, 1)
                        v.wait_ge(s_dv, 2 * u + 1)
                        v.tensor_tensor(sb_x[par][:, :], sb_sb[par][:, :],
                                        sb_x[par][:, :], ALU.add).then_inc(s_dv, 1)


def build_C(TPC):
    NTG = TPC // 128
    nc = bass.Bass("TRN2", target_bir_lowering=False)
    s_in = nc.dram_tensor("s_in", [D, TPC], BF16, kind="ExternalInput").ap()
    wo_cm = nc.dram_tensor("wo_cm", [D, D], F32, kind="ExternalInput").ap()
    bias_row = nc.dram_tensor("bias_row", [1, D], F32, kind="ExternalInput").ap()
    ones_row = nc.dram_tensor("ones_row", [1, 128], F32, kind="ExternalInput").ap()
    g_in = nc.dram_tensor("g", [TPC, D], F32, kind="ExternalInput").ap()
    x_in = nc.dram_tensor("x", [TPC, D], F32, kind="ExternalInput").ap()
    xo = nc.dram_tensor("xo", [TPC, D], F32, kind="ExternalOutput").ap()
    es = contextlib.ExitStack()
    with es:
        sb_or = es.enter_context(_sb(nc, "c_or", [1, 128], F32))
        s_c = es.enter_context(_sm(nc, "c_or_ld"))
        with nc.Block(no_gpsimd_drain=True) as block:
            @block.gpsimd
            def _(g_):
                g_.dma_start(out=sb_or[:, :], in_=ones_row).then_inc(s_c, 16)
                g_.wait_ge(s_c, 16)

        def sin_dma(g_, sb_s, sem):
            g_.dma_start(out=sb_s[:, :].rearrange("p (c t) -> p c t", c=DC),
                         in_=s_in.rearrange("(c p) t -> p c t", p=128)).then_inc(sem, 16)
            return 16

        _emit_post(nc, sin_dma, wo_cm, bias_row, sb_or, g_in, x_in, xo, TPC)
    return nc


def _emit_tr(nc, src, dst, R, C, sb_ident, px, src_off=0):
    NR = R // 128
    NCC = C // 128
    RB = min(NR, 64)
    NPASS = (NR + RB - 1) // RB
    es = contextlib.ExitStack()
    with es:
        sb_a = [es.enter_context(_sb(nc, f"{px}tr_a{i}", [128, RB * 128], BF16)) for i in range(2)]
        sb_o = [es.enter_context(_sb(nc, f"{px}tr_o{i}", [128, RB * 128], BF16)) for i in range(2)]
        ps = [es.enter_context(_pt(nc, f"{px}tr_ps{i}", [128, 1024], BF16)) for i in range(2)]
        s_ld = [es.enter_context(_sm(nc, f"{px}tr_ld{i}")) for i in range(2)]
        s_pe = es.enter_context(_sm(nc, f"{px}tr_pe"))
        s_dv = es.enter_context(_sm(nc, f"{px}tr_dv"))
        s_st = [es.enter_context(_sm(nc, f"{px}tr_st{i}")) for i in range(2)]
        G = [0]

        for cc in range(NCC):
            for p in range(NPASS):
                it = cc * NPASS + p
                par = it % 2
                rb = min(RB, NR - p * RB)
                ng = (rb + 7) // 8
                g0 = G[0]
                with nc.Block(no_gpsimd_drain=True) as block:
                    @block.sync
                    def _(sy, cc=cc, p=p, it=it, par=par, rb=rb, ng=ng, g0=g0):
                        sy.dma_start(
                            out=sb_a[par][:, 0:rb * 128].rearrange("q (r t) -> q r t", t=128),
                            in_=bass.AP(src, src_off + (p * RB) * 128 * C + cc * 128,
                                        [[C, 128], [128 * C, rb], [1, 128]]),
                        ).then_inc(s_ld[par], 16)
                        sy.wait_ge(s_dv, g0 + ng)
                        sy.dma_start(
                            out=bass.AP(dst, cc * 128 * R + p * RB * 128,
                                        [[R, 128], [1, rb * 128]]),
                            in_=sb_o[par][:, 0:rb * 128],
                        ).then_inc(s_st[par], 16)
                        sy.wait_ge(s_st[par], 16 * (it // 2 + 1))

                    @block.tensor
                    def _(t, it=it, par=par, rb=rb, ng=ng, g0=g0):
                        t.wait_ge(s_ld[par], 16 * (it // 2 + 1))
                        for g8 in range(ng):
                            gg = g0 + g8
                            if gg >= 2:
                                t.wait_ge(s_dv, gg - 1)
                            n8 = min(8, rb - g8 * 8)
                            ins = None
                            for j in range(n8):
                                r_ = g8 * 8 + j
                                ins = t.transpose(ps[gg % 2][:, j * 128:(j + 1) * 128],
                                                  sb_a[par][:, r_ * 128:(r_ + 1) * 128],
                                                  sb_ident[:, :])
                            ins.then_inc(s_pe, 1)

                    @block.vector
                    def _(v, it=it, par=par, rb=rb, ng=ng, g0=g0):
                        for g8 in range(ng):
                            gg = g0 + g8
                            n8 = min(8, rb - g8 * 8)
                            v.wait_ge(s_pe, gg + 1)
                            v.tensor_copy(
                                sb_o[par][:, g8 * 8 * 128:(g8 * 8 + n8) * 128]
                                .rearrange("q (r t) -> q r t", t=128),
                                ps[gg % 2][:, 0:n8 * 128].rearrange("q (r t) -> q r t", t=128),
                            ).then_inc(s_dv, 1)
                G[0] += ng



def _emit_wfold(nc, wupT, wgT, wdT, sb_h128, sb_h16, sb_h64, upf, gatef, downf, u1, u1g, u2):
    COLS = 1024
    HCOL = 256
    es = contextlib.ExitStack()
    with es:
        sb_in = [es.enter_context(_sb(nc, f"w_in{i}", [128, 2048], BF16)) for i in range(2)]
        sb_inF = [es.enter_context(_sb(nc, f"w_inF{i}", [128, 1024], F32)) for i in range(2)]
        sb_o = [es.enter_context(_sb(nc, f"w_o{i}", [128, 2048], BF16)) for i in range(2)]
        ps = [es.enter_context(_pt(nc, f"w_ps{i}", [128, 2048], F32)) for i in range(2)]
        s_ld = [es.enter_context(_sm(nc, f"w_ld{i}")) for i in range(2)]
        s_ls = [es.enter_context(_sm(nc, f"w_ls{i}")) for i in range(2)]
        s_cv = es.enter_context(_sm(nc, "w_cv"))
        s_pe = es.enter_context(_sm(nc, "w_pe"))
        s_ac = es.enter_context(_sm(nc, "w_ac"))
        s_st = [es.enter_context(_sm(nc, f"w_st{i}")) for i in range(2)]

        descs1, descs2 = [], []
        for (wt, out_d, mid_d, cols, nA) in [
            (wupT, upf, u1, COLS, 16), (wgT, gatef, u1g, COLS, 16),
            (wdT, downf, u2, HCOL, 64),
        ]:
            hm = sb_h16 if nA == 16 else sb_h64
            nrm = 1.0 / math.sqrt(128.0 * nA)
            BA = max(1, 2048 // cols // 2)
            for a0 in range(0, nA, BA):
                ba = min(BA, nA - a0)
                descs1.append((
                    bass.AP(wt, a0 * 128 * cols, [[cols, 128], [128 * cols, ba], [1, cols]]),
                    bass.AP(mid_d, a0 * 128 * cols, [[cols, 128], [128 * cols, ba], [1, cols]]),
                    128, ba * cols, sb_h128[:, :], nrm))
            BC = max(1, 2048 // cols)
            for c0 in range(0, 128, BC):
                bc_ = min(BC, 128 - c0)
                descs2.append((
                    bass.AP(mid_d, c0 * cols, [[128 * cols, nA], [cols, bc_], [1, cols]]),
                    bass.AP(out_d, c0 * cols, [[128 * cols, nA], [cols, bc_], [1, cols]]),
                    nA, bc_ * cols, hm[0:nA, 0:nA], 1.0))

        def emit_phase(descs, base_it, fp32_in):
            assert base_it % 2 == 0
            s_l = s_ld if fp32_in else s_ls

            def load(eng, idx):
                _, _, rows, cols, _, _ = descs[idx]
                dst = sb_inF if fp32_in else sb_in
                eng.dma_start(out=dst[(base_it + idx) % 2][0:rows, 0:cols],
                              in_=descs[idx][0]).then_inc(s_l[(base_it + idx) % 2], 16)

            with nc.Block(no_gpsimd_drain=True) as block:
                @block.sync
                def _(sy):
                    load(sy, 0)

            for idx in range(len(descs)):
                it = base_it + idx
                par = it % 2
                _, out_ap, rows, cols, lhs_ap, scale = descs[idx]
                with nc.Block(no_gpsimd_drain=True) as block:
                    if fp32_in:
                        @block.vector
                        def _(v, par=par, idx=idx, rows=rows, cols=cols):
                            v.wait_ge(s_l[par], 16 * (idx // 2 + 1))
                            v.tensor_copy(sb_in[par][0:rows, 0:cols],
                                          sb_inF[par][0:rows, 0:cols]).then_inc(s_cv, 1)

                    @block.tensor
                    def _(t, it=it, par=par, idx=idx, rows=rows, cols=cols,
                          lhs_ap=lhs_ap):
                        if fp32_in:
                            t.wait_ge(s_cv, idx + 1)
                        else:
                            t.wait_ge(s_l[par], 16 * (idx // 2 + 1))
                        ins = None
                        for j in range((cols + 511) // 512):
                            w = min(512, cols - j * 512)
                            ins = t.matmul(ps[par][0:rows, j * 512:j * 512 + w], lhs_ap,
                                           sb_in[par][0:rows, j * 512:j * 512 + w],
                                           start=True, stop=True)
                        ins.then_inc(s_pe, 1)

                    @block.scalar
                    def _(s, it=it, par=par, rows=rows, cols=cols, scale=scale):
                        s.wait_ge(s_pe, it + 1)
                        if it >= 2:
                            s.wait_ge(s_st[par], 16 * (it // 2))
                        s.activation(sb_o[par][0:rows, 0:cols], ps[par][0:rows, 0:cols],
                                     AF.Identity, scale=scale).then_inc(s_ac, 1)

                    @block.sync
                    def _(sy, it=it, par=par, idx=idx, out_ap=out_ap, rows=rows,
                          cols=cols):
                        if idx + 1 < len(descs):
                            load(sy, idx + 1)
                        sy.wait_ge(s_ac, it + 1)
                        sy.dma_start(out=out_ap,
                                     in_=sb_o[par][0:rows, 0:cols]).then_inc(s_st[par], 16)

        emit_phase(descs1, 0, fp32_in=True)
        n1 = len(descs1)
        with nc.Block(no_gpsimd_drain=True) as block:
            @block.sync
            def _(sy):
                sy.wait_ge(s_st[0], 16 * ((n1 + 1) // 2))
                sy.wait_ge(s_st[1], 16 * (n1 // 2))
        emit_phase(descs2, n1, fp32_in=False)
        n2 = n1 + len(descs2)
        with nc.Block(no_gpsimd_drain=True) as block:
            @block.sync
            def _(sy):
                sy.wait_ge(s_st[0], 16 * ((n2 + 1) // 2))
                sy.wait_ge(s_st[1], 16 * (n2 // 2))


def build_W():
    nc = bass.Bass("TRN2", target_bir_lowering=False)
    wupT = nc.dram_tensor("wupT", [D, 1024], F32, kind="ExternalInput")
    wgT = nc.dram_tensor("wgT", [D, 1024], F32, kind="ExternalInput")
    wdT = nc.dram_tensor("wdT", [INTER, 256], F32, kind="ExternalInput")
    h128 = nc.dram_tensor("h128", [128, 128], BF16, kind="ExternalInput")
    h16 = nc.dram_tensor("h16", [16, 16], BF16, kind="ExternalInput")
    h64 = nc.dram_tensor("h64", [64, 64], BF16, kind="ExternalInput")
    upf = nc.dram_tensor("upf", [D, 1024], BF16, kind="ExternalOutput")
    gatef = nc.dram_tensor("gatef", [D, 1024], BF16, kind="ExternalOutput")
    downf = nc.dram_tensor("downf", [INTER, 256], BF16, kind="ExternalOutput")
    u1 = nc.dram_tensor("u1", [D, 1024], BF16)
    u1g = nc.dram_tensor("u1g", [D, 1024], BF16)
    u2 = nc.dram_tensor("u2", [INTER, 256], BF16)
    es = contextlib.ExitStack()
    with es:
        sb_h128 = es.enter_context(_sb(nc, "wc_h128", [128, 128], BF16))
        sb_h16 = es.enter_context(_sb(nc, "wc_h16", [16, 16], BF16))
        sb_h64 = es.enter_context(_sb(nc, "wc_h64", [64, 64], BF16))
        s_l0 = es.enter_context(_sm(nc, "wc_l0"))
        with nc.Block(no_gpsimd_drain=True) as block:
            @block.gpsimd
            def _(g_):
                for dst, hsrc in [(sb_h128, h128), (sb_h16, h16), (sb_h64, h64)]:
                    g_.dma_start(out=dst[0:hsrc.shape[0], 0:hsrc.shape[1]],
                                 in_=hsrc.ap()).then_inc(s_l0, 16)
                g_.wait_ge(s_l0, 48)
        _emit_wfold(nc, wupT, wgT, wdT, sb_h128, sb_h16, sb_h64, upf, gatef, downf, u1, u1g, u2)
    return nc



TPC = 2048
SEQ = 8192
NCORE = 8


def _rs128(v):
    return np.ascontiguousarray(_f32(v).reshape(-1, 128).T)


def _build_wqkv_l0(inp, h0):
    blk = np.empty((D, HPC * 384), np.float32)
    qs, qg = inp["L0_attn_Wq_sem"], inp["L0_attn_Wq_geo"]
    ks, kg = inp["L0_attn_Wk_sem"], inp["L0_attn_Wk_geo"]
    wv = inp["L0_attn_Wv"]
    for j in range(HPC):
        h = h0 + j
        blk[:, j * 384 + 0:j * 384 + 64] = qs[h * 64:(h + 1) * 64, :].T
        blk[:, j * 384 + 64:j * 384 + 128] = qg[h * 64:(h + 1) * 64, :].T
        blk[:, j * 384 + 128:j * 384 + 192] = ks[h * 64:(h + 1) * 64, :].T
        blk[:, j * 384 + 192:j * 384 + 256] = kg[h * 64:(h + 1) * 64, :].T
        blk[:, j * 384 + 256:j * 384 + 384] = wv[h * 128:(h + 1) * 128, :].T
    return np.ascontiguousarray(blk)


def _slice_qrm(payf_b, h0, tpc=None, seq=None):
    tpc = tpc or TPC
    seq = seq or SEQ
    nrank = seq // tpc
    out = np.empty((HPC * 64, seq), payf_b.dtype)
    for r in range(nrank):
        r0 = r * PAY1 + QR_BASE + h0 * 64
        out[:, r * tpc:(r + 1) * tpc] = payf_b[r0:r0 + HPC * 64, :]
    return np.ascontiguousarray(out)


def _build_meta(h0):
    m = np.zeros((1, 32), np.int32)
    m[0, 0] = QR_BASE + h0 * 64
    for hloc in range(HPC):
        for r in range(SEQ // TPC):
            m[0, 8 + hloc * 4 + r] = r * PAY1 + QR_BASE + (h0 + hloc) * 64
    return m


def _run(nc, in_maps):
    return run_bass_kernel_spmd(nc, in_maps, core_ids=list(range(NCORE)))


def kernel(**inp):
    C = host_constants(SEQ)
    x = _f32(inp["L0_attn_x"])
    xs = [np.ascontiguousarray(x[c // 4, (c % 4) * TPC:(c % 4 + 1) * TPC, :])
          for c in range(NCORE)]

    ncW = build_W()
    inW = []
    for c in range(NCORE):
        inW.append(dict(
            wupT=np.ascontiguousarray(_f32(inp["L1_ffn_W_up"])[c * 1024:(c + 1) * 1024, :].T),
            wgT=np.ascontiguousarray(_f32(inp["L1_ffn_W_gate"])[c * 1024:(c + 1) * 1024, :].T),
            wdT=np.ascontiguousarray(_f32(inp["L1_ffn_W_down"])[c * 256:(c + 1) * 256, :].T),
            h128=C["h128"], h16=C["h16"], h64=C["h64"]))
    rW = _run(ncW, inW)
    wupf = np.concatenate([np.asarray(rW.results[c]["upf"]) for c in range(NCORE)], axis=1)
    wgf = np.concatenate([np.asarray(rW.results[c]["gatef"]) for c in range(NCORE)], axis=1)
    wdf = np.concatenate([np.asarray(rW.results[c]["downf"]) for c in range(NCORE)], axis=1)

    wsug = _f32(inp["L1_ffn_Wscale_ug"])
    for L in (0, 1):
        ncA = build_A(L, TPC)
        inA = []
        for c in range(NCORE):
            qc = c % 4
            sl = slice(qc * TPC, (qc + 1) * TPC)
            d = dict(x=xs[c], ident=C["ident"], ones_row=C["ones_row"])
            if L == 0:
                d.update(wg=inp["L0_ffn_w_gate"], wu=inp["L0_ffn_w_up"],
                         wd=inp["L0_ffn_w_down"],
                         sg_rs=_rs128(inp["L0_ffn_s_gate"]),
                         su_rs=_rs128(inp["L0_ffn_s_up"]),
                         sd_row=_f32(inp["L0_ffn_s_down"])[None, :])
            else:
                bpre = np.zeros((128, 17), np.float32)
                wdb = _f32(inp["L1_attn_W_down_b"])
                for ocx in range(8):
                    bpre[:, ocx] = wdb[ocx * 128:(ocx + 1) * 128]
                bpre[0:64, 8] = _f32(inp["L1_attn_Wkr_b"])
                wqrb = _f32(inp["L1_attn_Wqr_b"])
                for j in range(8):
                    bpre[:, 9 + j] = wqrb[j * 128:(j + 1) * 128]
                d.update(wg=wgf, wu=wupf, wd=wdf,
                         sg_rs=_rs128(wsug[INTER:]), su_rs=_rs128(wsug[:INTER]),
                         sd_row=_f32(inp["L1_ffn_Wscale_dp"])[None, :],
                         wpre=np.ascontiguousarray(np.hstack([
                             _f32(inp["L1_attn_W_down_k"]),
                             _f32(inp["L1_attn_Wkr_k"]),
                             _f32(inp["L1_attn_Wqr_k"])])),
                         bpre_rs=bpre,
                         krcos=np.ascontiguousarray(C["l1_kr_cos"][:, sl]),
                         krsin=np.ascontiguousarray(C["l1_kr_sin"][:, sl]),
                         qrcos=np.ascontiguousarray(C["l1_qr_cos"][:, sl]),
                         qrsin=np.ascontiguousarray(C["l1_qr_sin"][:, sl]),
                         permT=C["l1_permT"],
                         sudp_rs=_rs128(inp["L1_ffn_SU_dp"]),
                         suug_rs=_rs128(inp["L1_ffn_SU_ug"]))
            inA.append(d)
        rA = _run(ncA, inA)
        pays = [np.asarray(rA.results[c]["pay"]) for c in range(NCORE)]
        gs = [np.asarray(rA.results[c]["g"]) for c in range(NCORE)]
        payf = {b: np.ascontiguousarray(np.concatenate([pays[b * 4 + r] for r in range(4)], axis=0))
                for b in (0, 1)}

        ncB = build_B(L, TPC, SEQ)
        inB = []
        for c in range(NCORE):
            b = c // 4
            h0 = (c % 4) * HPC
            d = dict(payf=payf[b], ident=C["ident"], ones_col=C["ones_col"],
                     ones_row=C["ones_row"], causal=C["causal"],
                     meta=_build_meta(h0))
            if L == 1:
                d["qrm"] = _slice_qrm(payf[b], h0)
            if L == 0:
                d.update(wqkv=_build_wqkv_l0(inp, h0), cmask=C["l0_cmask"],
                         smask=C["l0_smask"], permT=C["l0_permT"])
            else:
                wukv_f = _f32(inp["L1_attn_W_uKV_k"])
                wuq_f = _f32(inp["L1_attn_W_uQ_k"])
                bkv_f = _f32(inp["L1_attn_W_uKV_b"])
                bq_f = _f32(inp["L1_attn_W_uQ_b"])
                wukv = np.empty((512, HPC * 256), np.float32)
                bkv = np.empty((128, HPC * 2), np.float32)
                wuq = np.empty((512, HPC * 128), np.float32)
                bq = np.empty((128, HPC), np.float32)
                for j in range(HPC):
                    h = h0 + j
                    wukv[:, j * 256:j * 256 + 128] = wukv_f[:, h * 128:(h + 1) * 128]
                    wukv[:, j * 256 + 128:j * 256 + 256] = wukv_f[:, 2048 + h * 128:2048 + (h + 1) * 128]
                    bkv[:, 2 * j] = bkv_f[h * 128:(h + 1) * 128]
                    bkv[:, 2 * j + 1] = bkv_f[2048 + h * 128:2048 + (h + 1) * 128]
                    wuq[:, j * 128:(j + 1) * 128] = wuq_f[:, h * 128:(h + 1) * 128]
                    bq[:, j] = bq_f[h * 128:(h + 1) * 128]
                d.update(wukv=np.ascontiguousarray(wukv), wuq=np.ascontiguousarray(wuq),
                         bkv_rs=np.ascontiguousarray(bkv), bq_rs=np.ascontiguousarray(bq))
            inB.append(d)
        rB = _run(ncB, inB)
        att = [np.asarray(rB.results[c]["attnout"]) for c in range(NCORE)]

        ncC = build_C(TPC)
        inC = []
        for c in range(NCORE):
            b = c // 4
            qc = c % 4
            sl = slice(qc * TPC, (qc + 1) * TPC)
            s_in = np.ascontiguousarray(
                np.concatenate([att[b * 4 + j][:, sl] for j in range(4)], axis=0))
            if L == 0:
                wo_cm = np.ascontiguousarray(_f32(inp["L0_attn_Wo"]).T)
                bias_row = np.zeros((1, D), np.float32)
            else:
                wo_cm = _f32(inp["L1_attn_Wout_k"])
                bias_row = _f32(inp["L1_attn_Wout_b"])[None, :]
            inC.append(dict(s_in=s_in, wo_cm=wo_cm, bias_row=bias_row,
                            ones_row=C["ones_row"], g=gs[c], x=xs[c]))
        rC = _run(ncC, inC)
        xs = [np.ascontiguousarray(np.asarray(rC.results[c]["xo"])) for c in range(NCORE)]

    y = np.empty((2, SEQ, D), np.float32)
    for c in range(NCORE):
        y[c // 4, (c % 4) * TPC:(c % 4 + 1) * TPC, :] = xs[c]
    return y



_FUSE_SKIP = set()
_FUSE_OPTS = {}


def build_fused(TPC, SEQ):
    NRANK = SEQ // TPC
    _SEMST[0] = None
    nc = bass.Bass("TRN2", target_bir_lowering=False)
    DT = nc.dram_tensor

    x0 = DT("x", [TPC, D], F32, kind="ExternalInput")
    ident = DT("ident", [128, 128], BF16, kind="ExternalInput")
    ones_col = DT("ones_col", [128, 1], BF16, kind="ExternalInput")
    ones_row = DT("ones_row", [1, 128], F32, kind="ExternalInput")
    causal = DT("causal", [128, 1024], BF16, kind="ExternalInput")
    meta = DT("meta", [1, 8], I32, kind="ExternalInput")
    wupT = DT("wupT", [D, 1024], F32, kind="ExternalInput")
    wgT = DT("wgT", [D, 1024], F32, kind="ExternalInput")
    wdT = DT("wdT", [INTER, 256], F32, kind="ExternalInput")
    h128 = DT("h128", [128, 128], BF16, kind="ExternalInput")
    h16 = DT("h16", [16, 16], BF16, kind="ExternalInput")
    h64 = DT("h64", [64, 64], BF16, kind="ExternalInput")
    wg0 = DT("wg0", [D, INTER], I8, kind="ExternalInput")
    wu0 = DT("wu0", [D, INTER], I8, kind="ExternalInput")
    wd0 = DT("wd0", [INTER, D], I8, kind="ExternalInput")
    sg0 = DT("sg0", [128, IC], F32, kind="ExternalInput")
    su0 = DT("su0", [128, IC], F32, kind="ExternalInput")
    sd0r = DT("sd0r", [1, D], F32, kind="ExternalInput")
    wqkv = DT("wqkv", [D, HPC * 384], F32, kind="ExternalInput")
    cmask = DT("cmask", [128, SEQ], BF16, kind="ExternalInput")
    smask = DT("smask", [128, SEQ], BF16, kind="ExternalInput")
    permT0 = DT("permT0", [128, 128], BF16, kind="ExternalInput")
    wocm0 = DT("wocm0", [D, D], F32, kind="ExternalInput")
    b0r = DT("b0r", [1, D], F32, kind="ExternalInput")
    wpre = DT("wpre", [D, WPRE_COLS], F32, kind="ExternalInput")
    bpre = DT("bpre", [128, 17], F32, kind="ExternalInput")
    krcos = DT("krcos", [64, TPC], BF16, kind="ExternalInput")
    krsin = DT("krsin", [64, TPC], BF16, kind="ExternalInput")
    qrcos = DT("qrcos", [1024, TPC], BF16, kind="ExternalInput")
    qrsin = DT("qrsin", [1024, TPC], BF16, kind="ExternalInput")
    permT1 = DT("permT1", [128, 128], BF16, kind="ExternalInput")
    sudp = DT("sudp", [128, IC], F32, kind="ExternalInput")
    suug = DT("suug", [128, DC], F32, kind="ExternalInput")
    sg1 = DT("sg1", [128, IC], F32, kind="ExternalInput")
    su1 = DT("su1", [128, IC], F32, kind="ExternalInput")
    sd1r = DT("sd1r", [1, D], F32, kind="ExternalInput")
    wukv = DT("wukv", [512, HPC * 256], F32, kind="ExternalInput")
    wuq = DT("wuq", [512, HPC * 128], F32, kind="ExternalInput")
    bkv = DT("bkv", [128, HPC * 2], F32, kind="ExternalInput")
    bq = DT("bq", [128, HPC], F32, kind="ExternalInput")
    wocm1 = DT("wocm1", [D, D], F32, kind="ExternalInput")
    b1r = DT("b1r", [1, D], F32, kind="ExternalInput")

    y = DT("y", [TPC, D], F32, kind="ExternalOutput")

    fuse_d = DT("fuse", [INTER, TPC], BF16)
    gbuf = DT("gbuf", [TPC, D], F32)
    x1 = DT("x1", [TPC, D], F32)
    pay0_in = DT("pay0_in", [D, TPC], BF16)
    pay0_g = DT("pay0_g", [NRANK * D, TPC], BF16)
    pay1_in = DT("pay1_in", [PAY1, TPC], BF16)
    pay1_g = DT("pay1_g", [NRANK * PAY1, TPC], BF16)
    qrt_in = DT("qrt_in", [TPC, 1024], BF16)
    qrt_g = DT("qrt_g", [SEQ, 1024], BF16)
    qrh = DT("qrh", [SEQ, 256], BF16)
    qrm_scr = DT("qrm_scr", [256, SEQ], BF16)
    att_in = DT("att_in", [HPC * 128, SEQ], BF16)
    s_scr = DT("s_scr", [D, TPC], BF16)
    att_g = [DT(f"att_g{i}", [NRANK * HPC * 128, SEQ], BF16) for i in range(2)]
    qd = DT("qd", [HPC * 128, SEQ], BF16)
    kd = DT("kd", [HPC * 128, SEQ], BF16)
    vd = DT("vd", [HPC * SEQ, 128], BF16)
    wfu_in = DT("wfu_in", [D, 1024], BF16)
    wfg_in = DT("wfg_in", [D, 1024], BF16)
    wfd_in = DT("wfd_in", [INTER, 256], BF16)
    wfu_g = DT("wfu_g", [8 * D, 1024], BF16)
    wfg_g = DT("wfg_g", [8 * D, 1024], BF16)
    wfd_g = DT("wfd_g", [8 * INTER, 256], BF16)
    wu1f = DT("wu1f", [D, INTER], BF16)
    wg1f = DT("wg1f", [D, INTER], BF16)
    wd1f = DT("wd1f", [INTER, D], BF16)
    u1s = DT("u1s", [D, 1024], BF16)
    u1gs = DT("u1gs", [D, 1024], BF16)
    u2s = DT("u2s", [INTER, 256], BF16)

    GROUPS = [[0, 1, 2, 3], [4, 5, 6, 7]]
    ALLG = [[0, 1, 2, 3, 4, 5, 6, 7]]

    es = contextlib.ExitStack()
    with es:
        sb_ident = es.enter_context(_sb(nc, "g_ident", [128, 128], BF16))
        sb_ones = es.enter_context(_sb(nc, "g_ones", [128, 1], BF16))
        sb_or = es.enter_context(_sb(nc, "g_or", [1, 128], F32))
        sb_caus = es.enter_context(_sb(nc, "g_caus", [128, 1024], BF16))
        sb_meta = es.enter_context(_sb(nc, "g_meta", [1, 8], I32))
        sb_h128 = es.enter_context(_sb(nc, "g_h128", [128, 128], BF16))
        sb_h16 = es.enter_context(_sb(nc, "g_h16", [16, 16], BF16))
        sb_h64 = es.enter_context(_sb(nc, "g_h64", [64, 64], BF16))
        s_g0 = es.enter_context(_sm(nc, "g_l0"))
        cc_w = [es.enter_context(_sm(nc, f"cc_w{i}")) for i in range(3)]
        s_wcp = es.enter_context(_sm(nc, "s_wcp"))
        cc_pay = [es.enter_context(_sm(nc, f"cc_pay{i}")) for i in range(2)]
        cc_att = [es.enter_context(_sm(nc, f"cc_att{i}")) for i in range(2)]
        cc_qrt = es.enter_context(_sm(nc, "cc_qrt"))
        s_qrh = es.enter_context(_sm(nc, "s_qrh"))

        with nc.Block(no_gpsimd_drain=True) as block:
            @block.gpsimd
            def _(g_):
                n = 0
                for dst, srt in [(sb_ident, ident), (sb_ones, ones_col), (sb_or, ones_row),
                                 (sb_caus, causal), (sb_meta, meta), (sb_h128, h128),
                                 (sb_h16, h16), (sb_h64, h64)]:
                    g_.dma_start(out=dst[0:srt.shape[0], 0:srt.shape[1]],
                                 in_=srt.ap()).then_inc(s_g0, 16)
                    n += 16
                g_.wait_ge(s_g0, n)

        s_fls = es.enter_context(nc.semaphore("s_fls"))
        s_flg = es.enter_context(nc.semaphore("s_flg"))
        sb_fence = es.enter_context(_sb(nc, "g_fence", [1, 64], BF16))
        assert s_flg.num <= 169, s_flg.num
        _SEMST[0] = _FenceState(s_flg.num + 1, s_fls, s_flg)

        if "wfold" not in _FUSE_SKIP:
            _PX[0] = "w_"
            _emit_wfold(nc, wupT, wgT, wdT, sb_h128, sb_h16, sb_h64,
                        wfu_in, wfg_in, wfd_in, u1s, u1gs, u2s)
            _PX[0] = ""
        with nc.Block(no_gpsimd_drain=True) as block:
            @block.gpsimd
            def _(g_):
                for i, (src, dstg) in enumerate([(wfu_in, wfu_g), (wfg_in, wfg_g),
                                                 (wfd_in, wfd_g)]):
                    if "wag" in _FUSE_SKIP:
                        g_.sem_inc(cc_w[i], 1)
                        continue
                    g_.collective_compute("AllGather", ALU.bypass, replica_groups=ALLG,
                                          ins=[src.ap()], outs=[dstg.ap()]).then_inc(cc_w[i])

        def wcp_copies(eng):
            for i, (srcg, dstf, rows, cols) in enumerate([
                    (wfu_g, wu1f, D, 1024), (wfg_g, wg1f, D, 1024),
                    (wfd_g, wd1f, INTER, 256)]):
                eng.wait_ge(cc_w[i], 1)
                nq = max(4, (rows * 8 + 16383) // 16384 + 1)
                while rows % nq:
                    nq += 1
                for q in range(nq):
                    r0 = q * (rows // nq)
                    eng.dma_start(
                        out=bass.AP(dstf, r0 * 8 * cols,
                                    [[8 * cols, rows // nq], [cols, 8], [1, cols]]),
                        in_=bass.AP(srcg, r0 * cols,
                                    [[cols, rows // nq], [rows * cols, 8], [1, cols]]),
                    ).then_inc(s_wcp, 16)

        layers = [Lx for Lx in (0, 1) if f"layer{Lx}" not in _FUSE_SKIP]
        for li, L in enumerate(layers):
            if li > 0:
                _sem_fence(nc, ident.ap(), sb_fence)
            px = f"l{L}"
            x_ap = (x0 if L == 0 else x1).ap()
            xo_ap = (x1 if L == 0 else y).ap()
            payg = pay0_g if L == 0 else pay1_g

            with contextlib.ExitStack() as el:
                sb_sg = el.enter_context(_sb(nc, px + "c_sg", [128, IC], F32))
                sb_su = el.enter_context(_sb(nc, px + "c_su", [128, IC], F32))
                sb_sdb = el.enter_context(_sb(nc, px + "c_sdb", [128, D], F32))
                sb_sdr = el.enter_context(_sb(nc, px + "c_sdr", [1, D], F32))
                if L == 1:
                    sb_perm = el.enter_context(_sb(nc, px + "c_perm", [128, 128], BF16))
                    sb_bpre = el.enter_context(_sb(nc, px + "c_bpre", [128, 17], F32))
                    sb_sudp = el.enter_context(_sb(nc, px + "c_sudp", [128, IC], F32))
                    sb_suug = el.enter_context(_sb(nc, px + "c_suug", [128, DC], F32))
                s_lc = el.enter_context(_sm(nc, px + "c_ld"))
                s_bc = el.enter_context(_sm(nc, px + "c_bc"))

                with nc.Block(no_gpsimd_drain=True) as block:
                    @block.gpsimd
                    def _(g_, L=L):
                        n = 0
                        pairs = ([(sb_sg, sg0), (sb_su, su0), (sb_sdr, sd0r)] if L == 0 else
                                 [(sb_sg, sg1), (sb_su, su1), (sb_sdr, sd1r),
                                  (sb_perm, permT1), (sb_bpre, bpre),
                                  (sb_sudp, sudp), (sb_suug, suug)])
                        for dst, srt in pairs:
                            g_.dma_start(out=dst[0:srt.shape[0], 0:srt.shape[1]],
                                         in_=srt.ap()).then_inc(s_lc, 16)
                            n += 16
                        if L == 1 and "wcp" in _FUSE_SKIP:
                            for _i in range(16):
                                g_.sem_inc(s_wcp, 16)
                        elif L == 1 and 0 not in layers:
                            wcp_copies(g_)
                        g_.wait_ge(s_lc, n)

                with contextlib.ExitStack() as eb:
                    ps_b = eb.enter_context(_pt(nc, px + "c_psb", [128, 512], F32))
                    with nc.Block(no_gpsimd_drain=True) as block:
                        @block.tensor
                        def _(t):
                            for j in range(D // 512):
                                if j >= 1:
                                    t.wait_ge(s_bc, 2 * j)
                                t.matmul(ps_b[:, :], sb_or[:, :],
                                         sb_sdr[:, j * 512:(j + 1) * 512],
                                         start=True, stop=True).then_inc(s_bc, 1)

                        @block.scalar
                        def _(s):
                            for j in range(D // 512):
                                s.wait_ge(s_bc, 2 * j + 1)
                                s.activation(sb_sdb[:, j * 512:(j + 1) * 512], ps_b[:, :],
                                             AF.Identity).then_inc(s_bc, 1)

                with contextlib.ExitStack() as ex:
                    xn = ex.enter_context(_sb(nc, px + "xn", [128, DC * TPC], BF16))
                    if f"ln{L}" not in _FUSE_SKIP:
                        _PX[0] = px + "ln_"
                        _emit_ln(nc, x_ap, xn, pay0_in.ap() if L == 0 else None, sb_ident, TPC)
                    if L == 1 and "pp1" not in _FUSE_SKIP:
                        _PX[0] = px + "pp_"
                        _emit_preproj(nc, xn, pay1_in.ap(), wpre.ap(), sb_bpre, sb_perm,
                                      krcos.ap(), krsin.ap(), qrcos.ap(), qrsin.ap(), TPC)
                        _PX[0] = px + "qt_"
                        _emit_tr(nc, pay1_in, qrt_in, 1024, TPC, sb_ident, px + "qt_",
                                 src_off=QR_BASE * TPC)
                    _PX[0] = ""
                    with nc.Block(no_gpsimd_drain=True) as block:
                        @block.gpsimd
                        def _(g_, L=L):
                            pin = pay0_in if L == 0 else pay1_in
                            g_.collective_compute("AllGather", ALU.bypass,
                                                  replica_groups=GROUPS,
                                                  ins=[pin.ap()], outs=[payg.ap()],
                                                  ).then_inc(cc_pay[L])
                            if L == 1:
                                if "payag" in _FUSE_SKIP:
                                    g_.sem_inc(cc_qrt, 1)
                                else:
                                    g_.collective_compute("AllGather", ALU.bypass,
                                                          replica_groups=GROUPS,
                                                          ins=[qrt_in.ap()], outs=[qrt_g.ap()],
                                                          ).then_inc(cc_qrt)
                    if L == 1 and "sx1" not in _FUSE_SKIP:
                        _PX[0] = px + "sx_"
                        _emit_scale_xn(nc, xn, sb_suug, TPC)
                    if L == 1:
                        with nc.Block(no_gpsimd_drain=True) as block:
                            @block.gpsimd
                            def _(g_):
                                g_.wait_ge(s_wcp, 16 * 16)
                    if f"ffn{L}" not in _FUSE_SKIP:
                        _PX[0] = px + "fg_"
                        _emit_ffn_upgate(nc, L, xn, (wg0 if L == 0 else wg1f).ap(),
                                         (wu0 if L == 0 else wu1f).ap(), sb_sg, sb_su,
                                         sb_sudp if L == 1 else None, fuse_d.ap(), TPC)
                if f"ffn{L}" not in _FUSE_SKIP:
                    _PX[0] = px + "fd_"
                    _emit_ffn_down(nc, L, (wd0 if L == 0 else wd1f).ap(), sb_sdb,
                                   fuse_d.ap(), gbuf.ap(), TPC)
                _PX[0] = ""

                with nc.Block(no_gpsimd_drain=True) as block:
                    @block.gpsimd
                    def _(g_, L=L):
                        g_.wait_ge(cc_pay[L], 1)
                        if L == 1 and "qrh" not in _FUSE_SKIP:
                            g_.wait_ge(cc_qrt, 1)
                            with g_.register("rq") as rq:
                                g_.reg_load(rq, sb_meta[0:1, 1:2])
                                val = g_.snap(rq, min_val=0, max_val=1024 - 256)
                                g_.dma_start(out=qrh.ap(),
                                             in_=qrt_g.ap()[:, bass.ds(val, 256)]).then_inc(s_qrh, 16)
                            g_.wait_ge(s_qrh, 16)
                if L == 1 and "qu" not in _FUSE_SKIP:
                    _emit_tr(nc, qrh, qrm_scr, SEQ, 256, sb_ident, px + "qu_")

                with contextlib.ExitStack() as ea:
                    if L == 0:
                        sb_w = ea.enter_context(_sb(nc, px + "a_w", [128, DC * HPC * 384], BF16))
                        sb_p0 = ea.enter_context(_sb(nc, px + "a_p0", [128, 128], BF16))
                        s_al = ea.enter_context(_sm(nc, px + "a_ld"))
                        with nc.Block(no_gpsimd_drain=True) as block:
                            @block.gpsimd
                            def _(g_):
                                g_.dma_start(out=sb_w[:, :].rearrange("p (c n) -> p c n", c=DC),
                                             in_=wqkv.ap().rearrange("(c p) n -> p c n", p=128),
                                             ).then_inc(s_al, 16)
                                g_.dma_start(out=sb_p0[:, :], in_=permT0.ap()).then_inc(s_al, 16)
                                g_.wait_ge(s_al, 32)
                        _PX[0] = px + "pj_"
                        _emit_proj_l0(nc, pay0_g, sb_w, sb_p0, sb_ident,
                                      cmask.ap(), smask.ap(), qd, kd, vd, TPC, SEQ)
                        if 1 in layers and "wcp" not in _FUSE_SKIP:
                            with nc.Block(no_gpsimd_drain=True) as blkw:
                                @blkw.scalar
                                def _(sc):
                                    wcp_copies(sc)
                    else:
                        sb_w = ea.enter_context(_sb(nc, px + "a_w", [128, 4 * HPC * 256], BF16))
                        sb_wq = ea.enter_context(_sb(nc, px + "a_wq", [128, 4 * HPC * 128], BF16))
                        sb_bkv = ea.enter_context(_sb(nc, px + "a_bkv", [128, HPC * 2], F32))
                        sb_bq = ea.enter_context(_sb(nc, px + "a_bq", [128, HPC], F32))
                        s_al = ea.enter_context(_sm(nc, px + "a_ld"))
                        with nc.Block(no_gpsimd_drain=True) as block:
                            @block.gpsimd
                            def _(g_):
                                g_.dma_start(out=sb_w[:, :].rearrange("p (c n) -> p c n", c=4),
                                             in_=wukv.ap().rearrange("(c p) n -> p c n", p=128),
                                             ).then_inc(s_al, 16)
                                g_.dma_start(out=sb_wq[:, :].rearrange("p (c n) -> p c n", c=4),
                                             in_=wuq.ap().rearrange("(c p) n -> p c n", p=128),
                                             ).then_inc(s_al, 16)
                                g_.dma_start(out=sb_bkv[:, :], in_=bkv.ap()).then_inc(s_al, 16)
                                g_.dma_start(out=sb_bq[:, :], in_=bq.ap()).then_inc(s_al, 16)
                                g_.wait_ge(s_al, 64)
                        if "pj1" not in _FUSE_SKIP:
                            _PX[0] = px + "pj_"
                            _emit_proj_l1(nc, pay1_g, sb_w, sb_wq, sb_bkv, sb_bq, sb_ident,
                                          qd, kd, vd, qrm_scr, TPC, SEQ)
                    if f"sdpa{L}" not in _FUSE_SKIP:
                        _PX[0] = px + "sd_"
                        _emit_sdpa(nc, L, qd, kd, vd, qrm_scr if L == 1 else None,
                                   pay1_g if L == 1 else pay0_g, meta,
                                   sb_ones, sb_or, sb_caus, att_in.ap(), TPC, SEQ)
                    _PX[0] = ""

                with nc.Block(no_gpsimd_drain=True) as block:
                    @block.gpsimd
                    def _(g_, L=L):
                        if "attag" in _FUSE_SKIP:
                            g_.sem_inc(cc_att[L], 1)
                        else:
                            g_.collective_compute("AllGather", ALU.bypass, replica_groups=GROUPS,
                                                  ins=[att_in.ap()], outs=[att_g[L].ap()],
                                                  ).then_inc(cc_att[L])

                def sin_dma(g_, sb_s, sem, L=L):
                    g_.wait_ge(cc_att[L], 1)
                    with g_.register(f"rt{L}") as rt:
                        g_.reg_load(rt, sb_meta[0:1, 0:1])
                        val = g_.snap(rt, min_val=0, max_val=SEQ - TPC)
                        g_.dma_start(out=s_scr.ap(),
                                     in_=att_g[L].ap()[:, bass.ds(val, TPC)]).then_inc(sem, 16)
                    g_.wait_ge(sem, 16)
                    g_.dma_start(
                        out=sb_s[:, :].rearrange("p (c t) -> p c t", c=DC),
                        in_=s_scr.ap().rearrange("(c p) t -> p c t", p=128),
                    ).then_inc(sem, 16)
                    return 32

                _PX[0] = px + "po_"
                if f"post{L}" not in _FUSE_SKIP:
                    _emit_post(nc, sin_dma, (wocm0 if L == 0 else wocm1).ap(),
                               (b0r if L == 0 else b1r).ap(), sb_or,
                               gbuf.ap(), x_ap, xo_ap, TPC)
                _PX[0] = ""

    _SEMST[0] = None
    return nc


def _dyn3(handle, seq, dc, tpc, val):
    ap = bass.AP(handle, 0, [[seq, 128], [128 * seq, dc], [1, seq]])
    return ap[:, :, bass.ds(val, tpc)]


FUSED = True


def kernel_fused(**inp):
    C = host_constants(SEQ)
    x = _f32(inp["L0_attn_x"])
    wsug = _f32(inp["L1_ffn_Wscale_ug"])
    nc = build_fused(TPC, SEQ)
    in_maps = []
    bpre = np.zeros((128, 17), np.float32)
    wdb = _f32(inp["L1_attn_W_down_b"])
    for ocx in range(8):
        bpre[:, ocx] = wdb[ocx * 128:(ocx + 1) * 128]
    bpre[0:64, 8] = _f32(inp["L1_attn_Wkr_b"])
    wqrb = _f32(inp["L1_attn_Wqr_b"])
    for j in range(8):
        bpre[:, 9 + j] = wqrb[j * 128:(j + 1) * 128]
    wpre_h = np.ascontiguousarray(np.hstack([
        _f32(inp["L1_attn_W_down_k"]), _f32(inp["L1_attn_Wkr_k"]),
        _f32(inp["L1_attn_Wqr_k"])]))
    wukv_f, wuq_f = _f32(inp["L1_attn_W_uKV_k"]), _f32(inp["L1_attn_W_uQ_k"])
    bkv_f, bq_f = _f32(inp["L1_attn_W_uKV_b"]), _f32(inp["L1_attn_W_uQ_b"])

    for c in range(NCORE):
        qc = c % 4
        h0 = qc * HPC
        sl = slice(qc * TPC, (qc + 1) * TPC)
        wukv = np.empty((512, HPC * 256), np.float32)
        bkv = np.empty((128, HPC * 2), np.float32)
        wuq = np.empty((512, HPC * 128), np.float32)
        bq = np.empty((128, HPC), np.float32)
        for j in range(HPC):
            h = h0 + j
            wukv[:, j * 256:j * 256 + 128] = wukv_f[:, h * 128:(h + 1) * 128]
            wukv[:, j * 256 + 128:j * 256 + 256] = wukv_f[:, 2048 + h * 128:2048 + (h + 1) * 128]
            bkv[:, 2 * j] = bkv_f[h * 128:(h + 1) * 128]
            bkv[:, 2 * j + 1] = bkv_f[2048 + h * 128:2048 + (h + 1) * 128]
            wuq[:, j * 128:(j + 1) * 128] = wuq_f[:, h * 128:(h + 1) * 128]
            bq[:, j] = bq_f[h * 128:(h + 1) * 128]
        m = np.zeros((1, 8), np.int32)
        m[0, 0] = qc * TPC
        m[0, 1] = h0 * 64
        d = dict(
            x=np.ascontiguousarray(x[c // 4, sl, :]),
            ident=C["ident"], ones_col=C["ones_col"], ones_row=C["ones_row"],
            causal=C["causal"], meta=m,
            wupT=np.ascontiguousarray(_f32(inp["L1_ffn_W_up"])[c * 1024:(c + 1) * 1024, :].T),
            wgT=np.ascontiguousarray(_f32(inp["L1_ffn_W_gate"])[c * 1024:(c + 1) * 1024, :].T),
            wdT=np.ascontiguousarray(_f32(inp["L1_ffn_W_down"])[c * 256:(c + 1) * 256, :].T),
            h128=C["h128"], h16=C["h16"], h64=C["h64"],
            wg0=inp["L0_ffn_w_gate"], wu0=inp["L0_ffn_w_up"], wd0=inp["L0_ffn_w_down"],
            sg0=_rs128(inp["L0_ffn_s_gate"]), su0=_rs128(inp["L0_ffn_s_up"]),
            sd0r=_f32(inp["L0_ffn_s_down"])[None, :],
            wqkv=_build_wqkv_l0(inp, h0),
            cmask=C["l0_cmask"], smask=C["l0_smask"], permT0=C["l0_permT"],
            wocm0=np.ascontiguousarray(_f32(inp["L0_attn_Wo"]).T),
            b0r=np.zeros((1, D), np.float32),
            wpre=wpre_h, bpre=bpre,
            krcos=np.ascontiguousarray(C["l1_kr_cos"][:, sl]),
            krsin=np.ascontiguousarray(C["l1_kr_sin"][:, sl]),
            qrcos=np.ascontiguousarray(C["l1_qr_cos"][:, sl]),
            qrsin=np.ascontiguousarray(C["l1_qr_sin"][:, sl]),
            permT1=C["l1_permT"],
            sudp=_rs128(inp["L1_ffn_SU_dp"]), suug=_rs128(inp["L1_ffn_SU_ug"]),
            sg1=_rs128(wsug[INTER:]), su1=_rs128(wsug[:INTER]),
            sd1r=_f32(inp["L1_ffn_Wscale_dp"])[None, :],
            wukv=np.ascontiguousarray(wukv), wuq=np.ascontiguousarray(wuq),
            bkv_rs=np.ascontiguousarray(bkv), bq_rs=np.ascontiguousarray(bq),
            wocm1=_f32(inp["L1_attn_Wout_k"]),
            b1r=_f32(inp["L1_attn_Wout_b"])[None, :],
        )
        d["bkv"] = d.pop("bkv_rs")
        d["bq"] = d.pop("bq_rs")
        in_maps.append(d)
    res = _run(nc, in_maps)
    yout = np.empty((2, SEQ, D), np.float32)
    for c in range(NCORE):
        yout[c // 4, (c % 4) * TPC:(c % 4 + 1) * TPC, :] = np.asarray(res.results[c]["y"])
    return yout


_kernel_unfused = kernel


def kernel(**inp):
    if FUSED:
        return kernel_fused(**inp)
    return _kernel_unfused(**inp)


# version 14 (final)
# speedup vs baseline: 1.1367x; 1.0153x over previous
import contextlib
import math

import numpy as np
import ml_dtypes

import concourse.bass as bass
import concourse.mybir as mybir
from concourse.bass_utils import run_bass_kernel_spmd

F32 = mybir.dt.float32
BF16 = mybir.dt.bfloat16
I8 = mybir.dt.int8
I32 = mybir.dt.int32
AF = mybir.ActivationFunctionType
ALU = mybir.AluOpType
AX = mybir.AxisListType
BF = ml_dtypes.bfloat16

D = 2048
INTER = 8192
DC = D // 128
IC = INTER // 128
HPC = 4
EPS = 1e-5
ISQ = 1.0 / math.sqrt(128.0)
PAY1 = 2176
QR_BASE = 1152
WPRE_COLS = 2112


def _had(n):
    h = np.array([[1.0]])
    while h.shape[0] < n:
        h = np.block([[h, h], [h, -h]])
    return h


def _bf(x):
    return np.ascontiguousarray(np.asarray(x)).astype(BF)


def _f32(x):
    return np.ascontiguousarray(np.asarray(x), dtype=np.float32)


def host_constants(SEQ):
    c = {}
    c["ident"] = _bf(np.eye(128))
    c["ones_col"] = _bf(np.ones((128, 1)))
    c["ones_row"] = _f32(np.ones((1, 128)))
    i = np.arange(128)[:, None]
    u = np.arange(1024)[None, :]
    c["causal"] = _bf((i <= (u - 512)).astype(np.float64))

    inv = 1.0 / (10000.0 ** (np.arange(0, 64, 2) / 64.0))
    fr = np.outer(np.arange(SEQ), inv)
    cosf, sinf = np.cos(fr).T, np.sin(fr).T
    cm = np.ones((128, SEQ))
    sm = np.zeros((128, SEQ))
    cm[64:96], cm[96:128] = cosf, cosf
    sm[64:96], sm[96:128] = -sinf, sinf
    c["l0_cmask"], c["l0_smask"] = _bf(cm), _bf(sm)
    p0 = np.zeros((128, 128))
    for j in range(32):
        p0[64 + j, 96 + j] = 1.0
        p0[96 + j, 64 + j] = 1.0
    c["l0_permT"] = _bf(p0.T)

    def r1(d):
        freq = np.arange(SEQ)[:, None] + 1.0
        pos = np.repeat(np.arange(d // 2)[:, None], 2, axis=-1).reshape(1, -1)
        theta = np.exp(-2.0 * pos / d * np.log(10000.0))
        ang = freq * theta
        return np.cos(ang).T, np.sin(ang).T

    c["l1_kr_cos"], c["l1_kr_sin"] = map(_bf, r1(64))
    c["l1_qr_cos"], c["l1_qr_sin"] = map(_bf, r1(1024))
    p1 = np.zeros((128, 128))
    for j in range(64):
        p1[2 * j, 2 * j + 1] = -1.0
        p1[2 * j + 1, 2 * j] = 1.0
    c["l1_permT"] = _bf(p1.T)

    c["h128"] = _bf(_had(128))
    c["h16"] = _bf(_had(16))
    c["h64"] = _bf(_had(64))
    return c


_PX = [""]


def _sb(nc, name, shape, dt):
    return nc.sbuf_tensor(_PX[0] + name, shape, dt)


def _pt(nc, name, shape, dt):
    return nc.psum_tensor(_PX[0] + name, shape, dt)


_SEMST = [None]


class _FenceState:
    def __init__(self, fstart, fl_sync, fl_gp):
        import collections
        self.fstart = fstart
        self.clean = collections.deque(range(fstart, 256))
        self.dirty = set()
        self.fl_sync = fl_sync
        self.fl_gp = fl_gp
        self.nf = 0


@contextlib.contextmanager
def _sm(nc, name):
    st = _SEMST[0]
    if st is None:
        with nc.semaphore(_PX[0] + name) as h:
            yield h
        return
    assert st.clean, f"sem pool exhausted at {_PX[0] + name}: add a _sem_fence"
    num = st.clean.popleft()
    h = nc.alloc_semaphore(name=_PX[0] + name, num=num)
    try:
        yield h
    finally:
        nc.release_semaphore(h)
        st.dirty.add(num)


def _sem_fence(nc, src_ap, sb_f):
    st = _SEMST[0]
    st.nf += 1
    k = st.nf
    with nc.Block(no_gpsimd_drain=True) as blk:
        @blk.sync
        def _(sy):
            sy.dma_start(out=sb_f[0:1, 0:16], in_=src_ap[0:1, 0:16]).then_inc(st.fl_sync, 16)
            sy.wait_ge(st.fl_sync, 16 * k)

        @blk.gpsimd
        def _(g_):
            g_.dma_start(out=sb_f[0:1, 16:32], in_=src_ap[0:1, 16:32]).then_inc(st.fl_gp, 16)
            g_.wait_ge(st.fl_gp, 16 * k)
    with nc.Block(no_gpsimd_drain=True) as blk:
        @blk.gpsimd
        def _(g_):
            g_.dma_reset(range(st.fstart, 256))
            g_.sem_clear(range(st.fstart, 256))
    st.clean.extend(sorted(st.dirty))
    st.dirty.clear()



def build_A(L, TPC):
    TG = TPC // 128
    NT = TPC // 512
    DPAY = D if L == 0 else PAY1
    nc = bass.Bass("TRN2", target_bir_lowering=False)

    x = nc.dram_tensor("x", [TPC, D], F32, kind="ExternalInput").ap()
    ident = nc.dram_tensor("ident", [128, 128], BF16, kind="ExternalInput").ap()
    wdt = I8 if L == 0 else BF16
    wg = nc.dram_tensor("wg", [D, INTER], wdt, kind="ExternalInput").ap()
    wu = nc.dram_tensor("wu", [D, INTER], wdt, kind="ExternalInput").ap()
    wd = nc.dram_tensor("wd", [INTER, D], wdt, kind="ExternalInput").ap()
    sg_rs = nc.dram_tensor("sg_rs", [128, IC], F32, kind="ExternalInput").ap()
    su_rs = nc.dram_tensor("su_rs", [128, IC], F32, kind="ExternalInput").ap()
    sd_row = nc.dram_tensor("sd_row", [1, D], F32, kind="ExternalInput").ap()
    ones_row = nc.dram_tensor("ones_row", [1, 128], F32, kind="ExternalInput").ap()
    if L == 1:
        wpre = nc.dram_tensor("wpre", [D, WPRE_COLS], F32, kind="ExternalInput").ap()
        bpre_rs = nc.dram_tensor("bpre_rs", [128, 17], F32, kind="ExternalInput").ap()
        krcos = nc.dram_tensor("krcos", [64, TPC], BF16, kind="ExternalInput").ap()
        krsin = nc.dram_tensor("krsin", [64, TPC], BF16, kind="ExternalInput").ap()
        qrcos = nc.dram_tensor("qrcos", [1024, TPC], BF16, kind="ExternalInput").ap()
        qrsin = nc.dram_tensor("qrsin", [1024, TPC], BF16, kind="ExternalInput").ap()
        permT = nc.dram_tensor("permT", [128, 128], BF16, kind="ExternalInput").ap()
        sudp_rs = nc.dram_tensor("sudp_rs", [128, IC], F32, kind="ExternalInput").ap()
        suug_rs = nc.dram_tensor("suug_rs", [128, DC], F32, kind="ExternalInput").ap()

    pay = nc.dram_tensor("pay", [DPAY, TPC], BF16, kind="ExternalOutput").ap()
    g_out = nc.dram_tensor("g", [TPC, D], F32, kind="ExternalOutput").ap()
    fuse_d = nc.dram_tensor("fuse", [INTER, TPC], BF16).ap()

    es = contextlib.ExitStack()
    with es:
        sb_ident = es.enter_context(_sb(nc, "c_ident", [128, 128], BF16))
        sb_sg = es.enter_context(_sb(nc, "c_sg", [128, IC], F32))
        sb_su = es.enter_context(_sb(nc, "c_su", [128, IC], F32))
        sb_sdb = es.enter_context(_sb(nc, "c_sdb", [128, D], F32))
        sb_or = es.enter_context(_sb(nc, "c_or", [1, 128], F32))
        sb_sdr = es.enter_context(_sb(nc, "c_sdr", [1, D], F32))
        if L == 1:
            sb_perm = es.enter_context(_sb(nc, "c_perm", [128, 128], BF16))
            sb_bpre = es.enter_context(_sb(nc, "c_bpre", [128, 17], F32))
            sb_sudp = es.enter_context(_sb(nc, "c_sudp", [128, IC], F32))
            sb_suug = es.enter_context(_sb(nc, "c_suug", [128, DC], F32))

        ld0 = es.enter_context(_sm(nc, "ld0"))
        c0 = es.enter_context(_sm(nc, "c0"))

        with nc.Block(no_gpsimd_drain=True) as block:
            @block.gpsimd
            def _(g_):
                n = 0
                pairs = [(sb_ident, ident), (sb_sg, sg_rs), (sb_su, su_rs),
                         (sb_or, ones_row), (sb_sdr, sd_row)]
                if L == 1:
                    pairs += [(sb_perm, permT), (sb_bpre, bpre_rs),
                              (sb_sudp, sudp_rs), (sb_suug, suug_rs)]
                for dst, src in pairs:
                    g_.dma_start(out=dst[:, :], in_=src).then_inc(ld0, 16)
                    n += 16
                g_.wait_ge(ld0, n)

        with contextlib.ExitStack() as bces:
            ps_bc0 = bces.enter_context(_pt(nc, "ps_bc0", [128, 512], F32))
            with nc.Block(no_gpsimd_drain=True) as block:
                @block.tensor
                def _(t):
                    for j in range(D // 512):
                        if j >= 1:
                            t.wait_ge(c0, 2 * j)
                        t.matmul(ps_bc0[:, :], sb_or[:, :], sb_sdr[:, j * 512:(j + 1) * 512],
                                 start=True, stop=True).then_inc(c0, 1)

                @block.scalar
                def _(s):
                    for j in range(D // 512):
                        s.wait_ge(c0, 2 * j + 1)
                        s.activation(sb_sdb[:, j * 512:(j + 1) * 512], ps_bc0[:, :],
                                     AF.Identity).then_inc(c0, 1)

        with contextlib.ExitStack() as exn:
            xn = exn.enter_context(_sb(nc, "xn", [128, DC * TPC], BF16))
            _emit_ln(nc, x, xn, pay if L == 0 else None, sb_ident, TPC)

            if L == 1:
                _emit_preproj(nc, xn, pay, wpre, sb_bpre, sb_perm,
                              krcos, krsin, qrcos, qrsin, TPC)
                _emit_scale_xn(nc, xn, sb_suug, TPC)

            _emit_ffn_upgate(nc, L, xn, wg, wu, sb_sg, sb_su,
                             sb_sudp if L == 1 else None, fuse_d, TPC)
        _emit_ffn_down(nc, L, wd, sb_sdb, fuse_d, g_out, TPC)

    return nc


def _emit_ln(nc, x, xn, pay_ap, sb_ident, TPC):
    TG = TPC // 128
    NDV = 6
    with contextlib.ExitStack() as es:
        sb_x = [es.enter_context(_sb(nc, f"ln_x{i}", [128, D], F32)) for i in range(2)]
        sb_sum = es.enter_context(_sb(nc, "ln_sum", [128, 1], F32))
        sb_negm = es.enter_context(_sb(nc, "ln_negm", [128, 1], F32))
        sb_xc = es.enter_context(_sb(nc, "ln_xc", [128, D], F32))
        sb_sq = es.enter_context(_sb(nc, "ln_sq", [128, D], F32))
        sb_ss = es.enter_context(_sb(nc, "ln_ss", [128, 1], F32))
        sb_veps = es.enter_context(_sb(nc, "ln_veps", [128, 1], F32))
        sb_iv = es.enter_context(_sb(nc, "ln_iv", [128, 1], F32))
        sb_rstd = es.enter_context(_sb(nc, "ln_rstd", [128, 1], F32))
        sb_xtm = [es.enter_context(_sb(nc, f"ln_xtm{i}", [128, D], BF16)) for i in range(2)]
        ps_tr = [es.enter_context(_pt(nc, f"ln_tr{i}", [128, 1024], BF16)) for i in range(2)]
        s_ld = [es.enter_context(_sm(nc, f"ln_ld{i}")) for i in range(2)]
        s_dv = es.enter_context(_sm(nc, "ln_dv"))
        s_ac = es.enter_context(_sm(nc, "ln_ac"))
        s_xn = es.enter_context(_sm(nc, "ln_xn"))
        s_pe = es.enter_context(_sm(nc, "ln_pe"))
        s_ev = es.enter_context(_sm(nc, "ln_ev"))
        s_ps = es.enter_context(_sm(nc, "ln_ps"))

        with nc.Block(no_gpsimd_drain=True) as block:
            @block.sync
            def _(sy):
                for tg in range(TG):
                    if tg >= 2:
                        sy.wait_ge(s_xn, tg - 1)
                    sy.dma_start(out=sb_x[tg % 2][:, :],
                                 in_=x[tg * 128:(tg + 1) * 128, :]).then_inc(s_ld[tg % 2], 16)
                if pay_ap is not None:
                    sy.wait_ge(s_ev, 2 * TG)
                    for ch in range(DC):
                        sy.dma_start(out=pay_ap[ch * 128:(ch + 1) * 128, :],
                                     in_=xn[:, ch * TPC:(ch + 1) * TPC]).then_inc(s_ps, 16)
                    sy.wait_ge(s_ps, DC * 16)

            @block.vector
            def _(v):
                xn3 = xn[:, :].rearrange("p (c t) -> p c t", c=DC)
                for tg in range(TG + 1):
                    if tg < TG:
                        xb = sb_x[tg % 2]
                        v.wait_ge(s_ld[tg % 2], (tg // 2 + 1) * 16)
                        v.tensor_reduce(sb_sum[:, :], xb[:, :], axis=AX.X, op=ALU.add).then_inc(s_dv, 1)
                        v.wait_ge(s_dv, tg * NDV + 1)
                        v.tensor_scalar(sb_negm[:, :], sb_sum[:, :], -1.0 / D, None, ALU.mult).then_inc(s_dv, 1)
                        v.wait_ge(s_dv, tg * NDV + 2)
                        v.tensor_scalar(sb_xc[:, :], xb[:, :], sb_negm[:, :], None, ALU.add).then_inc(s_dv, 1)
                        v.wait_ge(s_dv, tg * NDV + 3)
                        v.scalar_tensor_tensor(sb_sq[:, :], sb_xc[:, :], 1.0, sb_xc[:, :],
                                               ALU.mult, ALU.mult,
                                               accum_out=sb_ss[:, :]).then_inc(s_dv, 1)
                        v.wait_ge(s_dv, tg * NDV + 4)
                        v.tensor_scalar(sb_veps[:, :], sb_ss[:, :], 1.0 / D, EPS,
                                        ALU.mult, ALU.add).then_inc(s_dv, 1)
                        v.wait_ge(s_dv, tg * NDV + 5)
                        v.reciprocal(sb_iv[:, :], sb_veps[:, :]).then_inc(s_dv, 1)
                        v.wait_ge(s_ac, tg + 1)
                        if tg >= 2:
                            v.wait_ge(s_pe, tg - 1)
                        v.tensor_scalar(sb_xtm[tg % 2][:, :], sb_xc[:, :], sb_rstd[:, :], None,
                                        ALU.mult).then_inc(s_xn, 1)
                    if tg >= 1:
                        tp = tg - 1
                        v.wait_ge(s_pe, tp + 1)
                        for half in range(2):
                            v.tensor_copy(
                                xn3[:, half * 8:half * 8 + 8, tp * 128:(tp + 1) * 128],
                                ps_tr[half][:, :].rearrange("p (c t) -> p c t", c=8),
                            ).then_inc(s_ev, 1)

            @block.scalar
            def _(s):
                for tg in range(TG):
                    s.wait_ge(s_dv, tg * NDV + 6)
                    s.activation(sb_rstd[:, :], sb_iv[:, :], AF.Sqrt).then_inc(s_ac, 1)

            @block.tensor
            def _(t):
                for tg in range(TG):
                    t.wait_ge(s_xn, tg + 1)
                    if tg >= 1:
                        t.wait_ge(s_ev, 2 * tg)
                    ins = None
                    for half in range(2):
                        for j in range(8):
                            ch = half * 8 + j
                            ins = t.transpose(ps_tr[half][:, j * 128:(j + 1) * 128],
                                              sb_xtm[tg % 2][:, ch * 128:(ch + 1) * 128],
                                              sb_ident[:, :])
                    ins.then_inc(s_pe, 1)



def _emit_scale_xn(nc, xn, sb_suug, TPC):
    with contextlib.ExitStack() as es:
        s_dv = es.enter_context(_sm(nc, "sx_dv"))
        sb_t = es.enter_context(_sb(nc, "sx_t", [128, TPC], BF16))
        with nc.Block(no_gpsimd_drain=True) as block:
            @block.vector
            def _(v):
                n = 0
                for ch in range(DC):
                    v.tensor_scalar(sb_t[:, :], xn[:, ch * TPC:(ch + 1) * TPC],
                                    sb_suug[:, ch:ch + 1], None, ALU.mult).then_inc(s_dv, 1)
                    n += 1
                    v.wait_ge(s_dv, n)
                    v.tensor_copy(xn[:, ch * TPC:(ch + 1) * TPC], sb_t[:, :]).then_inc(s_dv, 1)
                    n += 1
                    v.wait_ge(s_dv, n)


def _emit_preproj(nc, xn, pay, wpre, sb_bpre, sb_perm, krcos, krsin, qrcos, qrsin, TPC):
    NT = TPC // 512
    chunks = ([(oc * 128, 128, oc * 128, oc, "plain") for oc in range(8)]
              + [(1024, 64, 1024, 8, "kr")]
              + [(QR_BASE + j * 128, 128, 1088 + j * 128, 9 + j, "qr") for j in range(8)])
    NCH = len(chunks)
    wpre_r = wpre.rearrange("(c p) n -> p c n", p=128)

    with contextlib.ExitStack() as es:
        sb_w = [es.enter_context(_sb(nc, f"pp_w{i}", [128, DC * 128], BF16)) for i in range(2)]
        sb_o = es.enter_context(_sb(nc, "pp_o", [128, TPC], BF16))
        sb_cos = es.enter_context(_sb(nc, "pp_cos", [128, TPC], BF16))
        sb_sin = es.enter_context(_sb(nc, "pp_sin", [128, TPC], BF16))
        sb_t1 = es.enter_context(_sb(nc, "pp_t1", [128, TPC], BF16))
        sb_pay = es.enter_context(_sb(nc, "pp_pay", [128, TPC], BF16))
        ps_pr = es.enter_context(_pt(nc, "pp_pr", [128, 4 * 512], F32))
        ps_rt = es.enter_context(_pt(nc, "pp_rt", [128, 4 * 512], F32))
        s_wl = [es.enter_context(_sm(nc, f"pp_wl{i}")) for i in range(2)]
        s_tb = es.enter_context(_sm(nc, "pp_tb"))
        s_pe = es.enter_context(_sm(nc, "pp_pe"))
        s_rt = es.enter_context(_sm(nc, "pp_rt"))
        s_ac = es.enter_context(_sm(nc, "pp_ac"))
        s_dv = es.enter_context(_sm(nc, "pp_dv"))
        s_st = es.enter_context(_sm(nc, "pp_st"))

        with nc.Block(no_gpsimd_drain=True) as block:
            @block.gpsimd
            def _(g_):
                row0, rows, w0, bc, kind = chunks[0]
                g_.dma_start(out=sb_w[0][:, 0:DC * rows],
                             in_=wpre_r[:, :, w0:w0 + rows]).then_inc(s_wl[0], 16)

        ndv = 0
        for i, (row0, rows, w0, bc, kind) in enumerate(chunks):
            rope = kind != "plain"
            ndv_start = ndv

            with nc.Block(no_gpsimd_drain=True) as block:
                @block.gpsimd
                def _(g_, i=i):
                    if i + 1 < NCH:
                        _r2, rows2, w02, _, _ = chunks[i + 1]
                        g_.dma_start(out=sb_w[(i + 1) % 2][:, 0:DC * rows2],
                                     in_=wpre_r[:, :, w02:w02 + rows2]).then_inc(s_wl[(i + 1) % 2], 16)

                @block.sync
                def _(sy, i=i, rope=rope, rows=rows, bc=bc, kind=kind, ndv_start=ndv_start, row0=row0):
                    if rope:
                        csrc = krcos if kind == "kr" else qrcos
                        ssrc = krsin if kind == "kr" else qrsin
                        r0 = 0 if kind == "kr" else (bc - 9) * 128
                        sy.dma_start(out=sb_cos[0:rows, :], in_=csrc[r0:r0 + rows, :]).then_inc(s_tb, 16)
                        sy.dma_start(out=sb_sin[0:rows, :], in_=ssrc[r0:r0 + rows, :]).then_inc(s_tb, 16)
                        sy.wait_ge(s_dv, ndv_start + 2 + NT)
                        src = sb_pay
                    else:
                        sy.wait_ge(s_ac, i + 1)
                        src = sb_o
                    sy.dma_start(out=pay[row0:row0 + rows, :],
                                 in_=src[0:rows, :]).then_inc(s_st, 16)
                    sy.wait_ge(s_st, 16 * (i + 1))

                @block.tensor
                def _(t, i=i, rows=rows, rope=rope, kind=kind):
                    t.wait_ge(s_wl[i % 2], (i // 2 + 1) * 16)
                    ins = None
                    for dc in range(DC):
                        for tt in range(NT):
                            ins = t.matmul(ps_pr[0:rows, tt * 512:(tt + 1) * 512],
                                           sb_w[i % 2][:, dc * rows:(dc + 1) * rows],
                                           xn[:, dc * TPC + tt * 512:dc * TPC + (tt + 1) * 512],
                                           start=(dc == 0), stop=(dc == DC - 1))
                    ins.then_inc(s_pe, 1)
                    if rope:
                        t.wait_ge(s_ac, i + 1)
                        psl = sb_perm[:, :] if kind == "qr" else sb_perm[0:64, 0:64]
                        for tt in range(NT):
                            ins = t.matmul(ps_rt[0:rows, tt * 512:(tt + 1) * 512], psl,
                                           sb_o[0:rows, tt * 512:(tt + 1) * 512],
                                           start=True, stop=True)
                        ins.then_inc(s_rt, 1)

                @block.scalar
                def _(s, i=i, rows=rows, bc=bc):
                    s.wait_ge(s_pe, i + 1)
                    for tt in range(NT):
                        ins = s.activation(sb_o[0:rows, tt * 512:(tt + 1) * 512],
                                           ps_pr[0:rows, tt * 512:(tt + 1) * 512], AF.Identity,
                                           bias=sb_bpre[0:rows, bc:bc + 1])
                        if tt == NT - 1:
                            ins.then_inc(s_ac, 1)

                if rope:
                    nrt = sum(1 for j in range(i + 1) if chunks[j][4] != "plain")

                    @block.vector
                    def _(v, i=i, rows=rows, ndv_start=ndv_start, nrt=nrt):
                        v.wait_ge(s_tb, 32 * nrt)
                        v.wait_ge(s_ac, i + 1)
                        v.wait_ge(s_rt, nrt)
                        v.tensor_tensor(sb_t1[0:rows, :], sb_o[0:rows, :],
                                        sb_cos[0:rows, :], ALU.mult).then_inc(s_dv, 1)
                        for tt in range(NT):
                            v.tensor_tensor(sb_pay[0:rows, tt * 512:(tt + 1) * 512],
                                            ps_rt[0:rows, tt * 512:(tt + 1) * 512],
                                            sb_sin[0:rows, tt * 512:(tt + 1) * 512],
                                            ALU.mult).then_inc(s_dv, 1)
                        v.wait_ge(s_dv, ndv_start + 1 + NT)
                        v.tensor_tensor(sb_pay[0:rows, :], sb_pay[0:rows, :],
                                        sb_t1[0:rows, :], ALU.add).then_inc(s_dv, 1)

                    ndv += 2 + NT


def _emit_ffn_upgate(nc, L, xn, wg, wu, sb_sg, sb_su, sb_sudp, fuse_d, TPC):
    TH = TPC // 2
    NT2 = TH // 512
    NSLOT = IC * 2
    wg_r = wg.rearrange("(c p) n -> p c n", p=128)
    wu_r = wu.rearrange("(c p) n -> p c n", p=128)
    cast = (L == 0)

    with contextlib.ExitStack() as es:
        sb_wg = [es.enter_context(_sb(nc, f"fg_wg{i}", [128, DC * 128], BF16)) for i in range(2)]
        sb_wu = [es.enter_context(_sb(nc, f"fg_wu{i}", [128, DC * 128], BF16)) for i in range(2)]
        sb_gate = [es.enter_context(_sb(nc, f"fg_gate{i}", [128, TH], BF16)) for i in range(2)]
        sb_f1 = [es.enter_context(_sb(nc, f"fg_f1{i}", [128, TH], BF16)) for i in range(2)]
        sb_fuse = [es.enter_context(_sb(nc, f"fg_fuse{i}", [128, TH], BF16)) for i in range(2)]
        ps_g = [es.enter_context(_pt(nc, f"fg_psg{i}", [128, TH if TH <= 1024 else 1024], F32))
                for i in range(2)]
        ps_u = [es.enter_context(_pt(nc, f"fg_psu{i}", [128, TH if TH <= 1024 else 1024], F32))
                for i in range(2)]
        s_wl = [es.enter_context(_sm(nc, f"fg_wl{i}")) for i in range(2)]
        s_pe = es.enter_context(_sm(nc, "fg_pe"))
        s_ac = es.enter_context(_sm(nc, "fg_ac"))
        s_dv = es.enter_context(_sm(nc, "fg_dv"))
        s_st = es.enter_context(_sm(nc, "fg_st"))
        assert TH <= 1024, "token half must fit 2 PSUM banks"

        def wload(g_, i):
            dma = g_.dma_start
            dma(out=sb_wg[i % 2][:, :], in_=wg_r[:, :, i * 128:(i + 1) * 128]).then_inc(s_wl[i % 2], 16)
            dma(out=sb_wu[i % 2][:, :], in_=wu_r[:, :, i * 128:(i + 1) * 128]).then_inc(s_wl[i % 2], 16)

        with nc.Block(no_gpsimd_drain=True) as block:
            @block.gpsimd
            def _(g_):
                wload(g_, 0)

        for k in range(NSLOT + 1):
            with nc.Block(no_gpsimd_drain=True) as block:
                @block.gpsimd
                def _(g_, k=k):
                    if k < NSLOT and k % 2 == 0 and (k // 2) + 1 < IC:
                        wload(g_, (k // 2) + 1)

                @block.tensor
                def _(t, k=k):
                    if k >= NSLOT:
                        return
                    i, half = k // 2, k % 2
                    par = k % 2
                    t.wait_ge(s_wl[i % 2], 32 * (i // 2 + 1))
                    if k >= 2:
                        t.wait_ge(s_dv, 2 * (k - 1))
                    ins = None
                    for dc in range(DC):
                        for tt in range(NT2):
                            c0 = half * TH + tt * 512
                            ins = t.matmul(ps_g[par][:, tt * 512:(tt + 1) * 512],
                                           sb_wg[i % 2][:, dc * 128:(dc + 1) * 128],
                                           xn[:, dc * TPC + c0:dc * TPC + c0 + 512],
                                           start=(dc == 0), stop=(dc == DC - 1))
                    ins.then_inc(s_pe, 1)
                    for dc in range(DC):
                        for tt in range(NT2):
                            c0 = half * TH + tt * 512
                            ins = t.matmul(ps_u[par][:, tt * 512:(tt + 1) * 512],
                                           sb_wu[i % 2][:, dc * 128:(dc + 1) * 128],
                                           xn[:, dc * TPC + c0:dc * TPC + c0 + 512],
                                           start=(dc == 0), stop=(dc == DC - 1))
                    ins.then_inc(s_pe, 1)

                @block.scalar
                def _(s, k=k):
                    j = k - 1
                    if not (0 <= j < NSLOT):
                        return
                    i = j // 2
                    par = j % 2
                    s.wait_ge(s_pe, 2 * j + 1)
                    s.activation(sb_gate[par][:, :], ps_g[par][:, 0:TH], AF.Silu,
                                 scale=sb_sg[:, i:i + 1]).then_inc(s_ac, 1)

                @block.vector
                def _(v, k=k):
                    j = k - 1
                    if not (0 <= j < NSLOT):
                        return
                    i = j // 2
                    par = j % 2
                    v.wait_ge(s_ac, j + 1)
                    v.wait_ge(s_pe, 2 * j + 2)
                    if k >= 3:
                        v.wait_ge(s_st, 16 * (k - 2))
                    if sb_sudp is None:
                        v.tensor_scalar(sb_f1[par][:, :], ps_u[par][:, 0:TH],
                                        sb_su[:, i:i + 1], None, ALU.mult).then_inc(s_dv, 1)
                    else:
                        v.tensor_scalar(sb_f1[par][:, :], ps_u[par][:, 0:TH],
                                        sb_su[:, i:i + 1], sb_sudp[:, i:i + 1],
                                        ALU.mult, ALU.mult).then_inc(s_dv, 1)
                    v.wait_ge(s_dv, 2 * j + 1)
                    v.tensor_tensor(sb_fuse[par][:, :], sb_gate[par][:, :],
                                    sb_f1[par][:, :], ALU.mult).then_inc(s_dv, 1)

                @block.sync
                def _(sy, k=k):
                    j = k - 1
                    if not (0 <= j < NSLOT):
                        return
                    i, half = j // 2, j % 2
                    sy.wait_ge(s_dv, 2 * j + 2)
                    sy.dma_start(out=fuse_d[i * 128:(i + 1) * 128, half * TH:(half + 1) * TH],
                                 in_=sb_fuse[j % 2][:, :]).then_inc(s_st, 16)
                    sy.wait_ge(s_st, 16 * (j + 1))

        with nc.Block(no_gpsimd_drain=True) as block:
            @block.sync
            def _(sy):
                sy.wait_ge(s_st, 16 * NSLOT)


def _emit_ffn_down(nc, L, wd, sb_sdb, fuse_d, g_out, TPC):
    TT = 128
    NTT = TPC // TT
    DH = D // 2
    NU = 2 * NTT
    cast = (L == 0)

    with contextlib.ExitStack() as es:
        sb_wr = es.enter_context(_sb(nc, "fd_wr", [128, IC * DH], BF16))
        sb_ft = [es.enter_context(_sb(nc, f"fd_ft{i}", [128, IC * TT], BF16)) for i in range(2)]
        sb_t = [es.enter_context(_sb(nc, f"fd_t{i}", [128, DH], F32)) for i in range(2)]
        sb_g = [es.enter_context(_sb(nc, f"fd_g{i}", [128, DH], F32)) for i in range(2)]
        ps_dn = [es.enter_context(_pt(nc, f"fd_ps{i}", [128, DH], F32)) for i in range(2)]
        s_fl = [es.enter_context(_sm(nc, f"fd_fl{i}")) for i in range(2)]
        s_wh = [es.enter_context(_sm(nc, f"fd_wh{i}")) for i in range(4)]
        s_pe = es.enter_context(_sm(nc, "fd_pe"))
        s_ac = es.enter_context(_sm(nc, "fd_ac"))
        s_dv = es.enter_context(_sm(nc, "fd_dv"))
        s_st = es.enter_context(_sm(nc, "fd_st"))

        fuse_r = fuse_d.rearrange("(c p) n -> p c n", p=128)
        wd_r = wd.rearrange("(c p) n -> p c n", p=128)
        GC = IC // 4

        def emit_wd_loads(eng, dd, groups):
            for g in groups:
                eng.dma_start(
                    out=sb_wr[:, g * GC * DH:(g + 1) * GC * DH]
                        .rearrange("p (c n) -> p c n", c=GC),
                    in_=wd_r[:, g * GC:(g + 1) * GC, dd * DH:(dd + 1) * DH],
                ).then_inc(s_wh[g], 16)

        sy_grps = () if cast else (0, 1)
        gp_grps = (0, 1, 2, 3) if cast else (2, 3)

        with nc.Block(no_gpsimd_drain=True) as block:
            @block.sync
            def _(sy):
                sy.dma_start(out=sb_ft[0][:, :].rearrange("p (c t) -> p c t", c=IC),
                             in_=fuse_r[:, :, 0:TT]).then_inc(s_fl[0], 16)
                emit_wd_loads(sy, 0, sy_grps)

            @block.gpsimd
            def _(g_):
                emit_wd_loads(g_, 0, gp_grps)

        for u in range(NU + 2):
            dd, tt = u // NTT, u % NTT
            if u == NTT:
                with nc.Block(no_gpsimd_drain=True) as blk2:
                    @blk2.sync
                    def _(sy):
                        sy.wait_ge(s_pe, NTT)
                        emit_wd_loads(sy, 1, sy_grps)

                    @blk2.gpsimd
                    def _(g_):
                        g_.wait_ge(s_pe, NTT)
                        emit_wd_loads(g_, 1, gp_grps)

            with nc.Block(no_gpsimd_drain=True) as block:
                @block.sync
                def _(sy, u=u):
                    if u + 1 < NU:
                        u1 = u + 1
                        sy.dma_start(
                            out=sb_ft[u1 % 2][:, :].rearrange("p (c t) -> p c t", c=IC),
                            in_=fuse_r[:, :, (u1 % NTT) * TT:(u1 % NTT + 1) * TT],
                        ).then_inc(s_fl[u1 % 2], 16)
                    us = u - 2
                    if us >= 0:
                        dds, tts = us // NTT, us % NTT
                        sy.wait_ge(s_ac, us + 1)
                        if us >= 1:
                            sy.wait_ge(s_st, 16 * us)
                        sy.dma_start(out=g_out[tts * TT:(tts + 1) * TT, dds * DH:(dds + 1) * DH],
                                     in_=sb_g[us % 2][:, :]).then_inc(s_st, 16)
                        if us == NU - 1:
                            sy.wait_ge(s_st, 16 * NU)

                @block.tensor
                def _(t, u=u, dd=dd, tt=tt):
                    if u >= NU:
                        return
                    t.wait_ge(s_fl[u % 2], 16 * (u // 2 + 1))
                    ins = None
                    for ic in range(IC):
                        if ic % GC == 0:
                            t.wait_ge(s_wh[ic // GC], 16 * (dd + 1))
                        for hw in range(DH // 512):
                            ins = t.matmul(ps_dn[u % 2][:, hw * 512:(hw + 1) * 512],
                                           sb_ft[u % 2][:, ic * TT:ic * TT + TT],
                                           sb_wr[:, ic * DH + hw * 512:ic * DH + (hw + 1) * 512],
                                           start=(ic == 0), stop=(ic == IC - 1))
                    ins.then_inc(s_pe, 1)

                @block.vector
                def _(v, u=u):
                    uv = u - 1
                    if 0 <= uv < NU:
                        v.tensor_tensor(sb_t[uv % 2][:, :], ps_dn[uv % 2][:, :],
                                        sb_sdb[:, (uv // NTT) * DH:(uv // NTT + 1) * DH],
                                        ALU.mult).then_inc(s_dv, 1)

                @block.scalar
                def _(s, u=u):
                    uv = u - 1
                    if 0 <= uv < NU:
                        s.wait_ge(s_dv, uv + 1)
                        if uv >= 2:
                            s.wait_ge(s_st, 16 * (uv - 1))
                        s.activation(sb_g[uv % 2][:, :], sb_t[uv % 2][:, :],
                                     AF.Silu).then_inc(s_ac, 1)



def build_B(L, TPC, SEQ):
    NRANK = SEQ // TPC
    DPAY = D if L == 0 else PAY1
    nc = bass.Bass("TRN2", target_bir_lowering=False)

    payf = nc.dram_tensor("payf", [NRANK * DPAY, TPC], BF16, kind="ExternalInput")
    ident = nc.dram_tensor("ident", [128, 128], BF16, kind="ExternalInput").ap()
    ones_col = nc.dram_tensor("ones_col", [128, 1], BF16, kind="ExternalInput").ap()
    ones_row = nc.dram_tensor("ones_row", [1, 128], F32, kind="ExternalInput").ap()
    causal = nc.dram_tensor("causal", [128, 1024], BF16, kind="ExternalInput").ap()
    meta = nc.dram_tensor("meta", [1, 32], I32, kind="ExternalInput").ap()
    if L == 0:
        wqkv = nc.dram_tensor("wqkv", [D, HPC * 384], F32, kind="ExternalInput").ap()
        cmask = nc.dram_tensor("cmask", [128, SEQ], BF16, kind="ExternalInput").ap()
        smask = nc.dram_tensor("smask", [128, SEQ], BF16, kind="ExternalInput").ap()
        permT = nc.dram_tensor("permT", [128, 128], BF16, kind="ExternalInput").ap()
    else:
        wukv = nc.dram_tensor("wukv", [512, HPC * 256], F32, kind="ExternalInput").ap()
        wuq = nc.dram_tensor("wuq", [512, HPC * 128], F32, kind="ExternalInput").ap()
        bkv_rs = nc.dram_tensor("bkv_rs", [128, HPC * 2], F32, kind="ExternalInput").ap()
        bq_rs = nc.dram_tensor("bq_rs", [128, HPC], F32, kind="ExternalInput").ap()

    attnout = nc.dram_tensor("attnout", [HPC * 128, SEQ], BF16, kind="ExternalOutput").ap()
    qd = nc.dram_tensor("qd", [HPC * 128, SEQ], BF16)
    kd = nc.dram_tensor("kd", [HPC * 128, SEQ], BF16)
    vd = nc.dram_tensor("vd", [HPC * SEQ, 128], BF16)
    qrd = (nc.dram_tensor("qrm", [HPC * 64, SEQ], BF16, kind="ExternalInput")
           if L == 1 else None)

    es = contextlib.ExitStack()
    with es:
        sb_ident = es.enter_context(_sb(nc, "c_ident", [128, 128], BF16))
        sb_ones = es.enter_context(_sb(nc, "c_ones", [128, 1], BF16))
        sb_or = es.enter_context(_sb(nc, "c_or", [1, 128], F32))
        sb_caus = es.enter_context(_sb(nc, "c_caus", [128, 1024], BF16))
        sb_meta = es.enter_context(_sb(nc, "c_meta", [1, 32], I32))
        if L == 0:
            sb_perm = es.enter_context(_sb(nc, "c_perm", [128, 128], BF16))
            sb_w = es.enter_context(_sb(nc, "c_wqkv", [128, DC * HPC * 384], BF16))
            WXW = HPC * 384
        else:
            sb_w = es.enter_context(_sb(nc, "c_wukv", [128, 4 * HPC * 256], BF16))
            sb_wq = es.enter_context(_sb(nc, "c_wuq", [128, 4 * HPC * 128], BF16))
            sb_bkv = es.enter_context(_sb(nc, "c_bkv", [128, HPC * 2], F32))
            sb_bq = es.enter_context(_sb(nc, "c_bq", [128, HPC], F32))
        ld0 = es.enter_context(_sm(nc, "ld0"))

        with nc.Block(no_gpsimd_drain=True) as block:
            @block.gpsimd
            def _(g_):
                n = 0
                pairs = [(sb_ident[:, :], ident), (sb_ones[:, :], ones_col),
                         (sb_or[:, :], ones_row), (sb_caus[:, :], causal),
                         (sb_meta[:, :], meta)]
                if L == 0:
                    pairs += [(sb_perm[:, :], permT),
                              (sb_w[:, :].rearrange("p (c n) -> p c n", c=DC),
                               wqkv.rearrange("(c p) n -> p c n", p=128))]
                else:
                    pairs += [(sb_w[:, :].rearrange("p (c n) -> p c n", c=4),
                               wukv.rearrange("(c p) n -> p c n", p=128)),
                              (sb_wq[:, :].rearrange("p (c n) -> p c n", c=4),
                               wuq.rearrange("(c p) n -> p c n", p=128)),
                              (sb_bkv[:, :], bkv_rs), (sb_bq[:, :], bq_rs)]
                for dst, src in pairs:
                    g_.dma_start(out=dst, in_=src).then_inc(ld0, 16)
                    n += 16
                g_.wait_ge(ld0, n)

        if L == 0:
            _emit_proj_l0(nc, payf, sb_w, sb_perm, sb_ident, cmask, smask,
                          qd, kd, vd, TPC, SEQ)
        else:
            _emit_proj_l1(nc, payf, sb_w, sb_wq, sb_bkv, sb_bq, sb_ident,
                          qd, kd, vd, qrd, TPC, SEQ)

        _emit_sdpa(nc, L, qd, kd, vd, qrd, payf, meta,
                   sb_ones, sb_or, sb_caus, attnout, TPC, SEQ)

    return nc


def _vd_store_ap(vd, h, t0, ngrp, SEQ):
    return bass.AP(vd, (h * SEQ + t0) * 128,
                   [[128, 128], [128 * 128, ngrp], [1, 128]])


def _emit_proj_l0(nc, payf, sb_w, sb_perm, sb_ident, cmask, smask, qd, kd, vd, TPC, SEQ):
    NRANK = SEQ // TPC
    TH = TPC // 2
    NT2 = TH // 512
    NG2 = TH // 128
    WXW = HPC * 384
    UPB = HPC * 3 * 2

    with contextlib.ExitStack() as es:
        sb_xb = es.enter_context(_sb(nc, "pj_xb", [128, DC * TPC], BF16))
        sb_raw = [es.enter_context(_sb(nc, f"pj_raw{i}", [128, TH], BF16)) for i in range(2)]
        sb_cm = es.enter_context(_sb(nc, "pj_cm", [128, TPC], BF16))
        sb_sm = es.enter_context(_sb(nc, "pj_sm", [128, TPC], BF16))
        sb_t1 = es.enter_context(_sb(nc, "pj_t1", [128, TH], BF16))
        sb_fin = [es.enter_context(_sb(nc, f"pj_fin{i}", [128, TH], BF16)) for i in range(2)]
        sb_vt = [es.enter_context(_sb(nc, f"pj_vt{i}", [128, TH], BF16)) for i in range(2)]
        ps_pr = [es.enter_context(_pt(nc, f"pj_pr{i}", [128, 1024], F32)) for i in range(2)]
        ps_rt = es.enter_context(_pt(nc, "pj_rt", [128, 1024], F32))
        ps_t = es.enter_context(_pt(nc, "pj_t", [128, 1024], BF16))
        s_xl = es.enter_context(_sm(nc, "pj_xl"))
        s_pe = es.enter_context(_sm(nc, "pj_pe"))
        s_rt = es.enter_context(_sm(nc, "pj_rt"))
        s_ac = es.enter_context(_sm(nc, "pj_ac"))
        s_dv = es.enter_context(_sm(nc, "pj_dv"))
        s_st = es.enter_context(_sm(nc, "pj_st"))
        s_dc = es.enter_context(_sm(nc, "pj_dc"))
        ndc = [0]

        def unit_info(u):
            tb, r = divmod(u, UPB)
            h, r2 = divmod(r, 6)
            o, half = divmod(r2, 2)
            return tb, h, o, half

        NU = NRANK * UPB
        for slot in range(NU + 1):
            with nc.Block(no_gpsimd_drain=True) as block:
                @block.sync
                def _(sy, slot=slot):
                    if slot < NU:
                        tb, h, o, half = unit_info(slot)
                        if slot % UPB == 0:
                            if tb >= 1:
                                sy.wait_ge(s_pe, tb * UPB)
                            sy.dma_start(
                                out=sb_xb[:, :].rearrange("p (c t) -> p c t", c=DC),
                                in_=payf.ap().rearrange("(r c p) t -> p r c t",
                                                        r=NRANK, p=128)[:, tb, :, :],
                            ).then_inc(s_xl, 16)
                            sy.dma_start(out=sb_cm[:, :],
                                         in_=cmask[:, tb * TPC:(tb + 1) * TPC]).then_inc(s_xl, 16)
                            sy.dma_start(out=sb_sm[:, :],
                                         in_=smask[:, tb * TPC:(tb + 1) * TPC]).then_inc(s_xl, 16)
                    u = slot - 1
                    if u >= 0:
                        tb, h, o, half = unit_info(u)
                        sy.wait_ge(s_dv, u + 1)
                        t0 = tb * TPC + half * TH
                        if o < 2:
                            dst = qd if o == 0 else kd
                            sy.dma_start(out=dst.ap()[h * 128:(h + 1) * 128, t0:t0 + TH],
                                         in_=sb_fin[u % 2][:, :]).then_inc(s_st, 16)
                        else:
                            sy.dma_start(out=_vd_store_ap(vd, h, t0, NG2, SEQ),
                                         in_=sb_vt[u % 2][:, :].rearrange(
                                             "p (g d) -> p g d", d=128)).then_inc(s_st, 16)
                        sy.wait_ge(s_st, 16 * (u + 1))

                @block.tensor
                def _(t, slot=slot):
                    if slot < NU:
                        u = slot
                        tb, h, o, half = unit_info(u)
                        par = u % 2
                        t.wait_ge(s_xl, 48 * (tb + 1))
                        if u >= 2:
                            t.wait_ge(s_ac, u - 1)
                        wcol = h * 384 + o * 128
                        ins = None
                        for dc in range(DC):
                            for tt in range(NT2):
                                c0 = half * TH + tt * 512
                                ins = t.matmul(ps_pr[par][:, tt * 512:(tt + 1) * 512],
                                               sb_w[:, dc * WXW + wcol:dc * WXW + wcol + 128],
                                               sb_xb[:, dc * TPC + c0:dc * TPC + c0 + 512],
                                               start=(dc == 0), stop=(dc == DC - 1))
                        ins.then_inc(s_pe, 1)
                    u = slot - 1
                    if u >= 0:
                        tb, h, o, half = unit_info(u)
                        par = u % 2
                        t.wait_ge(s_ac, u + 1)
                        if o < 2:
                            ins = None
                            for tt in range(NT2):
                                ins = t.matmul(ps_rt[:, tt * 512:(tt + 1) * 512], sb_perm[:, :],
                                               sb_raw[par][:, tt * 512:(tt + 1) * 512],
                                               start=True, stop=True)
                            ins.then_inc(s_rt, 1)
                        else:
                            t.wait_ge(s_dv, u)
                            ins = None
                            for gi in range(NG2):
                                ins = t.transpose(ps_t[:, gi * 128:(gi + 1) * 128],
                                                  sb_raw[par][:, gi * 128:(gi + 1) * 128],
                                                  sb_ident[:, :])
                            ins.then_inc(s_rt, 1)

                @block.scalar
                def _(s, slot=slot):
                    u = slot - 1
                    if u >= 0:
                        tb, h, o, half = unit_info(u)
                        par = u % 2
                        s.wait_ge(s_pe, u + 1)
                        if u >= 2:
                            s.wait_ge(s_dv, u - 1)
                        s.activation(sb_raw[par][:, :], ps_pr[par][:, 0:TH],
                                     AF.Identity).then_inc(s_ac, 1)

                @block.vector
                def _(v, slot=slot, ndc=ndc):
                    u = slot - 1
                    if u >= 0:
                        tb, h, o, half = unit_info(u)
                        par = u % 2
                        v.wait_ge(s_rt, u + 1)
                        if u >= 2:
                            v.wait_ge(s_st, 16 * (u - 1))
                        c0 = half * TH
                        if o < 2:
                            v.tensor_tensor(sb_t1[:, :], sb_raw[par][:, :],
                                            sb_cm[:, c0:c0 + TH], ALU.mult).then_inc(s_dc, 1)
                            ndc[0] += 1
                            v.tensor_tensor(sb_fin[u % 2][:, :], ps_rt[:, 0:TH],
                                            sb_sm[:, c0:c0 + TH], ALU.mult).then_inc(s_dc, 1)
                            ndc[0] += 1
                            v.wait_ge(s_dc, ndc[0])
                            v.tensor_tensor(sb_fin[u % 2][:, :], sb_fin[u % 2][:, :],
                                            sb_t1[:, :], ALU.add).then_inc(s_dv, 1)
                        else:
                            v.tensor_copy(sb_vt[u % 2][:, :], ps_t[:, 0:TH]).then_inc(s_dv, 1)


def _emit_proj_l1(nc, payf, sb_w, sb_wq, sb_bkv, sb_bq, sb_ident, qd, kd, vd, qrd, TPC, SEQ):
    NRANK = SEQ // TPC
    TH = TPC // 2
    NT2 = TH // 512
    NG2 = TH // 128
    UPB = HPC * 3 * 2
    WKV = HPC * 256
    WQ = HPC * 128

    with contextlib.ExitStack() as es:
        sb_cb = es.enter_context(_sb(nc, "pj_cb", [128, 8 * TPC], BF16))
        sb_raw = [es.enter_context(_sb(nc, f"pj_raw{i}", [128, TH], BF16)) for i in range(2)]
        sb_fin = [es.enter_context(_sb(nc, f"pj_fin{i}", [128, TH], BF16)) for i in range(2)]
        sb_vt = [es.enter_context(_sb(nc, f"pj_vt{i}", [128, TH], BF16)) for i in range(2)]
        sb_junk = es.enter_context(_sb(nc, "pj_junk", [1, 4], BF16))
        ps_pr = [es.enter_context(_pt(nc, f"pj_pr{i}", [128, 1024], F32)) for i in range(2)]
        ps_t = es.enter_context(_pt(nc, "pj_t", [128, 1024], BF16))
        s_xl = es.enter_context(_sm(nc, "pj_xl"))
        s_pe = es.enter_context(_sm(nc, "pj_pe"))
        s_rt = es.enter_context(_sm(nc, "pj_rt"))
        s_ac = es.enter_context(_sm(nc, "pj_ac"))
        s_dv = es.enter_context(_sm(nc, "pj_dv"))
        s_st = es.enter_context(_sm(nc, "pj_st"))
        s_qr = es.enter_context(_sm(nc, "pj_qr"))

        def unit_info(u):
            tb, r = divmod(u, UPB)
            h, r2 = divmod(r, 6)
            o, half = divmod(r2, 2)
            return tb, h, o, half

        NU = NRANK * UPB
        for slot in range(NU + 1):
            with nc.Block(no_gpsimd_drain=True) as block:
                @block.sync
                def _(sy, slot=slot):
                    if slot < NU:
                        tb, h, o, half = unit_info(slot)
                        if slot % UPB == 0:
                            if tb >= 1:
                                sy.wait_ge(s_pe, tb * UPB)
                            sy.dma_start(
                                out=sb_cb[:, :].rearrange("p (c t) -> p c t", c=8),
                                in_=bass.AP(payf, tb * PAY1 * TPC,
                                            [[TPC, 128], [128 * TPC, 8], [1, TPC]]),
                            ).then_inc(s_xl, 16)
                    u = slot - 1
                    if u >= 0:
                        tb, h, o, half = unit_info(u)
                        sy.wait_ge(s_dv, u + 1)
                        t0 = tb * TPC + half * TH
                        if o < 2:
                            dst = kd if o == 0 else qd
                            sy.dma_start(out=dst.ap()[h * 128:(h + 1) * 128, t0:t0 + TH],
                                         in_=sb_fin[u % 2][:, :]).then_inc(s_st, 16)
                        else:
                            sy.dma_start(out=_vd_store_ap(vd, h, t0, NG2, SEQ),
                                         in_=sb_vt[u % 2][:, :].rearrange(
                                             "p (g d) -> p g d", d=128)).then_inc(s_st, 16)
                        sy.wait_ge(s_st, 16 * (u + 1))

                @block.tensor
                def _(t, slot=slot):
                    if slot < NU:
                        u = slot
                        tb, h, o, half = unit_info(u)
                        par = u % 2
                        t.wait_ge(s_xl, 16 * (tb + 1))
                        if u >= 2:
                            t.wait_ge(s_ac, u - 1)
                        for cc in range(4):
                            if o == 1:
                                w_ap = sb_wq[:, cc * WQ + h * 128:cc * WQ + h * 128 + 128]
                                cbase = 4 + cc
                            else:
                                wcol = h * 256 + (0 if o == 0 else 128)
                                w_ap = sb_w[:, cc * WKV + wcol:cc * WKV + wcol + 128]
                                cbase = cc
                            for tt in range(NT2):
                                c0 = half * TH + tt * 512
                                ins = t.matmul(ps_pr[par][:, tt * 512:(tt + 1) * 512], w_ap,
                                               sb_cb[:, cbase * TPC + c0:cbase * TPC + c0 + 512],
                                               start=(cc == 0), stop=(cc == 3))
                        ins.then_inc(s_pe, 1)
                    u = slot - 1
                    if u >= 0:
                        tb, h, o, half = unit_info(u)
                        par = u % 2
                        if o == 2:
                            t.wait_ge(s_ac, u + 1)
                            t.wait_ge(s_dv, u)
                            ins = None
                            for gi in range(NG2):
                                ins = t.transpose(ps_t[:, gi * 128:(gi + 1) * 128],
                                                  sb_raw[par][:, gi * 128:(gi + 1) * 128],
                                                  sb_ident[:, :])
                            ins.then_inc(s_rt, 1)

                @block.scalar
                def _(s, slot=slot):
                    u = slot - 1
                    if u >= 0:
                        tb, h, o, half = unit_info(u)
                        par = u % 2
                        s.wait_ge(s_pe, u + 1)
                        if u >= 2:
                            s.wait_ge(s_dv, u - 1)
                        bias = (sb_bkv[:, 2 * h:2 * h + 1] if o == 0 else
                                sb_bq[:, h:h + 1] if o == 1 else
                                sb_bkv[:, 2 * h + 1:2 * h + 2])
                        if o == 2:
                            s.activation(sb_raw[par][:, :], ps_pr[par][:, 0:TH],
                                         AF.Identity, bias=bias).then_inc(s_ac, 1)
                        else:
                            s.activation(sb_fin[u % 2][:, :], ps_pr[par][:, 0:TH],
                                         AF.Identity, bias=bias).then_inc(s_ac, 1)

                @block.vector
                def _(v, slot=slot):
                    u = slot - 1
                    if u >= 0:
                        tb, h, o, half = unit_info(u)
                        if u >= 2:
                            v.wait_ge(s_st, 16 * (u - 1))
                        if o == 2:
                            v.wait_ge(s_rt, sum(1 for j in range(u + 1)
                                                if unit_info(j)[2] == 2))
                            v.tensor_copy(sb_vt[u % 2][:, :], ps_t[:, 0:TH]).then_inc(s_dv, 1)
                        else:
                            v.wait_ge(s_ac, u + 1)
                            v.tensor_copy(sb_junk[0:1, 0:2],
                                          sb_fin[u % 2][0:1, 0:2]).then_inc(s_dv, 1)



def _emit_qr_copy(nc, payf, qrd, sb_meta, TPC, SEQ):
    NRANK = SEQ // TPC
    with contextlib.ExitStack() as es:
        s_qr = es.enter_context(_sm(nc, "qrc"))
        with nc.Block(no_gpsimd_drain=True) as block:
            @block.gpsimd
            def _(g_):
                for r in range(NRANK):
                    val = g_.value_load(sb_meta[0:1, 8 + r:9 + r],
                                        min_val=0, max_val=NRANK * PAY1 - 256)
                    g_.dma_start(
                        out=qrd.ap()[:, r * TPC:(r + 1) * TPC],
                        in_=payf.ap()[bass.ds(val, HPC * 64), :],
                    ).then_inc(s_qr, 16)
                g_.wait_ge(s_qr, 16 * NRANK)


def _emit_sdpa(nc, L, qd, kd, vd, qrd, payf, meta, sb_ones, sb_or, sb_caus,
               attnout, TPC, SEQ):
    NRANK = SEQ // TPC
    SQT = SEQ // 512
    DPAY = D if L == 0 else PAY1

    with contextlib.ExitStack() as es:
        sb_meta = es.enter_context(_sb(nc, "sd_meta", [1, 8], I32))
        sb_k = es.enter_context(_sb(nc, "sd_k", [128, SEQ], BF16))
        sb_v = es.enter_context(_sb(nc, "sd_v", [128, SEQ], BF16))
        sb_qt = [es.enter_context(_sb(nc, f"sd_q{i}", [128, 512], BF16)) for i in range(2)]
        sb_p = [es.enter_context(_sb(nc, f"sd_p{i}", [128, 512], BF16)) for i in range(3)]
        sb_pt = [es.enter_context(_sb(nc, f"sd_pt{i}", [128, 512], BF16)) for i in range(2)]
        sb_li = es.enter_context(_sb(nc, "sd_li", [1, 512], F32))
        sb_bc = [es.enter_context(_sb(nc, f"sd_bc{i}", [128, 512], F32)) for i in range(2)]
        sb_ao = [es.enter_context(_sb(nc, f"sd_ao{i}", [128, 512], BF16)) for i in range(2)]
        if L == 1:
            sb_kr = es.enter_context(_sb(nc, "sd_kr", [64, SEQ], BF16))
            sb_qr = [es.enter_context(_sb(nc, f"sd_qr{i}", [64, 512], BF16)) for i in range(2)]
        ps_sc = [es.enter_context(_pt(nc, f"sd_sc{i}", [128, 512], F32)) for i in range(3)]
        ps_out = [es.enter_context(_pt(nc, f"sd_o{i}", [128, 512], F32)) for i in range(2)]
        ps_l = [es.enter_context(_pt(nc, f"sd_l{i}", [1, 512], F32)) for i in range(2)]
        ps_bc = es.enter_context(_pt(nc, "sd_bc", [128, 512], F32))
        s_kv = es.enter_context(_sm(nc, "sd_kv"))
        s_ql = [es.enter_context(_sm(nc, f"sd_ql{i}")) for i in range(2)]
        s_mm = es.enter_context(_sm(nc, "sd_mm"))
        s_ex = es.enter_context(_sm(nc, "sd_ex"))
        s_mk = es.enter_context(_sm(nc, "sd_mk"))
        s_pv = es.enter_context(_sm(nc, "sd_pv"))
        s_bm = es.enter_context(_sm(nc, "sd_bm"))
        s_bv = es.enter_context(_sm(nc, "sd_bv"))
        s_dc = es.enter_context(_sm(nc, "sd_dc"))
        s_fo = es.enter_context(_sm(nc, "sd_fo"))
        s_st = es.enter_context(_sm(nc, "sd_st"))

        def tiles_of():
            tl = []
            for qt in range(SQT):
                nk = 4 * qt + 4
                for kt in range(nk):
                    tl.append((qt, kt, kt >= 4 * qt))
            return tl

        TL = tiles_of()
        NTILE = len(TL)
        qt_last = {}
        qt_first = {}
        for m, (qt, kt, dg) in enumerate(TL):
            qt_last[qt] = m
            if qt not in qt_first:
                qt_first[qt] = m
        diag_upto = np.cumsum([1 if t[2] else 0 for t in TL]).tolist()

        nkv = 1 + 1 + (NRANK if L == 1 else 0)

        for h in range(HPC):
            with nc.Block(no_gpsimd_drain=True) as block:
                @block.sync
                def _(sy, h=h):
                    base_kv = h * nkv * 16
                    sy.dma_start(out=sb_k[:, :],
                                 in_=kd.ap()[h * 128:(h + 1) * 128, :]).then_inc(s_kv, 16)
                    sy.dma_start(out=sb_v[:, :].rearrange("p (g d) -> p g d", d=128),
                                 in_=bass.AP(vd, h * SEQ * 128,
                                             [[128, 128], [128 * 128, SEQ // 128], [1, 128]]),
                                 ).then_inc(s_kv, 16)
                    if L == 1:
                        for r in range(NRANK):
                            sy.dma_start(out=sb_kr[:, r * TPC:(r + 1) * TPC],
                                         in_=payf.ap()[r * PAY1 + 1024:r * PAY1 + 1088, :],
                                         ).then_inc(s_kv, 16)
                    for qt in range(SQT + 2):
                        if qt < SQT:
                            if qt >= 2:
                                sy.wait_ge(s_pv, h * NTILE + qt_last[qt - 2] + 1)
                            sy.dma_start(out=sb_qt[qt % 2][:, :],
                                         in_=qd.ap()[h * 128:(h + 1) * 128,
                                                     qt * 512:(qt + 1) * 512]).then_inc(s_ql[qt % 2], 16)
                            if L == 1:
                                sy.dma_start(out=sb_qr[qt % 2][0:64, :],
                                             in_=qrd.ap()[h * 64:(h + 1) * 64,
                                                          qt * 512:(qt + 1) * 512]).then_inc(s_ql[qt % 2], 16)
                        u = qt - 2
                        if u >= 0:
                            sy.wait_ge(s_fo, h * SQT + u + 1)
                            sy.dma_start(out=attnout[h * 128:(h + 1) * 128,
                                                     u * 512:(u + 1) * 512],
                                         in_=sb_ao[u % 2][:, :]).then_inc(s_st, 16)
                            sy.wait_ge(s_st, 16 * (h * SQT + u + 1))

                @block.tensor
                def _(t, h=h):
                    NQL = 1 if L == 0 else 2
                    t.wait_ge(s_kv, (h + 1) * nkv * 16)
                    mbase = h * NTILE

                    def emit_scores(m):
                        qt, kt, dg = TL[m]
                        if m == qt_first[qt]:
                            t.wait_ge(s_ql[qt % 2], 16 * NQL * ((h * SQT + qt) // 2 + 1))
                            if qt >= 2:
                                t.wait_ge(s_fo, h * SQT + qt - 1)
                        if m >= 3:
                            t.wait_ge(s_ex, mbase + m - 2)
                        ins = t.matmul(ps_sc[m % 3][:, :],
                                       sb_k[:, kt * 128:(kt + 1) * 128],
                                       sb_qt[qt % 2][:, :],
                                       start=True, stop=(L == 0))
                        if L == 1:
                            ins = t.matmul(ps_sc[m % 3][:, :],
                                           sb_kr[0:64, kt * 128:(kt + 1) * 128],
                                           sb_qr[qt % 2][0:64, :],
                                           start=False, stop=True)
                        ins.then_inc(s_mm, 1)

                    def emit_pv(m):
                        qt, kt, dg = TL[m]
                        if dg:
                            t.wait_ge(s_mk, _dg_count(TL, h, m, diag_upto))
                        else:
                            t.wait_ge(s_ex, mbase + m + 1)
                        first = (m == qt_first[qt])
                        last = (m == qt_last[qt])
                        t.matmul(ps_out[qt % 2][:, :],
                                 sb_v[:, kt * 128:(kt + 1) * 128],
                                 sb_p[m % 3][:, :],
                                 start=first, stop=last)
                        t.matmul(ps_l[qt % 2][0:1, :],
                                 sb_ones[:, 0:1],
                                 sb_p[m % 3][:, :],
                                 start=first, stop=last).then_inc(s_pv, 1)
                        if last:
                            t.wait_ge(s_dc, h * SQT + qt + 1)
                            if h * SQT + qt >= 1:
                                t.wait_ge(s_bv, h * SQT + qt)
                            t.matmul(ps_bc[:, :], sb_or[:, :], sb_li[:, :],
                                     start=True, stop=True).then_inc(s_bm, 1)

                    for m in range(NTILE):
                        emit_scores(m)
                        if m >= 2:
                            emit_pv(m - 2)
                    emit_pv(NTILE - 2)
                    emit_pv(NTILE - 1)

                @block.scalar
                def _(s, h=h):
                    mbase = h * NTILE
                    dbase = 0 if h == 0 else diag_upto[-1] * h
                    for m, (qt, kt, dg) in enumerate(TL):
                        s.wait_ge(s_mm, mbase + m + 1)
                        if dg:
                            dgc = dbase + diag_upto[m]
                            if dgc >= 3:
                                s.wait_ge(s_mk, dgc - 2)
                            s.activation(sb_pt[(dgc - 1) % 2][:, :], ps_sc[m % 3][:, :],
                                         AF.Exp, scale=ISQ).then_inc(s_ex, 1)
                        else:
                            if m >= 3:
                                s.wait_ge(s_pv, mbase + m - 2)
                            s.activation(sb_p[m % 3][:, :], ps_sc[m % 3][:, :],
                                         AF.Exp, scale=ISQ).then_inc(s_ex, 1)
                        if m == qt_last[qt]:
                            s.wait_ge(s_bm, h * SQT + qt + 1)
                            s.activation(sb_bc[qt % 2][:, :], ps_bc[:, :],
                                         AF.Identity).then_inc(s_bv, 1)

                @block.vector
                def _(v, h=h):
                    mbase = h * NTILE
                    dbase = 0 if h == 0 else diag_upto[-1] * h
                    for m, (qt, kt, dg) in enumerate(TL):
                        if dg:
                            dgc = dbase + diag_upto[m]
                            v.wait_ge(s_ex, mbase + m + 1)
                            if m >= 3:
                                v.wait_ge(s_pv, mbase + m - 2)
                            off = 512 - (kt * 128 - qt * 512)
                            v.tensor_tensor(sb_p[m % 3][:, :], sb_pt[(dgc - 1) % 2][:, :],
                                            sb_caus[:, off:off + 512], ALU.mult).then_inc(s_mk, 1)
                        if m == qt_last[qt]:
                            iq = h * SQT + qt
                            v.wait_ge(s_pv, mbase + m + 1)
                            if iq >= 1:
                                v.wait_ge(s_bm, iq)
                            v.reciprocal(sb_li[:, :], ps_l[qt % 2][0:1, :]).then_inc(s_dc, 1)
                            v.wait_ge(s_bv, iq + 1)
                            if iq >= 2:
                                v.wait_ge(s_st, 16 * (iq - 1))
                            v.tensor_tensor(sb_ao[qt % 2][:, :], ps_out[qt % 2][:, :],
                                            sb_bc[qt % 2][:, :], ALU.mult).then_inc(s_fo, 1)

        with nc.Block(no_gpsimd_drain=True) as block:
            @block.sync
            def _(sy):
                sy.wait_ge(s_st, 16 * HPC * SQT)


def _dg_count(TL, h, m, diag_upto):
    return (diag_upto[-1] * h) + diag_upto[m]



def _emit_post(nc, sin_dma, wo_cm, bias_row, ones_row_sb, g_in, x_in, xo,
               TPC, s_extld=None):
    NTG = TPC // 128
    es = contextlib.ExitStack()
    with es:
        sb_s = es.enter_context(_sb(nc, "po_s", [128, DC * TPC], BF16))
        sb_w = es.enter_context(_sb(nc, "po_w", [128, DC * D], BF16))
        sb_br = es.enter_context(_sb(nc, "po_br", [1, D], BF16))
        sb_one1 = es.enter_context(_sb(nc, "po_one1", [1, 128], BF16))
        sb_g = [es.enter_context(_sb(nc, f"po_g{i}", [128, D], F32)) for i in range(2)]
        sb_x = [es.enter_context(_sb(nc, f"po_x{i}", [128, D], F32)) for i in range(2)]
        sb_sb = [es.enter_context(_sb(nc, f"po_sb{i}", [128, D], F32)) for i in range(2)]
        ps_s = [es.enter_context(_pt(nc, f"po_ps{i}", [128, D], F32)) for i in range(2)]
        s_l0 = es.enter_context(_sm(nc, "po_l0"))
        s_lb = es.enter_context(_sm(nc, "po_lb"))
        s_ld = [es.enter_context(_sm(nc, f"po_ld{i}")) for i in range(2)]
        s_pe = es.enter_context(_sm(nc, "po_pe"))
        s_ac = es.enter_context(_sm(nc, "po_ac"))
        s_dv = es.enter_context(_sm(nc, "po_dv"))
        s_st = es.enter_context(_sm(nc, "po_st"))

        nl0 = [0]
        with nc.Block(no_gpsimd_drain=True) as block:
            @block.gpsimd
            def _(g_):
                g_.memset(sb_one1[:, :], 1.0)
                g_.dma_start(out=sb_w[:, :].rearrange("p (c n) -> p c n", c=DC),
                             in_=wo_cm.rearrange("(c p) n -> p c n", p=128)).then_inc(s_lb, 16)
                g_.dma_start(out=sb_br[:, :], in_=bias_row).then_inc(s_lb, 16)
                g_.wait_ge(s_lb, 32)

        for slot in range(NTG + 1):
            with nc.Block(no_gpsimd_drain=True) as block:
                if slot == 0:
                    @block.gpsimd
                    def _(g_):
                        nl0[0] += sin_dma(g_, sb_s, s_l0)

                @block.sync
                def _(sy, slot=slot):
                    if slot < NTG:
                        tg = slot
                        if tg >= 2:
                            sy.wait_ge(s_dv, 2 * (tg - 1))
                            sy.wait_ge(s_st, 16 * (tg - 1))
                        sy.dma_start(out=sb_g[tg % 2][:, :],
                                     in_=g_in[tg * 128:(tg + 1) * 128, :]).then_inc(s_ld[tg % 2], 16)
                        sy.dma_start(out=sb_x[tg % 2][:, :],
                                     in_=x_in[tg * 128:(tg + 1) * 128, :]).then_inc(s_ld[tg % 2], 16)
                    u = slot - 1
                    if u >= 0:
                        sy.wait_ge(s_dv, 2 * (u + 1))
                        if u >= 1:
                            sy.wait_ge(s_st, 16 * u)
                        sy.dma_start(out=xo[u * 128:(u + 1) * 128, :],
                                     in_=sb_x[u % 2][:, :]).then_inc(s_st, 16)
                        if u == NTG - 1:
                            sy.wait_ge(s_st, 16 * NTG)

                @block.tensor
                def _(t, slot=slot):
                    if slot < NTG:
                        tg = slot
                        par = tg % 2
                        if tg == 0:
                            t.wait_ge(s_l0, nl0[0])
                            t.wait_ge(s_lb, 32)
                            if s_extld is not None:
                                t.wait_ge(s_extld[0], s_extld[1])
                        if tg >= 2:
                            t.wait_ge(s_ac, tg - 1)
                        ins = None
                        for hw in range(D // 512):
                            t.matmul(ps_s[par][:, hw * 512:(hw + 1) * 512],
                                     sb_one1[:, :], sb_br[:, hw * 512:(hw + 1) * 512],
                                     start=True, stop=False)
                        for a in range(DC):
                            for hw in range(D // 512):
                                ins = t.matmul(ps_s[par][:, hw * 512:(hw + 1) * 512],
                                               sb_s[:, a * TPC + tg * 128:a * TPC + (tg + 1) * 128],
                                               sb_w[:, a * D + hw * 512:a * D + (hw + 1) * 512],
                                               start=False, stop=(a == DC - 1))
                        ins.then_inc(s_pe, 1)

                @block.scalar
                def _(s, slot=slot):
                    u = slot - 1
                    if u >= 0:
                        s.wait_ge(s_pe, u + 1)
                        if u >= 2:
                            s.wait_ge(s_dv, 2 * (u - 1))
                        s.activation(sb_sb[u % 2][:, :], ps_s[u % 2][:, 0:D],
                                     AF.Identity).then_inc(s_ac, 1)

                @block.vector
                def _(v, slot=slot):
                    u = slot - 1
                    if u >= 0:
                        par = u % 2
                        v.wait_ge(s_ac, u + 1)
                        v.wait_ge(s_ld[u % 2], 32 * (u // 2 + 1))
                        v.tensor_tensor(sb_sb[par][:, :], sb_sb[par][:, :],
                                        sb_g[par][:, :], ALU.mult).then_inc(s_dv, 1)
                        v.wait_ge(s_dv, 2 * u + 1)
                        v.tensor_tensor(sb_x[par][:, :], sb_sb[par][:, :],
                                        sb_x[par][:, :], ALU.add).then_inc(s_dv, 1)


def build_C(TPC):
    NTG = TPC // 128
    nc = bass.Bass("TRN2", target_bir_lowering=False)
    s_in = nc.dram_tensor("s_in", [D, TPC], BF16, kind="ExternalInput").ap()
    wo_cm = nc.dram_tensor("wo_cm", [D, D], F32, kind="ExternalInput").ap()
    bias_row = nc.dram_tensor("bias_row", [1, D], F32, kind="ExternalInput").ap()
    ones_row = nc.dram_tensor("ones_row", [1, 128], F32, kind="ExternalInput").ap()
    g_in = nc.dram_tensor("g", [TPC, D], F32, kind="ExternalInput").ap()
    x_in = nc.dram_tensor("x", [TPC, D], F32, kind="ExternalInput").ap()
    xo = nc.dram_tensor("xo", [TPC, D], F32, kind="ExternalOutput").ap()
    es = contextlib.ExitStack()
    with es:
        sb_or = es.enter_context(_sb(nc, "c_or", [1, 128], F32))
        s_c = es.enter_context(_sm(nc, "c_or_ld"))
        with nc.Block(no_gpsimd_drain=True) as block:
            @block.gpsimd
            def _(g_):
                g_.dma_start(out=sb_or[:, :], in_=ones_row).then_inc(s_c, 16)
                g_.wait_ge(s_c, 16)

        def sin_dma(g_, sb_s, sem):
            g_.dma_start(out=sb_s[:, :].rearrange("p (c t) -> p c t", c=DC),
                         in_=s_in.rearrange("(c p) t -> p c t", p=128)).then_inc(sem, 16)
            return 16

        _emit_post(nc, sin_dma, wo_cm, bias_row, sb_or, g_in, x_in, xo, TPC)
    return nc


def _emit_tr(nc, src, dst, R, C, sb_ident, px, src_off=0):
    NR = R // 128
    NCC = C // 128
    RB = min(NR, 64)
    NPASS = (NR + RB - 1) // RB
    es = contextlib.ExitStack()
    with es:
        sb_a = [es.enter_context(_sb(nc, f"{px}tr_a{i}", [128, RB * 128], BF16)) for i in range(2)]
        sb_o = [es.enter_context(_sb(nc, f"{px}tr_o{i}", [128, RB * 128], BF16)) for i in range(2)]
        ps = [es.enter_context(_pt(nc, f"{px}tr_ps{i}", [128, 1024], BF16)) for i in range(2)]
        s_ld = [es.enter_context(_sm(nc, f"{px}tr_ld{i}")) for i in range(2)]
        s_pe = es.enter_context(_sm(nc, f"{px}tr_pe"))
        s_dv = es.enter_context(_sm(nc, f"{px}tr_dv"))
        s_st = [es.enter_context(_sm(nc, f"{px}tr_st{i}")) for i in range(2)]
        G = [0]

        for cc in range(NCC):
            for p in range(NPASS):
                it = cc * NPASS + p
                par = it % 2
                rb = min(RB, NR - p * RB)
                ng = (rb + 7) // 8
                g0 = G[0]
                with nc.Block(no_gpsimd_drain=True) as block:
                    @block.sync
                    def _(sy, cc=cc, p=p, it=it, par=par, rb=rb, ng=ng, g0=g0):
                        sy.dma_start(
                            out=sb_a[par][:, 0:rb * 128].rearrange("q (r t) -> q r t", t=128),
                            in_=bass.AP(src, src_off + (p * RB) * 128 * C + cc * 128,
                                        [[C, 128], [128 * C, rb], [1, 128]]),
                        ).then_inc(s_ld[par], 16)
                        sy.wait_ge(s_dv, g0 + ng)
                        sy.dma_start(
                            out=bass.AP(dst, cc * 128 * R + p * RB * 128,
                                        [[R, 128], [1, rb * 128]]),
                            in_=sb_o[par][:, 0:rb * 128],
                        ).then_inc(s_st[par], 16)
                        sy.wait_ge(s_st[par], 16 * (it // 2 + 1))

                    @block.tensor
                    def _(t, it=it, par=par, rb=rb, ng=ng, g0=g0):
                        t.wait_ge(s_ld[par], 16 * (it // 2 + 1))
                        for g8 in range(ng):
                            gg = g0 + g8
                            if gg >= 2:
                                t.wait_ge(s_dv, gg - 1)
                            n8 = min(8, rb - g8 * 8)
                            ins = None
                            for j in range(n8):
                                r_ = g8 * 8 + j
                                ins = t.transpose(ps[gg % 2][:, j * 128:(j + 1) * 128],
                                                  sb_a[par][:, r_ * 128:(r_ + 1) * 128],
                                                  sb_ident[:, :])
                            ins.then_inc(s_pe, 1)

                    @block.vector
                    def _(v, it=it, par=par, rb=rb, ng=ng, g0=g0):
                        for g8 in range(ng):
                            gg = g0 + g8
                            n8 = min(8, rb - g8 * 8)
                            v.wait_ge(s_pe, gg + 1)
                            v.tensor_copy(
                                sb_o[par][:, g8 * 8 * 128:(g8 * 8 + n8) * 128]
                                .rearrange("q (r t) -> q r t", t=128),
                                ps[gg % 2][:, 0:n8 * 128].rearrange("q (r t) -> q r t", t=128),
                            ).then_inc(s_dv, 1)
                G[0] += ng



def _emit_wfold(nc, wupT, wgT, wdT, sb_h128, sb_h16, sb_h64, upf, gatef, downf, u1, u1g, u2):
    COLS = 1024
    HCOL = 256
    es = contextlib.ExitStack()
    with es:
        sb_in = [es.enter_context(_sb(nc, f"w_in{i}", [128, 2048], BF16)) for i in range(2)]
        sb_inF = [es.enter_context(_sb(nc, f"w_inF{i}", [128, 1024], F32)) for i in range(2)]
        sb_o = [es.enter_context(_sb(nc, f"w_o{i}", [128, 2048], BF16)) for i in range(2)]
        ps = [es.enter_context(_pt(nc, f"w_ps{i}", [128, 2048], F32)) for i in range(2)]
        s_ld = [es.enter_context(_sm(nc, f"w_ld{i}")) for i in range(2)]
        s_ls = [es.enter_context(_sm(nc, f"w_ls{i}")) for i in range(2)]
        s_cv = es.enter_context(_sm(nc, "w_cv"))
        s_pe = es.enter_context(_sm(nc, "w_pe"))
        s_ac = es.enter_context(_sm(nc, "w_ac"))
        s_st = [es.enter_context(_sm(nc, f"w_st{i}")) for i in range(2)]

        descs1, descs2 = [], []
        for (wt, out_d, mid_d, cols, nA) in [
            (wupT, upf, u1, COLS, 16), (wgT, gatef, u1g, COLS, 16),
            (wdT, downf, u2, HCOL, 64),
        ]:
            hm = sb_h16 if nA == 16 else sb_h64
            nrm = 1.0 / math.sqrt(128.0 * nA)
            BA = max(1, 2048 // cols // 2)
            for a0 in range(0, nA, BA):
                ba = min(BA, nA - a0)
                descs1.append((
                    bass.AP(wt, a0 * 128 * cols, [[cols, 128], [128 * cols, ba], [1, cols]]),
                    bass.AP(mid_d, a0 * 128 * cols, [[cols, 128], [128 * cols, ba], [1, cols]]),
                    128, ba * cols, sb_h128[:, :], nrm))
            BC = max(1, 2048 // cols)
            for c0 in range(0, 128, BC):
                bc_ = min(BC, 128 - c0)
                descs2.append((
                    bass.AP(mid_d, c0 * cols, [[128 * cols, nA], [cols, bc_], [1, cols]]),
                    bass.AP(out_d, c0 * cols, [[128 * cols, nA], [cols, bc_], [1, cols]]),
                    nA, bc_ * cols, hm[0:nA, 0:nA], 1.0))

        def emit_phase(descs, base_it, fp32_in):
            assert base_it % 2 == 0
            s_l = s_ld if fp32_in else s_ls

            def load(eng, idx):
                _, _, rows, cols, _, _ = descs[idx]
                dst = sb_inF if fp32_in else sb_in
                eng.dma_start(out=dst[(base_it + idx) % 2][0:rows, 0:cols],
                              in_=descs[idx][0]).then_inc(s_l[(base_it + idx) % 2], 16)

            with nc.Block(no_gpsimd_drain=True) as block:
                @block.sync
                def _(sy):
                    load(sy, 0)

            for idx in range(len(descs) + 1):
                with nc.Block(no_gpsimd_drain=True) as block:
                    if idx < len(descs):
                        it = base_it + idx
                        par = it % 2
                        _, _, rows, cols, lhs_ap, _ = descs[idx]
                        if fp32_in:
                            @block.vector
                            def _(v, par=par, idx=idx, rows=rows, cols=cols):
                                v.wait_ge(s_l[par], 16 * (idx // 2 + 1))
                                v.tensor_copy(sb_in[par][0:rows, 0:cols],
                                              sb_inF[par][0:rows, 0:cols]).then_inc(s_cv, 1)

                        @block.tensor
                        def _(t, it=it, par=par, idx=idx, rows=rows, cols=cols,
                              lhs_ap=lhs_ap):
                            if fp32_in:
                                t.wait_ge(s_cv, idx + 1)
                            else:
                                t.wait_ge(s_l[par], 16 * (idx // 2 + 1))
                            if idx >= 2:
                                t.wait_ge(s_ac, base_it + idx - 1)
                            ins = None
                            for j in range((cols + 511) // 512):
                                w = min(512, cols - j * 512)
                                ins = t.matmul(ps[par][0:rows, j * 512:j * 512 + w], lhs_ap,
                                               sb_in[par][0:rows, j * 512:j * 512 + w],
                                               start=True, stop=True)
                            ins.then_inc(s_pe, 1)

                    if idx >= 1:
                        ut = base_it + idx - 1
                        upar = ut % 2
                        _, out_ap, rows, cols, _, scale = descs[idx - 1]

                        @block.scalar
                        def _(s, ut=ut, upar=upar, rows=rows, cols=cols, scale=scale):
                            s.wait_ge(s_pe, ut + 1)
                            if ut >= 2:
                                s.wait_ge(s_st[upar], 16 * (ut // 2))
                            s.activation(sb_o[upar][0:rows, 0:cols], ps[upar][0:rows, 0:cols],
                                         AF.Identity, scale=scale).then_inc(s_ac, 1)

                    @block.sync
                    def _(sy, idx=idx):
                        if idx + 1 < len(descs):
                            load(sy, idx + 1)
                        if idx >= 1:
                            ut = base_it + idx - 1
                            _, out_ap, rows, cols, _, _ = descs[idx - 1]
                            sy.wait_ge(s_ac, ut + 1)
                            sy.dma_start(out=out_ap,
                                         in_=sb_o[ut % 2][0:rows, 0:cols]).then_inc(s_st[ut % 2], 16)

        emit_phase(descs1, 0, fp32_in=True)
        n1 = len(descs1)
        with nc.Block(no_gpsimd_drain=True) as block:
            @block.sync
            def _(sy):
                sy.wait_ge(s_st[0], 16 * ((n1 + 1) // 2))
                sy.wait_ge(s_st[1], 16 * (n1 // 2))
        emit_phase(descs2, n1, fp32_in=False)
        n2 = n1 + len(descs2)
        with nc.Block(no_gpsimd_drain=True) as block:
            @block.sync
            def _(sy):
                sy.wait_ge(s_st[0], 16 * ((n2 + 1) // 2))
                sy.wait_ge(s_st[1], 16 * (n2 // 2))


def build_W():
    nc = bass.Bass("TRN2", target_bir_lowering=False)
    wupT = nc.dram_tensor("wupT", [D, 1024], F32, kind="ExternalInput")
    wgT = nc.dram_tensor("wgT", [D, 1024], F32, kind="ExternalInput")
    wdT = nc.dram_tensor("wdT", [INTER, 256], F32, kind="ExternalInput")
    h128 = nc.dram_tensor("h128", [128, 128], BF16, kind="ExternalInput")
    h16 = nc.dram_tensor("h16", [16, 16], BF16, kind="ExternalInput")
    h64 = nc.dram_tensor("h64", [64, 64], BF16, kind="ExternalInput")
    upf = nc.dram_tensor("upf", [D, 1024], BF16, kind="ExternalOutput")
    gatef = nc.dram_tensor("gatef", [D, 1024], BF16, kind="ExternalOutput")
    downf = nc.dram_tensor("downf", [INTER, 256], BF16, kind="ExternalOutput")
    u1 = nc.dram_tensor("u1", [D, 1024], BF16)
    u1g = nc.dram_tensor("u1g", [D, 1024], BF16)
    u2 = nc.dram_tensor("u2", [INTER, 256], BF16)
    es = contextlib.ExitStack()
    with es:
        sb_h128 = es.enter_context(_sb(nc, "wc_h128", [128, 128], BF16))
        sb_h16 = es.enter_context(_sb(nc, "wc_h16", [16, 16], BF16))
        sb_h64 = es.enter_context(_sb(nc, "wc_h64", [64, 64], BF16))
        s_l0 = es.enter_context(_sm(nc, "wc_l0"))
        with nc.Block(no_gpsimd_drain=True) as block:
            @block.gpsimd
            def _(g_):
                for dst, hsrc in [(sb_h128, h128), (sb_h16, h16), (sb_h64, h64)]:
                    g_.dma_start(out=dst[0:hsrc.shape[0], 0:hsrc.shape[1]],
                                 in_=hsrc.ap()).then_inc(s_l0, 16)
                g_.wait_ge(s_l0, 48)
        _emit_wfold(nc, wupT, wgT, wdT, sb_h128, sb_h16, sb_h64, upf, gatef, downf, u1, u1g, u2)
    return nc



TPC = 2048
SEQ = 8192
NCORE = 8


def _rs128(v):
    return np.ascontiguousarray(_f32(v).reshape(-1, 128).T)


def _build_wqkv_l0(inp, h0):
    blk = np.empty((D, HPC * 384), np.float32)
    qs, qg = inp["L0_attn_Wq_sem"], inp["L0_attn_Wq_geo"]
    ks, kg = inp["L0_attn_Wk_sem"], inp["L0_attn_Wk_geo"]
    wv = inp["L0_attn_Wv"]
    for j in range(HPC):
        h = h0 + j
        blk[:, j * 384 + 0:j * 384 + 64] = qs[h * 64:(h + 1) * 64, :].T
        blk[:, j * 384 + 64:j * 384 + 128] = qg[h * 64:(h + 1) * 64, :].T
        blk[:, j * 384 + 128:j * 384 + 192] = ks[h * 64:(h + 1) * 64, :].T
        blk[:, j * 384 + 192:j * 384 + 256] = kg[h * 64:(h + 1) * 64, :].T
        blk[:, j * 384 + 256:j * 384 + 384] = wv[h * 128:(h + 1) * 128, :].T
    return np.ascontiguousarray(blk)


def _slice_qrm(payf_b, h0, tpc=None, seq=None):
    tpc = tpc or TPC
    seq = seq or SEQ
    nrank = seq // tpc
    out = np.empty((HPC * 64, seq), payf_b.dtype)
    for r in range(nrank):
        r0 = r * PAY1 + QR_BASE + h0 * 64
        out[:, r * tpc:(r + 1) * tpc] = payf_b[r0:r0 + HPC * 64, :]
    return np.ascontiguousarray(out)


def _build_meta(h0):
    m = np.zeros((1, 32), np.int32)
    m[0, 0] = QR_BASE + h0 * 64
    for hloc in range(HPC):
        for r in range(SEQ // TPC):
            m[0, 8 + hloc * 4 + r] = r * PAY1 + QR_BASE + (h0 + hloc) * 64
    return m


def _run(nc, in_maps):
    return run_bass_kernel_spmd(nc, in_maps, core_ids=list(range(NCORE)))


def kernel(**inp):
    C = host_constants(SEQ)
    x = _f32(inp["L0_attn_x"])
    xs = [np.ascontiguousarray(x[c // 4, (c % 4) * TPC:(c % 4 + 1) * TPC, :])
          for c in range(NCORE)]

    ncW = build_W()
    inW = []
    for c in range(NCORE):
        inW.append(dict(
            wupT=np.ascontiguousarray(_f32(inp["L1_ffn_W_up"])[c * 1024:(c + 1) * 1024, :].T),
            wgT=np.ascontiguousarray(_f32(inp["L1_ffn_W_gate"])[c * 1024:(c + 1) * 1024, :].T),
            wdT=np.ascontiguousarray(_f32(inp["L1_ffn_W_down"])[c * 256:(c + 1) * 256, :].T),
            h128=C["h128"], h16=C["h16"], h64=C["h64"]))
    rW = _run(ncW, inW)
    wupf = np.concatenate([np.asarray(rW.results[c]["upf"]) for c in range(NCORE)], axis=1)
    wgf = np.concatenate([np.asarray(rW.results[c]["gatef"]) for c in range(NCORE)], axis=1)
    wdf = np.concatenate([np.asarray(rW.results[c]["downf"]) for c in range(NCORE)], axis=1)

    wsug = _f32(inp["L1_ffn_Wscale_ug"])
    for L in (0, 1):
        ncA = build_A(L, TPC)
        inA = []
        for c in range(NCORE):
            qc = c % 4
            sl = slice(qc * TPC, (qc + 1) * TPC)
            d = dict(x=xs[c], ident=C["ident"], ones_row=C["ones_row"])
            if L == 0:
                d.update(wg=inp["L0_ffn_w_gate"], wu=inp["L0_ffn_w_up"],
                         wd=inp["L0_ffn_w_down"],
                         sg_rs=_rs128(inp["L0_ffn_s_gate"]),
                         su_rs=_rs128(inp["L0_ffn_s_up"]),
                         sd_row=_f32(inp["L0_ffn_s_down"])[None, :])
            else:
                bpre = np.zeros((128, 17), np.float32)
                wdb = _f32(inp["L1_attn_W_down_b"])
                for ocx in range(8):
                    bpre[:, ocx] = wdb[ocx * 128:(ocx + 1) * 128]
                bpre[0:64, 8] = _f32(inp["L1_attn_Wkr_b"])
                wqrb = _f32(inp["L1_attn_Wqr_b"])
                for j in range(8):
                    bpre[:, 9 + j] = wqrb[j * 128:(j + 1) * 128]
                d.update(wg=wgf, wu=wupf, wd=wdf,
                         sg_rs=_rs128(wsug[INTER:]), su_rs=_rs128(wsug[:INTER]),
                         sd_row=_f32(inp["L1_ffn_Wscale_dp"])[None, :],
                         wpre=np.ascontiguousarray(np.hstack([
                             _f32(inp["L1_attn_W_down_k"]),
                             _f32(inp["L1_attn_Wkr_k"]),
                             _f32(inp["L1_attn_Wqr_k"])])),
                         bpre_rs=bpre,
                         krcos=np.ascontiguousarray(C["l1_kr_cos"][:, sl]),
                         krsin=np.ascontiguousarray(C["l1_kr_sin"][:, sl]),
                         qrcos=np.ascontiguousarray(C["l1_qr_cos"][:, sl]),
                         qrsin=np.ascontiguousarray(C["l1_qr_sin"][:, sl]),
                         permT=C["l1_permT"],
                         sudp_rs=_rs128(inp["L1_ffn_SU_dp"]),
                         suug_rs=_rs128(inp["L1_ffn_SU_ug"]))
            inA.append(d)
        rA = _run(ncA, inA)
        pays = [np.asarray(rA.results[c]["pay"]) for c in range(NCORE)]
        gs = [np.asarray(rA.results[c]["g"]) for c in range(NCORE)]
        payf = {b: np.ascontiguousarray(np.concatenate([pays[b * 4 + r] for r in range(4)], axis=0))
                for b in (0, 1)}

        ncB = build_B(L, TPC, SEQ)
        inB = []
        for c in range(NCORE):
            b = c // 4
            h0 = (c % 4) * HPC
            d = dict(payf=payf[b], ident=C["ident"], ones_col=C["ones_col"],
                     ones_row=C["ones_row"], causal=C["causal"],
                     meta=_build_meta(h0))
            if L == 1:
                d["qrm"] = _slice_qrm(payf[b], h0)
            if L == 0:
                d.update(wqkv=_build_wqkv_l0(inp, h0), cmask=C["l0_cmask"],
                         smask=C["l0_smask"], permT=C["l0_permT"])
            else:
                wukv_f = _f32(inp["L1_attn_W_uKV_k"])
                wuq_f = _f32(inp["L1_attn_W_uQ_k"])
                bkv_f = _f32(inp["L1_attn_W_uKV_b"])
                bq_f = _f32(inp["L1_attn_W_uQ_b"])
                wukv = np.empty((512, HPC * 256), np.float32)
                bkv = np.empty((128, HPC * 2), np.float32)
                wuq = np.empty((512, HPC * 128), np.float32)
                bq = np.empty((128, HPC), np.float32)
                for j in range(HPC):
                    h = h0 + j
                    wukv[:, j * 256:j * 256 + 128] = wukv_f[:, h * 128:(h + 1) * 128]
                    wukv[:, j * 256 + 128:j * 256 + 256] = wukv_f[:, 2048 + h * 128:2048 + (h + 1) * 128]
                    bkv[:, 2 * j] = bkv_f[h * 128:(h + 1) * 128]
                    bkv[:, 2 * j + 1] = bkv_f[2048 + h * 128:2048 + (h + 1) * 128]
                    wuq[:, j * 128:(j + 1) * 128] = wuq_f[:, h * 128:(h + 1) * 128]
                    bq[:, j] = bq_f[h * 128:(h + 1) * 128]
                d.update(wukv=np.ascontiguousarray(wukv), wuq=np.ascontiguousarray(wuq),
                         bkv_rs=np.ascontiguousarray(bkv), bq_rs=np.ascontiguousarray(bq))
            inB.append(d)
        rB = _run(ncB, inB)
        att = [np.asarray(rB.results[c]["attnout"]) for c in range(NCORE)]

        ncC = build_C(TPC)
        inC = []
        for c in range(NCORE):
            b = c // 4
            qc = c % 4
            sl = slice(qc * TPC, (qc + 1) * TPC)
            s_in = np.ascontiguousarray(
                np.concatenate([att[b * 4 + j][:, sl] for j in range(4)], axis=0))
            if L == 0:
                wo_cm = np.ascontiguousarray(_f32(inp["L0_attn_Wo"]).T)
                bias_row = np.zeros((1, D), np.float32)
            else:
                wo_cm = _f32(inp["L1_attn_Wout_k"])
                bias_row = _f32(inp["L1_attn_Wout_b"])[None, :]
            inC.append(dict(s_in=s_in, wo_cm=wo_cm, bias_row=bias_row,
                            ones_row=C["ones_row"], g=gs[c], x=xs[c]))
        rC = _run(ncC, inC)
        xs = [np.ascontiguousarray(np.asarray(rC.results[c]["xo"])) for c in range(NCORE)]

    y = np.empty((2, SEQ, D), np.float32)
    for c in range(NCORE):
        y[c // 4, (c % 4) * TPC:(c % 4 + 1) * TPC, :] = xs[c]
    return y



_FUSE_SKIP = set()
_FUSE_OPTS = {}


def build_fused(TPC, SEQ):
    NRANK = SEQ // TPC
    _SEMST[0] = None
    nc = bass.Bass("TRN2", target_bir_lowering=False)
    DT = nc.dram_tensor

    x0 = DT("x", [TPC, D], F32, kind="ExternalInput")
    ident = DT("ident", [128, 128], BF16, kind="ExternalInput")
    ones_col = DT("ones_col", [128, 1], BF16, kind="ExternalInput")
    ones_row = DT("ones_row", [1, 128], F32, kind="ExternalInput")
    causal = DT("causal", [128, 1024], BF16, kind="ExternalInput")
    meta = DT("meta", [1, 8], I32, kind="ExternalInput")
    wupT = DT("wupT", [D, 1024], F32, kind="ExternalInput")
    wgT = DT("wgT", [D, 1024], F32, kind="ExternalInput")
    wdT = DT("wdT", [INTER, 256], F32, kind="ExternalInput")
    h128 = DT("h128", [128, 128], BF16, kind="ExternalInput")
    h16 = DT("h16", [16, 16], BF16, kind="ExternalInput")
    h64 = DT("h64", [64, 64], BF16, kind="ExternalInput")
    wg0 = DT("wg0", [D, INTER], I8, kind="ExternalInput")
    wu0 = DT("wu0", [D, INTER], I8, kind="ExternalInput")
    wd0 = DT("wd0", [INTER, D], I8, kind="ExternalInput")
    sg0 = DT("sg0", [128, IC], F32, kind="ExternalInput")
    su0 = DT("su0", [128, IC], F32, kind="ExternalInput")
    sd0r = DT("sd0r", [1, D], F32, kind="ExternalInput")
    wqkv = DT("wqkv", [D, HPC * 384], F32, kind="ExternalInput")
    cmask = DT("cmask", [128, SEQ], BF16, kind="ExternalInput")
    smask = DT("smask", [128, SEQ], BF16, kind="ExternalInput")
    permT0 = DT("permT0", [128, 128], BF16, kind="ExternalInput")
    wocm0 = DT("wocm0", [D, D], F32, kind="ExternalInput")
    b0r = DT("b0r", [1, D], F32, kind="ExternalInput")
    wpre = DT("wpre", [D, WPRE_COLS], F32, kind="ExternalInput")
    bpre = DT("bpre", [128, 17], F32, kind="ExternalInput")
    krcos = DT("krcos", [64, TPC], BF16, kind="ExternalInput")
    krsin = DT("krsin", [64, TPC], BF16, kind="ExternalInput")
    qrcos = DT("qrcos", [1024, TPC], BF16, kind="ExternalInput")
    qrsin = DT("qrsin", [1024, TPC], BF16, kind="ExternalInput")
    permT1 = DT("permT1", [128, 128], BF16, kind="ExternalInput")
    sudp = DT("sudp", [128, IC], F32, kind="ExternalInput")
    suug = DT("suug", [128, DC], F32, kind="ExternalInput")
    sg1 = DT("sg1", [128, IC], F32, kind="ExternalInput")
    su1 = DT("su1", [128, IC], F32, kind="ExternalInput")
    sd1r = DT("sd1r", [1, D], F32, kind="ExternalInput")
    wukv = DT("wukv", [512, HPC * 256], F32, kind="ExternalInput")
    wuq = DT("wuq", [512, HPC * 128], F32, kind="ExternalInput")
    bkv = DT("bkv", [128, HPC * 2], F32, kind="ExternalInput")
    bq = DT("bq", [128, HPC], F32, kind="ExternalInput")
    wocm1 = DT("wocm1", [D, D], F32, kind="ExternalInput")
    b1r = DT("b1r", [1, D], F32, kind="ExternalInput")

    y = DT("y", [TPC, D], F32, kind="ExternalOutput")

    fuse_d = DT("fuse", [INTER, TPC], BF16)
    gbuf = DT("gbuf", [TPC, D], F32)
    x1 = DT("x1", [TPC, D], F32)
    pay0_in = DT("pay0_in", [D, TPC], BF16)
    pay0_g = DT("pay0_g", [NRANK * D, TPC], BF16)
    pay1_in = DT("pay1_in", [PAY1, TPC], BF16)
    pay1_g = DT("pay1_g", [NRANK * PAY1, TPC], BF16)
    qrt_in = DT("qrt_in", [TPC, 1024], BF16)
    qrt_g = DT("qrt_g", [SEQ, 1024], BF16)
    qrh = DT("qrh", [SEQ, 256], BF16)
    qrm_scr = DT("qrm_scr", [256, SEQ], BF16)
    att_in = DT("att_in", [HPC * 128, SEQ], BF16)
    s_scr = DT("s_scr", [D, TPC], BF16)
    att_g = [DT(f"att_g{i}", [NRANK * HPC * 128, SEQ], BF16) for i in range(2)]
    qd = DT("qd", [HPC * 128, SEQ], BF16)
    kd = DT("kd", [HPC * 128, SEQ], BF16)
    vd = DT("vd", [HPC * SEQ, 128], BF16)
    wfu_in = DT("wfu_in", [D, 1024], BF16)
    wfg_in = DT("wfg_in", [D, 1024], BF16)
    wfd_in = DT("wfd_in", [INTER, 256], BF16)
    wfu_g = DT("wfu_g", [8 * D, 1024], BF16)
    wfg_g = DT("wfg_g", [8 * D, 1024], BF16)
    wfd_g = DT("wfd_g", [8 * INTER, 256], BF16)
    wu1f = DT("wu1f", [D, INTER], BF16)
    wg1f = DT("wg1f", [D, INTER], BF16)
    wd1f = DT("wd1f", [INTER, D], BF16)
    u1s = DT("u1s", [D, 1024], BF16)
    u1gs = DT("u1gs", [D, 1024], BF16)
    u2s = DT("u2s", [INTER, 256], BF16)

    GROUPS = [[0, 1, 2, 3], [4, 5, 6, 7]]
    ALLG = [[0, 1, 2, 3, 4, 5, 6, 7]]

    es = contextlib.ExitStack()
    with es:
        sb_ident = es.enter_context(_sb(nc, "g_ident", [128, 128], BF16))
        sb_ones = es.enter_context(_sb(nc, "g_ones", [128, 1], BF16))
        sb_or = es.enter_context(_sb(nc, "g_or", [1, 128], F32))
        sb_caus = es.enter_context(_sb(nc, "g_caus", [128, 1024], BF16))
        sb_meta = es.enter_context(_sb(nc, "g_meta", [1, 8], I32))
        sb_h128 = es.enter_context(_sb(nc, "g_h128", [128, 128], BF16))
        sb_h16 = es.enter_context(_sb(nc, "g_h16", [16, 16], BF16))
        sb_h64 = es.enter_context(_sb(nc, "g_h64", [64, 64], BF16))
        s_g0 = es.enter_context(_sm(nc, "g_l0"))
        cc_w = [es.enter_context(_sm(nc, f"cc_w{i}")) for i in range(3)]
        s_wcp = es.enter_context(_sm(nc, "s_wcp"))
        cc_pay = [es.enter_context(_sm(nc, f"cc_pay{i}")) for i in range(2)]
        cc_att = [es.enter_context(_sm(nc, f"cc_att{i}")) for i in range(2)]
        cc_qrt = es.enter_context(_sm(nc, "cc_qrt"))
        s_qrh = es.enter_context(_sm(nc, "s_qrh"))

        with nc.Block(no_gpsimd_drain=True) as block:
            @block.gpsimd
            def _(g_):
                n = 0
                for dst, srt in [(sb_ident, ident), (sb_ones, ones_col), (sb_or, ones_row),
                                 (sb_caus, causal), (sb_meta, meta), (sb_h128, h128),
                                 (sb_h16, h16), (sb_h64, h64)]:
                    g_.dma_start(out=dst[0:srt.shape[0], 0:srt.shape[1]],
                                 in_=srt.ap()).then_inc(s_g0, 16)
                    n += 16
                g_.wait_ge(s_g0, n)

        s_fls = es.enter_context(nc.semaphore("s_fls"))
        s_flg = es.enter_context(nc.semaphore("s_flg"))
        sb_fence = es.enter_context(_sb(nc, "g_fence", [1, 64], BF16))
        assert s_flg.num <= 169, s_flg.num
        _SEMST[0] = _FenceState(s_flg.num + 1, s_fls, s_flg)

        if "wfold" not in _FUSE_SKIP:
            _PX[0] = "w_"
            _emit_wfold(nc, wupT, wgT, wdT, sb_h128, sb_h16, sb_h64,
                        wfu_in, wfg_in, wfd_in, u1s, u1gs, u2s)
            _PX[0] = ""
        with nc.Block(no_gpsimd_drain=True) as block:
            @block.gpsimd
            def _(g_):
                for i, (src, dstg) in enumerate([(wfu_in, wfu_g), (wfg_in, wfg_g),
                                                 (wfd_in, wfd_g)]):
                    if "wag" in _FUSE_SKIP:
                        g_.sem_inc(cc_w[i], 1)
                        continue
                    g_.collective_compute("AllGather", ALU.bypass, replica_groups=ALLG,
                                          ins=[src.ap()], outs=[dstg.ap()]).then_inc(cc_w[i])

        def wcp_copies(eng):
            for i, (srcg, dstf, rows, cols) in enumerate([
                    (wfu_g, wu1f, D, 1024), (wfg_g, wg1f, D, 1024),
                    (wfd_g, wd1f, INTER, 256)]):
                eng.wait_ge(cc_w[i], 1)
                nq = max(4, (rows * 8 + 16383) // 16384 + 1)
                while rows % nq:
                    nq += 1
                for q in range(nq):
                    r0 = q * (rows // nq)
                    eng.dma_start(
                        out=bass.AP(dstf, r0 * 8 * cols,
                                    [[8 * cols, rows // nq], [cols, 8], [1, cols]]),
                        in_=bass.AP(srcg, r0 * cols,
                                    [[cols, rows // nq], [rows * cols, 8], [1, cols]]),
                    ).then_inc(s_wcp, 16)

        layers = [Lx for Lx in (0, 1) if f"layer{Lx}" not in _FUSE_SKIP]
        for li, L in enumerate(layers):
            if li > 0:
                _sem_fence(nc, ident.ap(), sb_fence)
            px = f"l{L}"
            x_ap = (x0 if L == 0 else x1).ap()
            xo_ap = (x1 if L == 0 else y).ap()
            payg = pay0_g if L == 0 else pay1_g

            with contextlib.ExitStack() as el:
                sb_sg = el.enter_context(_sb(nc, px + "c_sg", [128, IC], F32))
                sb_su = el.enter_context(_sb(nc, px + "c_su", [128, IC], F32))
                sb_sdb = el.enter_context(_sb(nc, px + "c_sdb", [128, D], F32))
                sb_sdr = el.enter_context(_sb(nc, px + "c_sdr", [1, D], F32))
                if L == 1:
                    sb_perm = el.enter_context(_sb(nc, px + "c_perm", [128, 128], BF16))
                    sb_bpre = el.enter_context(_sb(nc, px + "c_bpre", [128, 17], F32))
                    sb_sudp = el.enter_context(_sb(nc, px + "c_sudp", [128, IC], F32))
                    sb_suug = el.enter_context(_sb(nc, px + "c_suug", [128, DC], F32))
                s_lc = el.enter_context(_sm(nc, px + "c_ld"))
                s_bc = el.enter_context(_sm(nc, px + "c_bc"))

                with nc.Block(no_gpsimd_drain=True) as block:
                    @block.gpsimd
                    def _(g_, L=L):
                        n = 0
                        pairs = ([(sb_sg, sg0), (sb_su, su0), (sb_sdr, sd0r)] if L == 0 else
                                 [(sb_sg, sg1), (sb_su, su1), (sb_sdr, sd1r),
                                  (sb_perm, permT1), (sb_bpre, bpre),
                                  (sb_sudp, sudp), (sb_suug, suug)])
                        for dst, srt in pairs:
                            g_.dma_start(out=dst[0:srt.shape[0], 0:srt.shape[1]],
                                         in_=srt.ap()).then_inc(s_lc, 16)
                            n += 16
                        if L == 1 and "wcp" in _FUSE_SKIP:
                            for _i in range(16):
                                g_.sem_inc(s_wcp, 16)
                        elif L == 1 and 0 not in layers:
                            wcp_copies(g_)
                        g_.wait_ge(s_lc, n)

                with contextlib.ExitStack() as eb:
                    ps_b = eb.enter_context(_pt(nc, px + "c_psb", [128, 512], F32))
                    with nc.Block(no_gpsimd_drain=True) as block:
                        @block.tensor
                        def _(t):
                            for j in range(D // 512):
                                if j >= 1:
                                    t.wait_ge(s_bc, 2 * j)
                                t.matmul(ps_b[:, :], sb_or[:, :],
                                         sb_sdr[:, j * 512:(j + 1) * 512],
                                         start=True, stop=True).then_inc(s_bc, 1)

                        @block.scalar
                        def _(s):
                            for j in range(D // 512):
                                s.wait_ge(s_bc, 2 * j + 1)
                                s.activation(sb_sdb[:, j * 512:(j + 1) * 512], ps_b[:, :],
                                             AF.Identity).then_inc(s_bc, 1)

                with contextlib.ExitStack() as ex:
                    xn = ex.enter_context(_sb(nc, px + "xn", [128, DC * TPC], BF16))
                    if f"ln{L}" not in _FUSE_SKIP:
                        _PX[0] = px + "ln_"
                        _emit_ln(nc, x_ap, xn, pay0_in.ap() if L == 0 else None, sb_ident, TPC)
                    if L == 1 and "pp1" not in _FUSE_SKIP:
                        _PX[0] = px + "pp_"
                        _emit_preproj(nc, xn, pay1_in.ap(), wpre.ap(), sb_bpre, sb_perm,
                                      krcos.ap(), krsin.ap(), qrcos.ap(), qrsin.ap(), TPC)
                        _PX[0] = px + "qt_"
                        _emit_tr(nc, pay1_in, qrt_in, 1024, TPC, sb_ident, px + "qt_",
                                 src_off=QR_BASE * TPC)
                    _PX[0] = ""
                    with nc.Block(no_gpsimd_drain=True) as block:
                        @block.gpsimd
                        def _(g_, L=L):
                            pin = pay0_in if L == 0 else pay1_in
                            g_.collective_compute("AllGather", ALU.bypass,
                                                  replica_groups=GROUPS,
                                                  ins=[pin.ap()], outs=[payg.ap()],
                                                  ).then_inc(cc_pay[L])
                            if L == 1:
                                if "payag" in _FUSE_SKIP:
                                    g_.sem_inc(cc_qrt, 1)
                                else:
                                    g_.collective_compute("AllGather", ALU.bypass,
                                                          replica_groups=GROUPS,
                                                          ins=[qrt_in.ap()], outs=[qrt_g.ap()],
                                                          ).then_inc(cc_qrt)
                    if L == 1 and "sx1" not in _FUSE_SKIP:
                        _PX[0] = px + "sx_"
                        _emit_scale_xn(nc, xn, sb_suug, TPC)
                    if L == 1:
                        with nc.Block(no_gpsimd_drain=True) as block:
                            @block.gpsimd
                            def _(g_):
                                g_.wait_ge(s_wcp, 16 * 16)
                    if f"ffn{L}" not in _FUSE_SKIP:
                        _PX[0] = px + "fg_"
                        _emit_ffn_upgate(nc, L, xn, (wg0 if L == 0 else wg1f).ap(),
                                         (wu0 if L == 0 else wu1f).ap(), sb_sg, sb_su,
                                         sb_sudp if L == 1 else None, fuse_d.ap(), TPC)
                if f"ffn{L}" not in _FUSE_SKIP:
                    _PX[0] = px + "fd_"
                    _emit_ffn_down(nc, L, (wd0 if L == 0 else wd1f).ap(), sb_sdb,
                                   fuse_d.ap(), gbuf.ap(), TPC)
                _PX[0] = ""

                with nc.Block(no_gpsimd_drain=True) as block:
                    @block.gpsimd
                    def _(g_, L=L):
                        g_.wait_ge(cc_pay[L], 1)
                        if L == 1 and "qrh" not in _FUSE_SKIP:
                            g_.wait_ge(cc_qrt, 1)
                            with g_.register("rq") as rq:
                                g_.reg_load(rq, sb_meta[0:1, 1:2])
                                val = g_.snap(rq, min_val=0, max_val=1024 - 256)
                                g_.dma_start(out=qrh.ap(),
                                             in_=qrt_g.ap()[:, bass.ds(val, 256)]).then_inc(s_qrh, 16)
                            g_.wait_ge(s_qrh, 16)
                if L == 1 and "qu" not in _FUSE_SKIP:
                    _emit_tr(nc, qrh, qrm_scr, SEQ, 256, sb_ident, px + "qu_")

                with contextlib.ExitStack() as ea:
                    if L == 0:
                        sb_w = ea.enter_context(_sb(nc, px + "a_w", [128, DC * HPC * 384], BF16))
                        sb_p0 = ea.enter_context(_sb(nc, px + "a_p0", [128, 128], BF16))
                        s_al = ea.enter_context(_sm(nc, px + "a_ld"))
                        with nc.Block(no_gpsimd_drain=True) as block:
                            @block.gpsimd
                            def _(g_):
                                g_.dma_start(out=sb_w[:, :].rearrange("p (c n) -> p c n", c=DC),
                                             in_=wqkv.ap().rearrange("(c p) n -> p c n", p=128),
                                             ).then_inc(s_al, 16)
                                g_.dma_start(out=sb_p0[:, :], in_=permT0.ap()).then_inc(s_al, 16)
                                g_.wait_ge(s_al, 32)
                        _PX[0] = px + "pj_"
                        _emit_proj_l0(nc, pay0_g, sb_w, sb_p0, sb_ident,
                                      cmask.ap(), smask.ap(), qd, kd, vd, TPC, SEQ)
                        if 1 in layers and "wcp" not in _FUSE_SKIP:
                            with nc.Block(no_gpsimd_drain=True) as blkw:
                                @blkw.scalar
                                def _(sc):
                                    wcp_copies(sc)
                    else:
                        sb_w = ea.enter_context(_sb(nc, px + "a_w", [128, 4 * HPC * 256], BF16))
                        sb_wq = ea.enter_context(_sb(nc, px + "a_wq", [128, 4 * HPC * 128], BF16))
                        sb_bkv = ea.enter_context(_sb(nc, px + "a_bkv", [128, HPC * 2], F32))
                        sb_bq = ea.enter_context(_sb(nc, px + "a_bq", [128, HPC], F32))
                        s_al = ea.enter_context(_sm(nc, px + "a_ld"))
                        with nc.Block(no_gpsimd_drain=True) as block:
                            @block.gpsimd
                            def _(g_):
                                g_.dma_start(out=sb_w[:, :].rearrange("p (c n) -> p c n", c=4),
                                             in_=wukv.ap().rearrange("(c p) n -> p c n", p=128),
                                             ).then_inc(s_al, 16)
                                g_.dma_start(out=sb_wq[:, :].rearrange("p (c n) -> p c n", c=4),
                                             in_=wuq.ap().rearrange("(c p) n -> p c n", p=128),
                                             ).then_inc(s_al, 16)
                                g_.dma_start(out=sb_bkv[:, :], in_=bkv.ap()).then_inc(s_al, 16)
                                g_.dma_start(out=sb_bq[:, :], in_=bq.ap()).then_inc(s_al, 16)
                                g_.wait_ge(s_al, 64)
                        if "pj1" not in _FUSE_SKIP:
                            _PX[0] = px + "pj_"
                            _emit_proj_l1(nc, pay1_g, sb_w, sb_wq, sb_bkv, sb_bq, sb_ident,
                                          qd, kd, vd, qrm_scr, TPC, SEQ)
                    if f"sdpa{L}" not in _FUSE_SKIP:
                        _PX[0] = px + "sd_"
                        _emit_sdpa(nc, L, qd, kd, vd, qrm_scr if L == 1 else None,
                                   pay1_g if L == 1 else pay0_g, meta,
                                   sb_ones, sb_or, sb_caus, att_in.ap(), TPC, SEQ)
                    _PX[0] = ""

                with nc.Block(no_gpsimd_drain=True) as block:
                    @block.gpsimd
                    def _(g_, L=L):
                        if "attag" in _FUSE_SKIP:
                            g_.sem_inc(cc_att[L], 1)
                        else:
                            g_.collective_compute("AllGather", ALU.bypass, replica_groups=GROUPS,
                                                  ins=[att_in.ap()], outs=[att_g[L].ap()],
                                                  ).then_inc(cc_att[L])

                def sin_dma(g_, sb_s, sem, L=L):
                    g_.wait_ge(cc_att[L], 1)
                    with g_.register(f"rt{L}") as rt:
                        g_.reg_load(rt, sb_meta[0:1, 0:1])
                        val = g_.snap(rt, min_val=0, max_val=SEQ - TPC)
                        g_.dma_start(out=s_scr.ap(),
                                     in_=att_g[L].ap()[:, bass.ds(val, TPC)]).then_inc(sem, 16)
                    g_.wait_ge(sem, 16)
                    g_.dma_start(
                        out=sb_s[:, :].rearrange("p (c t) -> p c t", c=DC),
                        in_=s_scr.ap().rearrange("(c p) t -> p c t", p=128),
                    ).then_inc(sem, 16)
                    return 32

                _PX[0] = px + "po_"
                if f"post{L}" not in _FUSE_SKIP:
                    _emit_post(nc, sin_dma, (wocm0 if L == 0 else wocm1).ap(),
                               (b0r if L == 0 else b1r).ap(), sb_or,
                               gbuf.ap(), x_ap, xo_ap, TPC)
                _PX[0] = ""

    _SEMST[0] = None
    return nc


def _dyn3(handle, seq, dc, tpc, val):
    ap = bass.AP(handle, 0, [[seq, 128], [128 * seq, dc], [1, seq]])
    return ap[:, :, bass.ds(val, tpc)]


FUSED = True


def kernel_fused(**inp):
    C = host_constants(SEQ)
    x = _f32(inp["L0_attn_x"])
    wsug = _f32(inp["L1_ffn_Wscale_ug"])
    nc = build_fused(TPC, SEQ)
    in_maps = []
    bpre = np.zeros((128, 17), np.float32)
    wdb = _f32(inp["L1_attn_W_down_b"])
    for ocx in range(8):
        bpre[:, ocx] = wdb[ocx * 128:(ocx + 1) * 128]
    bpre[0:64, 8] = _f32(inp["L1_attn_Wkr_b"])
    wqrb = _f32(inp["L1_attn_Wqr_b"])
    for j in range(8):
        bpre[:, 9 + j] = wqrb[j * 128:(j + 1) * 128]
    wpre_h = np.ascontiguousarray(np.hstack([
        _f32(inp["L1_attn_W_down_k"]), _f32(inp["L1_attn_Wkr_k"]),
        _f32(inp["L1_attn_Wqr_k"])]))
    wukv_f, wuq_f = _f32(inp["L1_attn_W_uKV_k"]), _f32(inp["L1_attn_W_uQ_k"])
    bkv_f, bq_f = _f32(inp["L1_attn_W_uKV_b"]), _f32(inp["L1_attn_W_uQ_b"])

    for c in range(NCORE):
        qc = c % 4
        h0 = qc * HPC
        sl = slice(qc * TPC, (qc + 1) * TPC)
        wukv = np.empty((512, HPC * 256), np.float32)
        bkv = np.empty((128, HPC * 2), np.float32)
        wuq = np.empty((512, HPC * 128), np.float32)
        bq = np.empty((128, HPC), np.float32)
        for j in range(HPC):
            h = h0 + j
            wukv[:, j * 256:j * 256 + 128] = wukv_f[:, h * 128:(h + 1) * 128]
            wukv[:, j * 256 + 128:j * 256 + 256] = wukv_f[:, 2048 + h * 128:2048 + (h + 1) * 128]
            bkv[:, 2 * j] = bkv_f[h * 128:(h + 1) * 128]
            bkv[:, 2 * j + 1] = bkv_f[2048 + h * 128:2048 + (h + 1) * 128]
            wuq[:, j * 128:(j + 1) * 128] = wuq_f[:, h * 128:(h + 1) * 128]
            bq[:, j] = bq_f[h * 128:(h + 1) * 128]
        m = np.zeros((1, 8), np.int32)
        m[0, 0] = qc * TPC
        m[0, 1] = h0 * 64
        d = dict(
            x=np.ascontiguousarray(x[c // 4, sl, :]),
            ident=C["ident"], ones_col=C["ones_col"], ones_row=C["ones_row"],
            causal=C["causal"], meta=m,
            wupT=np.ascontiguousarray(_f32(inp["L1_ffn_W_up"])[c * 1024:(c + 1) * 1024, :].T),
            wgT=np.ascontiguousarray(_f32(inp["L1_ffn_W_gate"])[c * 1024:(c + 1) * 1024, :].T),
            wdT=np.ascontiguousarray(_f32(inp["L1_ffn_W_down"])[c * 256:(c + 1) * 256, :].T),
            h128=C["h128"], h16=C["h16"], h64=C["h64"],
            wg0=inp["L0_ffn_w_gate"], wu0=inp["L0_ffn_w_up"], wd0=inp["L0_ffn_w_down"],
            sg0=_rs128(inp["L0_ffn_s_gate"]), su0=_rs128(inp["L0_ffn_s_up"]),
            sd0r=_f32(inp["L0_ffn_s_down"])[None, :],
            wqkv=_build_wqkv_l0(inp, h0),
            cmask=C["l0_cmask"], smask=C["l0_smask"], permT0=C["l0_permT"],
            wocm0=np.ascontiguousarray(_f32(inp["L0_attn_Wo"]).T),
            b0r=np.zeros((1, D), np.float32),
            wpre=wpre_h, bpre=bpre,
            krcos=np.ascontiguousarray(C["l1_kr_cos"][:, sl]),
            krsin=np.ascontiguousarray(C["l1_kr_sin"][:, sl]),
            qrcos=np.ascontiguousarray(C["l1_qr_cos"][:, sl]),
            qrsin=np.ascontiguousarray(C["l1_qr_sin"][:, sl]),
            permT1=C["l1_permT"],
            sudp=_rs128(inp["L1_ffn_SU_dp"]), suug=_rs128(inp["L1_ffn_SU_ug"]),
            sg1=_rs128(wsug[INTER:]), su1=_rs128(wsug[:INTER]),
            sd1r=_f32(inp["L1_ffn_Wscale_dp"])[None, :],
            wukv=np.ascontiguousarray(wukv), wuq=np.ascontiguousarray(wuq),
            bkv_rs=np.ascontiguousarray(bkv), bq_rs=np.ascontiguousarray(bq),
            wocm1=_f32(inp["L1_attn_Wout_k"]),
            b1r=_f32(inp["L1_attn_Wout_b"])[None, :],
        )
        d["bkv"] = d.pop("bkv_rs")
        d["bq"] = d.pop("bq_rs")
        in_maps.append(d)
    res = _run(nc, in_maps)
    yout = np.empty((2, SEQ, D), np.float32)
    for c in range(NCORE):
        yout[c // 4, (c % 4) * TPC:(c % 4 + 1) * TPC, :] = np.asarray(res.results[c]["y"])
    return yout


_kernel_unfused = kernel


def kernel(**inp):
    if FUSED:
        return kernel_fused(**inp)
    return _kernel_unfused(**inp)
